# version 08
# speedup vs baseline: 1.3900x; 1.0017x over previous
import contextlib
import numpy as np
import concourse.bass as bass
import concourse.mybir as mybir
from concourse.bass_utils import run_bass_kernel_spmd

F32 = mybir.dt.float32
BF16 = mybir.dt.bfloat16
F8 = mybir.dt.float8e4
AF = mybir.ActivationFunctionType
ALU = mybir.AluOpType
AX = mybir.AxisListType

MAGIC = 12582912.0
T = 2048
D = 1024
NT = T // 512


class K:
    def __init__(self, nc):
        self.nc = nc
        self._sv = {}
        self._n = 0

    def name(self, p):
        self._n += 1
        return f"{p}_{self._n}"

    SEM_POOL = 40

    def gsem(self):
        if not hasattr(self, "_gi"):
            self._gi = 0
        self._gi += 1
        h = self.nc.alloc_semaphore(name=f"gs{self._gi}")
        self._sv[h] = 0
        return h

    def dsem(self, name=None):
        if not hasattr(self, "_dpool"):
            self._dpool = [self.nc.alloc_semaphore(name=f"dp{i}")
                           for i in range(16)]
            for h in self._dpool:
                self._sv[h] = 0
            self._dpi = 0
        h = self._dpool[self._dpi % 16]
        self._dpi += 1
        return h

    def sem(self, name=None):
        if not hasattr(self, "_pool"):
            self._pool = [self.nc.alloc_semaphore(name=f"sp{i}")
                          for i in range(self.SEM_POOL)]
            for h in self._pool:
                self._sv[h] = 0
            self._pi = 0
        h = self._pool[self._pi % self.SEM_POOL]
        self._pi += 1
        return h

    def inc(self, ins, sem, n=1):
        ins.then_inc(sem, n)
        self._sv[sem] += n
        return ins

    def v(self, sem):
        return self._sv[sem]

    def init_common(self):
        nc = self.nc
        self.ident = nc.alloc_sbuf_tensor("c_ident", [128, 128], F32).ap()
        self.ones_col = nc.alloc_sbuf_tensor("c_ones_col", [128, 1], F32).ap()
        self.ones_row = nc.alloc_sbuf_tensor("c_ones_row", [1, 128], F32).ap()
        self.cpool = nc.alloc_sbuf_tensor("c_cpool", [128, 4], F32).ap()
        with nc.Block() as block:
            @block.gpsimd
            def _(g):
                g.memset(self.cpool[:, 0:1], 1e-5)
                g.memset(self.cpool[:, 1:2], 1e-6)
                g.memset(self.cpool[:, 2:3], 0.5)
                g.memset(self.cpool[:, 3:4], 1.0)
                g.memset(self.ident[:, :], 1.0)
                g.memset(self.ones_col[:, :], 1.0)
                g.memset(self.ones_row[:, :], 1.0)
                g.drain()
                g.affine_select(
                    out=self.ident[:, :], in_=self.ident[:, :],
                    pattern=[[-1, 128]], compare_op=ALU.is_equal,
                    fill=0.0, channel_multiplier=1, base=0)
                g.drain()

    def load_x_fm(self, x_in, xt):
        nc, ld = self.nc, self.dsem()
        with nc.Block() as block:
            @block.sync
            def _(s):
                for i in range(8):
                    self.inc(s.dma_start(out=xt[i][:, :], in_=x_in[i, :, :]), ld, 16)
                s.wait_ge(ld, self.v(ld))

    def store_x_fm(self, x_out, xt):
        nc, st = self.nc, self.gsem()
        with nc.Block() as block:
            @block.gpsimd
            def _(g):
                for i in range(8):
                    self.inc(g.dma_start(out=x_out[i, :, :], in_=xt[i][:, :]), st, 16)
                g.wait_ge(st, self.v(st))

    def ln_rows(self, xt, mu_row, r_row, tmp_row, sq_bufs, ps_row, eps_ap=None,
                n_d=8):
        if eps_ap is None:
            eps_ap = self.cpool[:, 0:1]
        """mu_row = mean over d; r_row = 1/sqrt(var+eps); rows are [1,2048].

        ps_row: [1, >=2048] psum view (4 banks). sq_bufs: 2 scratch tiles.
        """
        nc = self.nc
        mm = self.sem()
        mm_t = self.v(mm) + 1
        with nc.Block() as block:
            @block.tensor
            def _(t):
                ins = None
                for q in range(NT):
                    for i in range(n_d):
                        ins = t.matmul(ps_row[0:1, q * 512:(q + 1) * 512],
                                       self.ones_col[:, :],
                                       xt[i][:, q * 512:(q + 1) * 512],
                                       start=(i == 0), stop=(i == n_d - 1))
                self.inc(ins, mm, 1)

            @block.vector
            def _(v):
                v.wait_ge(mm, mm_t)
                v.tensor_scalar(out=mu_row[:, :], in0=ps_row[0:1, 0:2048],
                                scalar1=1.0 / (128.0 * n_d), scalar2=None,
                                op0=ALU.mult, op1=ALU.bypass)
                v.drain()

        sq = self.sem()
        mm2 = self.sem()
        sq0, mm20 = self.v(sq), self.v(mm2)
        with nc.Block() as block:
            nb = len(sq_bufs)

            @block.scalar
            def _(s):
                for i in range(n_d):
                    if i >= nb:
                        s.wait_ge(mm2, mm20 + i - nb + 1)
                    s.activation(out=sq_bufs[i % nb][:, :], in_=xt[i][:, :],
                                 func=AF.Square)
                    s.drain()
                    self.inc(s.nop(), sq, 1)

            @block.tensor
            def _(t):
                for i in range(n_d):
                    t.wait_ge(sq, sq0 + i + 1)
                    ins = None
                    for q in range(NT):
                        ins = t.matmul(ps_row[0:1, q * 512:(q + 1) * 512],
                                       self.ones_col[:, :],
                                       sq_bufs[i % nb][:, q * 512:(q + 1) * 512],
                                       start=(i == 0), stop=(i == n_d - 1))
                    self.inc(ins, mm2, 1)

            mm2_t = self.v(mm2)

            @block.vector
            def _(v):
                v.wait_ge(mm2, mm2_t)
                v.tensor_scalar(out=tmp_row[:, :], in0=ps_row[0:1, 0:2048],
                                scalar1=1.0 / (128.0 * n_d), scalar2=None,
                                op0=ALU.mult, op1=ALU.bypass)
                v.tensor_tensor(out=r_row[:, :], in0=mu_row[:, :],
                                in1=mu_row[:, :], op=ALU.mult)
                v.drain()
                v.tensor_tensor(out=r_row[:, :], in0=tmp_row[:, :],
                                in1=r_row[:, :], op=ALU.subtract)
                v.drain()

        with nc.Block() as block:
            @block.scalar
            def _(s):
                s.activation(out=r_row[:, :], in_=r_row[:, :], func=AF.Sqrt,
                             bias=eps_ap[0:1, 0:1], scale=1.0)
                s.drain()

        with nc.Block() as block:
            @block.vector
            def _(v):
                v.reciprocal(out=r_row[:, :], in_=r_row[:, :])
                v.drain()

    def bcast_row(self, t_eng, row_ap, ps_tile, n_cols):
        ins = None
        for q in range(0, n_cols, 512):
            w = min(512, n_cols - q)
            ins = t_eng.matmul(ps_tile[:, q:q + w], self.ones_row[:, :],
                               row_ap[0:1, q:q + w], start=True, stop=True)
        return ins

    def part_reduce(self, tiles, out_row, pst2, pcols, red_col, acc_col,
                    pam_buf, mode="absmax"):
        red_op = ALU.min if mode == "min" else ALU.max
        use_abs = (mode == "absmax")
        width = tiles[0].shape[1]
        nsl = width // 128
        nc = self.nc
        n_tiles = len(tiles)
        tr = self.sem()
        rd = self.sem()
        tr0, rd0 = self.v(tr), self.v(rd)
        with nc.Block() as block:
            @block.tensor
            def _(t):
                n = 0
                for s in range(nsl):
                    for i in range(n_tiles):
                        if n >= 2:
                            t.wait_ge(rd, rd0 + n - 1)
                        self.inc(
                            t.transpose(pst2[:, (n % 2) * 512:(n % 2) * 512 + 128],
                                        tiles[i][:, s * 128:(s + 1) * 128],
                                        self.ident[:, :]), tr, 1)
                        n += 1

            @block.vector
            def _(v):
                n = 0
                for s in range(nsl):
                    for i in range(n_tiles):
                        v.wait_ge(tr, tr0 + n + 1)
                        dst = acc_col if i == 0 else red_col
                        v.tensor_reduce(
                            out=dst[:, :],
                            in_=pst2[:, (n % 2) * 512:(n % 2) * 512 + 128],
                            axis=AX.X, op=red_op,
                            apply_absolute_value=use_abs)
                        v.drain()
                        if i > 0:
                            v.tensor_tensor(out=acc_col[:, :], in0=acc_col[:, :],
                                            in1=red_col[:, :], op=red_op)
                            v.drain()
                        self.inc(v.nop(), rd, 1)
                        n += 1
                    v.tensor_copy(out=pam_buf[:, s:s + 1], in_=acc_col[:, :])
                    v.drain()

        t2 = self.sem()
        c2 = self.sem()
        t20, c20 = self.v(t2), self.v(c2)
        with nc.Block() as block:
            @block.tensor
            def _(t):
                for s in range(nsl):
                    if s >= 4:
                        t.wait_ge(c2, c20 + s - 3)
                    self.inc(
                        t.transpose(pcols[0:1, (s % 4) * 512:(s % 4) * 512 + 128],
                                    pam_buf[:, s:s + 1], self.ident[:, :]),
                        t2, 1)

            @block.vector
            def _(v):
                for s in range(nsl):
                    v.wait_ge(t2, t20 + s + 1)
                    v.tensor_copy(out=out_row[0:1, s * 128:(s + 1) * 128],
                                  in_=pcols[0:1, (s % 4) * 512:(s % 4) * 512 + 128])
                    v.drain()
                    self.inc(v.nop(), c2, 1)

    def bcast_scalar_to_col(self, scal_ap, col_out, ps_col):
        nc = self.nc
        bs = self.sem()
        bs0 = self.v(bs)
        with nc.Block() as block:
            @block.tensor
            def _(t):
                self.inc(t.matmul(ps_col[0:128, 0:1], self.ones_row[:, :],
                                  scal_ap, start=True, stop=True), bs, 1)

            @block.vector
            def _(v):
                v.wait_ge(bs, bs0 + 1)
                v.tensor_copy(out=col_out[:, 0:1], in_=ps_col[0:128, 0:1])
                v.drain()



def weight_meanabs(k, wT, n_chunks, chunk_cols, asum, col, pst2, scal_ap,
                   inv_count):
    nc = k.nc
    wf_cm = nc.sbuf_tensor(k.name("wma_wf"), [128, 2 * chunk_cols], F32)
    wf = wf_cm.__enter__().ap()
    ld2 = [k.dsem(), k.dsem()]
    rd = k.sem()
    ld20 = [k.v(ld2[0]), k.v(ld2[1])]
    rd0 = k.v(rd)
    with nc.Block() as block:
        @block.sync
        def _(s):
            for i in range(n_chunks):
                if i >= 2:
                    s.wait_ge(rd, rd0 + i - 1)
                k.inc(s.dma_start(
                    out=wf[:, (i % 2) * chunk_cols:(i % 2 + 1) * chunk_cols],
                    in_=wT[i * 128:(i + 1) * 128, :]), ld2[i % 2], 16)

        @block.vector
        def _(v):
            for i in range(n_chunks):
                v.wait_ge(ld2[i % 2], ld20[i % 2] + 16 * (i // 2 + 1))
                v.tensor_reduce(
                    out=asum[:, i:i + 1],
                    in_=wf[:, (i % 2) * chunk_cols:(i % 2 + 1) * chunk_cols],
                    axis=AX.X, op=ALU.add, apply_absolute_value=True)
                v.drain()
                k.inc(v.nop(), rd, 1)
            v.tensor_reduce(out=col[:, :], in_=asum[:, 0:n_chunks], axis=AX.X,
                            op=ALU.add)
            v.drain()

    wf_cm.__exit__(None, None, None)
    tr = k.sem()
    tr0 = k.v(tr)
    with nc.Block() as block:
        @block.tensor
        def _(t):
            k.inc(t.transpose(pst2[0:1, 0:128], col[:, :], k.ident[:, :]),
                  tr, 1)

        @block.vector
        def _(v):
            v.wait_ge(tr, tr0 + 1)
            v.tensor_reduce(out=scal_ap, in_=pst2[0:1, 0:128], axis=AX.X,
                            op=ALU.add)
            v.drain()
            v.tensor_scalar(out=scal_ap, in0=scal_ap,
                            scalar1=float(inv_count), scalar2=1e-5,
                            op0=ALU.mult, op1=ALU.max)
            v.drain()


def weight_quantize(k, wT, n_chunks, chunk_cols, wq, scal_ap, col, pst2,
                    lim, tmp_scal):
    nc = k.nc
    with nc.Block() as block:
        @block.vector
        def _(v):
            v.reciprocal(out=tmp_scal, in_=scal_ap)
            v.drain()

    k.bcast_scalar_to_col(tmp_scal, col, pst2)

    wf_cm = nc.sbuf_tensor(k.name("wq_wf"), [128, 2 * chunk_cols], F32)
    wf = wf_cm.__enter__().ap()
    ld2 = [k.dsem(), k.dsem()]
    rd = k.sem()
    ld20 = [k.v(ld2[0]), k.v(ld2[1])]
    rd0 = k.v(rd)
    with nc.Block() as block:
        @block.sync
        def _(s):
            for i in range(n_chunks):
                if i >= 2:
                    s.wait_ge(rd, rd0 + i - 1)
                k.inc(s.dma_start(
                    out=wf[:, (i % 2) * chunk_cols:(i % 2 + 1) * chunk_cols],
                    in_=wT[i * 128:(i + 1) * 128, :]), ld2[i % 2], 16)

        @block.vector
        def _(v):
            for i in range(n_chunks):
                v.wait_ge(ld2[i % 2], ld20[i % 2] + 16 * (i // 2 + 1))
                src = wf[:, (i % 2) * chunk_cols:(i % 2 + 1) * chunk_cols]
                dst = wq[:, i * chunk_cols:(i + 1) * chunk_cols]
                v.tensor_scalar(out=src, in0=src, scalar1=col[:, 0:1],
                                scalar2=MAGIC, op0=ALU.mult, op1=ALU.add)
                v.drain()
                v.tensor_scalar(out=src, in0=src, scalar1=MAGIC,
                                scalar2=float(lim), op0=ALU.subtract,
                                op1=ALU.min)
                v.drain()
                v.tensor_scalar(out=dst, in0=src, scalar1=float(-lim),
                                scalar2=None, op0=ALU.max, op1=ALU.bypass)
                v.drain()
                k.inc(v.nop(), rd, 1)
    wf_cm.__exit__(None, None, None)



def build_l1_ffn(k, xt, dram):
    nc = k.nc
    CT = 32

    w1T, w2T = dram["l1f_w1T"], dram["l1f_w2T"]
    convw, convb = dram["l1f_convw"], dram["l1f_convb"]
    spill = dram["l1f_spill"]

    with contextlib.ExitStack() as es0:
        wq_t = es0.enter_context(nc.sbuf_tensor("f_wq", [128, 32768], F8))
        asum_t = es0.enter_context(nc.sbuf_tensor("f_asum", [128, 32], F32))
        col_t = es0.enter_context(nc.sbuf_tensor("f_col", [128, 1], F32))
        col2_t = es0.enter_context(nc.sbuf_tensor("f_col2", [128, 1], F32))
        rowA_t = es0.enter_context(nc.sbuf_tensor("f_rowA", [1, 2048], F32))
        rowB_t = es0.enter_context(nc.sbuf_tensor("f_rowB", [1, 2048], F32))
        rowC_t = es0.enter_context(nc.sbuf_tensor("f_rowC", [1, 2048], F32))
        rowD_t = es0.enter_context(nc.sbuf_tensor("f_rowD", [1, 2048], F32))
        scal_t = es0.enter_context(nc.sbuf_tensor("f_scal", [1, 4], F32))
        pam_t = es0.enter_context(nc.sbuf_tensor("f_pam", [128, 16], F32))
        kq_t = es0.enter_context(nc.sbuf_tensor("f_kq", [128, 8 * 2048], BF16))
        cw_t = es0.enter_context(nc.sbuf_tensor("f_convw", [128, 96], F32))
        cb_t = es0.enter_context(nc.sbuf_tensor("f_convb", [128, 32], F32))
        psA_t = es0.enter_context(nc.psum_tensor("f_psA", [128, 2048], F32))
        psB_t = es0.enter_context(nc.psum_tensor("f_psB", [128, 2048], F32))

        wq, asum = wq_t.ap(), asum_t.ap()
        col, col2 = col_t.ap(), col2_t.ap()
        rowA, rowB = rowA_t.ap(), rowB_t.ap()
        rowC, rowD = rowC_t.ap(), rowD_t.ap()
        rowE, rowF = rowB, rowC
        scal, pam = scal_t.ap(), pam_t.ap()
        kq, cw, cb = kq_t.ap(), cw_t.ap(), cb_t.ap()
        psa, psb = psA_t.ap(), psB_t.ap()
        ps_row = psa[0:1, :]
        pst2 = psb[:, 0:1024]

        M1 = scal[0:1, 0:1]
        M2 = scal[0:1, 1:2]
        TMP = scal[0:1, 2:3]

        weight_meanabs(k, w1T, 8, 4096, asum, col, pst2, M1,
                       1.0 / (4096.0 * 1024.0))
        weight_quantize(k, w1T, 8, 4096, wq, M1, col, pst2, 1.0, TMP)

        ldc = k.dsem()
        with nc.Block() as block:
            @block.sync
            def _(s):
                k.inc(s.dma_start(out=cw[:, :], in_=convw[:, :]), ldc, 16)
                k.inc(s.dma_start(out=cb[:, :], in_=convb[:, :]), ldc, 16)
                s.wait_ge(ldc, k.v(ldc))

        with contextlib.ExitStack() as es1:
            sqa_t = es1.enter_context(nc.sbuf_tensor("f_sqa", [128, 2048], F32))
            sq_bufs = [sqa_t.ap()]

            k.ln_rows(xt, rowA, rowB, rowC, sq_bufs, ps_row)
            k.part_reduce(xt, rowC, pst2, ps_row, col, col2, pam, mode="max")
            k.part_reduce(xt, rowD, pst2, ps_row, col, col2, pam, mode="min")

            with nc.Block() as block:
                @block.vector
                def _(v):
                    v.tensor_tensor(out=rowC[:, :], in0=rowC[:, :],
                                    in1=rowA[:, :], op=ALU.subtract)
                    v.tensor_tensor(out=rowD[:, :], in0=rowA[:, :],
                                    in1=rowD[:, :], op=ALU.subtract)
                    v.drain()
                    v.tensor_tensor(out=rowC[:, :], in0=rowC[:, :],
                                    in1=rowD[:, :], op=ALU.max)
                    v.drain()
                    v.tensor_tensor(out=rowC[:, :], in0=rowC[:, :],
                                    in1=rowB[:, :], op=ALU.mult)
                    v.drain()
                    v.tensor_scalar(out=rowC[:, :], in0=rowC[:, :],
                                    scalar1=1e-5, scalar2=None,
                                    op0=ALU.max, op1=ALU.bypass)
                    v.drain()
                    v.reciprocal(out=rowD[:, :], in_=rowC[:, :])
                    v.drain()
                    v.tensor_scalar(out=rowD[:, :], in0=rowD[:, :],
                                    scalar1=127.0, scalar2=None,
                                    op0=ALU.mult, op1=ALU.bypass)
                    v.drain()
                    v.tensor_tensor(out=rowE[:, :], in0=rowB[:, :],
                                    in1=rowD[:, :], op=ALU.mult)
                    v.drain()
                    v.scalar_tensor_tensor(out=rowF[:, :], in0=rowC[:, :],
                                           scalar=M1, in1=rowC[:, :],
                                           op0=ALU.mult, op1=ALU.bypass)
                    v.drain()
                    v.tensor_scalar(out=rowF[:, :], in0=rowF[:, :],
                                    scalar1=1.0 / 127.0, scalar2=None,
                                    op0=ALU.mult, op1=ALU.bypass)
                    v.drain()

            mq = k.sem()
            mq0 = k.v(mq)
            with nc.Block() as block:
                @block.tensor
                def _(t):
                    k.bcast_row(t, rowA, psa, 2048)
                    k.inc(k.bcast_row(t, rowE, psb, 2048), mq, 1)

                @block.vector
                def _(v):
                    v.wait_ge(mq, mq0 + 1)
                    for i in range(8):
                        v.tensor_tensor(out=sq_bufs[0][:, :], in0=xt[i][:, :],
                                        in1=psa[:, :], op=ALU.subtract)
                        v.drain()
                        v.tensor_tensor(out=sq_bufs[0][:, :],
                                        in0=sq_bufs[0][:, :],
                                        in1=psb[:, :], op=ALU.mult)
                        v.drain()
                        v.tensor_scalar(out=sq_bufs[0][:, :],
                                        in0=sq_bufs[0][:, :],
                                        scalar1=MAGIC, scalar2=MAGIC,
                                        op0=ALU.add, op1=ALU.subtract)
                        v.drain()
                        v.tensor_scalar(out=kq[:, i * 2048:(i + 1) * 2048],
                                        in0=sq_bufs[0][:, :],
                                        scalar1=127.0, scalar2=-127.0,
                                        op0=ALU.min, op1=ALU.max)
                        v.drain()

        with contextlib.ExitStack() as es2:
            rs1b_t = es2.enter_context(nc.sbuf_tensor("f_rs1b", [128, 2048], F32))
            h_t = es2.enter_context(nc.sbuf_tensor("f_h", [128, 2048], F32))
            z1_t = es2.enter_context(nc.sbuf_tensor("f_z1", [128, 2048], F32))
            z2_t = es2.enter_context(nc.sbuf_tensor("f_z2", [128, 2048], F32))
            hmax_t = es2.enter_context(nc.sbuf_tensor("f_hmax", [128, 2048], F32))

            rs1b = rs1b_t.ap()
            h, z1, z2 = h_t.ap(), z1_t.ap(), z2_t.ap()
            z3, h2o, habs = z1, z2, z1
            hmax = hmax_t.ap()

            bq = k.sem()
            bq0 = k.v(bq)
            with nc.Block() as block:
                @block.tensor
                def _(t):
                    k.inc(k.bcast_row(t, rowF, psa, 2048), bq, 1)

                @block.vector
                def _(v):
                    v.wait_ge(bq, bq0 + 1)
                    v.tensor_copy(out=rs1b[:, :], in_=psa[:, :])
                    v.drain()

            mmd = k.sem()
            cvd = k.sem()
            sil = k.sem()
            sabs = k.sem()
            spw = k.gsem()
            mmd0, cvd0, sil0, sabs0, spw0 = (
                k.v(mmd), k.v(cvd), k.v(sil), k.v(sabs), k.v(spw))
            with nc.Block() as block:
                @block.tensor
                def _(t):
                    for j in range(CT):
                        ps = psa if j % 2 == 0 else psb
                        if j >= 2:
                            t.wait_ge(cvd, cvd0 + j - 1)
                        ins = None
                        for i in range(8):
                            lhs = wq[:, i * 4096 + j * 128:i * 4096 + (j + 1) * 128]
                            for q in range(NT):
                                ins = t.matmul(
                                    ps[:, q * 512:(q + 1) * 512], lhs,
                                    kq[:, i * 2048 + q * 512:i * 2048 + (q + 1) * 512],
                                    start=(i == 0), stop=(i == 7))
                        k.inc(ins, mmd, 1)

                @block.vector
                def _(v):
                    for j in range(CT):
                        ps = psa if j % 2 == 0 else psb
                        v.wait_ge(mmd, mmd0 + j + 1)
                        if j >= 1:
                            v.wait_ge(sabs, sabs0 + 2 * j)
                            v.wait_ge(spw, spw0 + 16 * j)
                        v.tensor_tensor(out=h[:, :], in0=ps[:, :],
                                        in1=rs1b[:, :], op=ALU.mult)
                        v.drain()
                        k.inc(v.nop(), cvd, 1)
                        v.tensor_scalar(out=z1[:, :], in0=h[:, :],
                                        scalar1=cw[:, 3 * j + 1:3 * j + 2],
                                        scalar2=cb[:, j:j + 1],
                                        op0=ALU.mult, op1=ALU.add)
                        v.drain()
                        v.tensor_copy(out=z2[:, 0:1], in_=z1[:, 0:1])
                        v.scalar_tensor_tensor(out=z2[:, 1:2048],
                                               in0=h[:, 0:2047],
                                               scalar=cw[:, 3 * j:3 * j + 1],
                                               in1=z1[:, 1:2048],
                                               op0=ALU.mult, op1=ALU.add)
                        v.drain()
                        v.tensor_copy(out=z3[:, 2047:2048], in_=z2[:, 2047:2048])
                        v.scalar_tensor_tensor(out=z3[:, 0:2047],
                                               in0=h[:, 1:2048],
                                               scalar=cw[:, 3 * j + 2:3 * j + 3],
                                               in1=z2[:, 0:2047],
                                               op0=ALU.mult, op1=ALU.add)
                        v.drain()
                        k.inc(v.nop(), sil, 1)
                        v.wait_ge(sabs, sabs0 + 2 * j + 2)
                        if j == 0:
                            v.tensor_copy(out=hmax[:, :], in_=habs[:, :])
                        else:
                            v.tensor_tensor(out=hmax[:, :], in0=hmax[:, :],
                                            in1=habs[:, :], op=ALU.max)
                        v.drain()

                @block.scalar
                def _(s):
                    for j in range(CT):
                        s.wait_ge(sil, sil0 + j + 1)
                        if j >= 1:
                            s.wait_ge(spw, spw0 + 16 * j)
                        s.activation(out=h2o[:, :], in_=z3[:, :], func=AF.Silu)
                        s.drain()
                        k.inc(s.nop(), sabs, 1)
                        s.activation(out=habs[:, :], in_=h2o[:, :], func=AF.Abs)
                        s.drain()
                        k.inc(s.nop(), sabs, 1)

                @block.gpsimd
                def _(g):
                    for j in range(CT):
                        g.wait_ge(sabs, sabs0 + 2 * j + 1)
                        k.inc(g.dma_start(out=spill[j, :, :], in_=h2o[:, :]),
                              spw, 16)
                        g.wait_ge(spw, spw0 + 16 * (j + 1))

            k.part_reduce([hmax], rowC, pst2, ps_row, col, col2, pam, mode="absmax")

        weight_meanabs(k, w2T, 32, 1024, asum, col, pst2, M2,
                       1.0 / (4096.0 * 1024.0))
        weight_quantize(k, w2T, 32, 1024, wq, M2, col, pst2, 1.0, TMP)

        with nc.Block() as block:
            @block.vector
            def _(v):
                v.tensor_scalar(out=rowC[:, :], in0=rowC[:, :],
                                scalar1=1e-5, scalar2=None,
                                op0=ALU.max, op1=ALU.bypass)
                v.drain()
                v.reciprocal(out=rowD[:, :], in_=rowC[:, :])
                v.drain()
                v.tensor_scalar(out=rowD[:, :], in0=rowD[:, :],
                                scalar1=127.0, scalar2=None,
                                op0=ALU.mult, op1=ALU.bypass)
                v.scalar_tensor_tensor(out=rowC[:, :], in0=rowC[:, :],
                                       scalar=M2, in1=rowC[:, :],
                                       op0=ALU.mult, op1=ALU.bypass)
                v.drain()
                v.tensor_scalar(out=rowC[:, :], in0=rowC[:, :],
                                scalar1=1.0 / 127.0, scalar2=None,
                                op0=ALU.mult, op1=ALU.bypass)
                v.drain()

        with contextlib.ExitStack() as es3:
            h2s_t = es3.enter_context(nc.sbuf_tensor("f_h2s", [128, 1024], F32))
            k2_t = es3.enter_context(nc.sbuf_tensor("f_k2", [128, 1024], BF16))
            sb_t = es3.enter_context(nc.sbuf_tensor("f_sb", [128, 512], F32))
            rsb_t = es3.enter_context(nc.sbuf_tensor("f_rsb", [128, 512], F32))
            t1_t = es3.enter_context(nc.sbuf_tensor("f_t1", [128, 512], F32))

            h2s_f, k2_f = h2s_t.ap(), k2_t.ap()
            h2s = [h2s_f[:, 0:512], h2s_f[:, 512:1024]]
            k2 = [k2_f[:, 0:512], k2_f[:, 512:1024]]
            sbb, rsb, t1 = sb_t.ap(), rsb_t.ap(), t1_t.ap()

            for tt in range(4):
                tsl = slice(tt * 512, (tt + 1) * 512)
                bq = k.sem()
                bq0 = k.v(bq)
                with nc.Block() as block:
                    @block.tensor
                    def _(t):
                        k.bcast_row(t, rowD[0:1, tsl], psa[:, 0:512], 512)
                        k.inc(k.bcast_row(t, rowC[0:1, tsl], psa[:, 512:1024],
                                          512), bq, 1)

                    @block.vector
                    def _(v):
                        v.wait_ge(bq, bq0 + 1)
                        v.tensor_copy(out=sbb[:, :], in_=psa[:, 0:512])
                        v.tensor_copy(out=rsb[:, :], in_=psa[:, 512:1024])
                        v.drain()

                ldh2 = [k.dsem(), k.dsem()]
                kqd = k.sem()
                mme = k.sem()
                ldh20 = [k.v(ldh2[0]), k.v(ldh2[1])]
                kqd0, mme0 = k.v(kqd), k.v(mme)
                with nc.Block() as block:
                    @block.sync
                    def _(s):
                        for j in range(CT):
                            if j >= 2:
                                s.wait_ge(kqd, kqd0 + j - 1)
                            k.inc(s.dma_start(out=h2s[j % 2],
                                              in_=spill[j, :, tsl]),
                                  ldh2[j % 2], 16)

                    @block.vector
                    def _(v):
                        for j in range(CT):
                            v.wait_ge(ldh2[j % 2], ldh20[j % 2] + 16 * (j // 2 + 1))
                            if j >= 2:
                                v.wait_ge(mme, mme0 + j - 1)
                            v.tensor_tensor(out=t1[:, :], in0=h2s[j % 2],
                                            in1=sbb[:, :], op=ALU.mult)
                            v.drain()
                            v.tensor_scalar(out=t1[:, :], in0=t1[:, :],
                                            scalar1=MAGIC, scalar2=MAGIC,
                                            op0=ALU.add, op1=ALU.subtract)
                            v.drain()
                            v.tensor_scalar(out=k2[j % 2], in0=t1[:, :],
                                            scalar1=127.0, scalar2=-127.0,
                                            op0=ALU.min, op1=ALU.max)
                            v.drain()
                            k.inc(v.nop(), kqd, 1)

                    @block.tensor
                    def _(t):
                        for j in range(CT):
                            t.wait_ge(kqd, kqd0 + j + 1)
                            ins = None
                            for i in range(8):
                                lhs = wq[:, j * 1024 + i * 128:j * 1024 + (i + 1) * 128]
                                ps = (psa if i < 4 else psb)
                                ins = t.matmul(
                                    ps[:, (i % 4) * 512:(i % 4) * 512 + 512],
                                    lhs, k2[j % 2],
                                    start=(j == 0), stop=(j == CT - 1))
                            k.inc(ins, mme, 1)

                with nc.Block() as block:
                    @block.vector
                    def _(v):
                        for i in range(8):
                            ps = (psa if i < 4 else psb)
                            v.tensor_tensor(
                                out=t1[:, :],
                                in0=ps[:, (i % 4) * 512:(i % 4) * 512 + 512],
                                in1=rsb[:, :], op=ALU.mult)
                            v.drain()
                            v.tensor_tensor(out=xt[i][:, tsl],
                                            in0=xt[i][:, tsl],
                                            in1=t1[:, :], op=ALU.add)
                            v.drain()



def build_l0_ffn(k, xt, dram, upto="full"):
    nc = k.nc
    JT = 32

    win = dram["l0f_win"]
    bin_c = dram["l0f_binc"]
    gmax1 = dram["l0f_gmax1"]
    lmax = dram.get("l0f_lmax")
    spill = dram["l0f_spill"]

    with contextlib.ExitStack() as es0:
        psA_t = es0.enter_context(nc.psum_tensor("g_psA", [128, 2048], F32))
        psB_t = es0.enter_context(nc.psum_tensor("g_psB", [128, 2048], F32))
        amaxr_t = es0.enter_context(nc.sbuf_tensor("g_amaxr", [1, 4096], F32))
        scal_t = es0.enter_context(nc.sbuf_tensor("g_scal", [1, 8], F32))
        gmx_t = es0.enter_context(nc.sbuf_tensor("g_gmx", [1, 16], F32))
        col_t = es0.enter_context(nc.sbuf_tensor("g_col", [128, 1], F32))
        col2_t = es0.enter_context(nc.sbuf_tensor("g_col2", [128, 1], F32))
        s2col_t = es0.enter_context(nc.sbuf_tensor("g_s2col", [128, 1], F32))
        g2c_t = es0.enter_context(nc.sbuf_tensor("g_g2c", [128, 1], F32))
        pam_t = es0.enter_context(nc.sbuf_tensor("g_pam", [128, 32], F32))
        deqc_t = es0.enter_context(nc.sbuf_tensor("g_deqc", [128, 32], F32))
        binl_t = es0.enter_context(nc.sbuf_tensor("g_binl", [128, 32], F32))
        boutc_t = es0.enter_context(nc.sbuf_tensor("g_boutc", [128, 8], F32))

        psa, psb = psA_t.ap(), psB_t.ap()
        ps_row = psa[0:1, :]
        pst2 = psb[:, 0:1024]
        amaxr = amaxr_t.ap()
        scal, gmx = scal_t.ap(), gmx_t.ap()
        col, col2, s2col = col_t.ap(), col2_t.ap(), s2col_t.ap()
        g2c = g2c_t.ap()
        pam, deqc, binl, boutc = pam_t.ap(), deqc_t.ap(), binl_t.ap(), boutc_t.ap()

        G1 = scal[0:1, 0:1]
        G2 = scal[0:1, 1:2]
        SX = scal[0:1, 2:3]
        TMP = scal[0:1, 3:4]
        LM1 = scal[0:1, 4:5]
        LM2 = scal[0:1, 5:6]

        ldc = k.dsem()
        with nc.Block() as block:
            @block.gpsimd
            def _(g):
                g.memset(scal[0:1, :], 0.0)
                g.drain()

            @block.sync
            def _(s):
                k.inc(s.dma_start(out=binl[:, :], in_=bin_c[:, :]), ldc, 16)
                k.inc(s.dma_start(out=gmx[0:1, 0:8], in_=gmax1[0:1, :]), ldc, 16)
                if upto == "full":
                    k.inc(s.dma_start(out=gmx[0:1, 8:16],
                                      in_=dram["l0f_gmax2"][0:1, :]), ldc, 16)
                    k.inc(s.dma_start(out=boutc[:, :],
                                      in_=dram["l0f_boutc"][:, :]), ldc, 16)
                s.wait_ge(ldc, k.v(ldc))

        with nc.Block() as block:
            @block.vector
            def _(v):
                v.tensor_reduce(out=G1, in_=gmx[0:1, 0:8], axis=AX.X,
                                op=ALU.max)
                v.drain()
                v.tensor_scalar(out=G1, in0=G1, scalar1=1e-8, scalar2=None,
                                op0=ALU.max, op1=ALU.bypass)
                v.drain()
                v.reciprocal(out=SX, in_=G1)
                v.drain()
                v.tensor_scalar(out=SX, in0=SX, scalar1=32767.0, scalar2=None,
                                op0=ALU.mult, op1=ALU.bypass)
                v.drain()

        with contextlib.ExitStack() as esK:
            khi_t = esK.enter_context(nc.sbuf_tensor("g_khi", [128, 16384], BF16))
            klo_t = esK.enter_context(nc.sbuf_tensor("g_klo", [128, 16384], BF16))
            khi, klo = khi_t.ap(), klo_t.ap()

            with contextlib.ExitStack() as esL:
                rowA_t = esL.enter_context(nc.sbuf_tensor("g_rowA", [1, 2048], F32))
                rowB_t = esL.enter_context(nc.sbuf_tensor("g_rowB", [1, 2048], F32))
                sq_t = esL.enter_context(nc.sbuf_tensor("g_sq", [128, 2048], F32))
                sq2_t = esL.enter_context(nc.sbuf_tensor("g_sq2", [128, 2048], F32))
                rowA, rowB, sq = rowA_t.ap(), rowB_t.ap(), sq_t.ap()
                sq2 = sq2_t.ap()

                k.ln_rows(xt, rowA, rowB, amaxr_row_tmp(amaxr), [sq], ps_row)

                with nc.Block() as block:
                    @block.vector
                    def _(v):
                        v.scalar_tensor_tensor(out=rowB[:, :], in0=rowB[:, :],
                                               scalar=SX, in1=rowB[:, :],
                                               op0=ALU.mult, op1=ALU.bypass)
                        v.drain()

                mq = k.sem()
                mq0 = k.v(mq)
                with nc.Block() as block:
                    @block.tensor
                    def _(t):
                        k.bcast_row(t, rowA, psa, 2048)
                        k.inc(k.bcast_row(t, rowB, psb, 2048), mq, 1)

                    @block.vector
                    def _(v):
                        v.wait_ge(mq, mq0 + 1)
                        for i in range(8):
                            v.tensor_tensor(out=sq[:, :], in0=xt[i][:, :],
                                            in1=psa[:, :], op=ALU.subtract)
                            v.drain()
                            v.tensor_tensor(out=sq[:, :], in0=sq[:, :],
                                            in1=psb[:, :], op=ALU.mult)
                            v.drain()
                            v.tensor_scalar(out=sq[:, :], in0=sq[:, :],
                                            scalar1=MAGIC, scalar2=MAGIC,
                                            op0=ALU.add, op1=ALU.subtract)
                            v.drain()
                            v.tensor_scalar(out=sq[:, :], in0=sq[:, :],
                                            scalar1=32767.0, scalar2=-32768.0,
                                            op0=ALU.min, op1=ALU.max)
                            v.drain()
                            v.tensor_scalar(out=sq2[:, :], in0=sq[:, :],
                                            scalar1=1.0 / 256.0, scalar2=MAGIC,
                                            op0=ALU.mult, op1=ALU.add)
                            v.drain()
                            v.tensor_scalar(out=khi[:, i * 2048:(i + 1) * 2048],
                                            in0=sq2[:, :],
                                            scalar1=MAGIC, scalar2=256.0,
                                            op0=ALU.subtract, op1=ALU.mult)
                            v.drain()
                            v.tensor_tensor(out=klo[:, i * 2048:(i + 1) * 2048],
                                            in0=sq[:, :],
                                            in1=khi[:, i * 2048:(i + 1) * 2048],
                                            op=ALU.subtract)
                            v.drain()

            with contextlib.ExitStack() as es2:
                acc_t = es2.enter_context(nc.sbuf_tensor("g_acc", [128, 4096], F32))
                wfa_t = es2.enter_context(nc.sbuf_tensor("g_wfa", [128, 8192], F32))
                acc, wfa = acc_t.ap(), wfa_t.ap()
                ld2 = [k.dsem(), k.dsem()]
                rd = k.sem()
                ld20 = [k.v(ld2[0]), k.v(ld2[1])]
                rd0 = k.v(rd)
                with nc.Block() as block:
                    @block.sync
                    def _(s):
                        for i in range(8):
                            if i >= 2:
                                s.wait_ge(rd, rd0 + i - 1)
                            k.inc(s.dma_start(
                                out=wfa[:, (i % 2) * 4096:(i % 2 + 1) * 4096],
                                in_=win[i * 128:(i + 1) * 128, :]),
                                ld2[i % 2], 16)

                    @block.vector
                    def _(v):
                        for i in range(8):
                            v.wait_ge(ld2[i % 2], ld20[i % 2] + 16 * (i // 2 + 1))
                            w = wfa[:, (i % 2) * 4096:(i % 2 + 1) * 4096]
                            if i == 0:
                                v.tensor_scalar(out=acc[:, :], in0=w,
                                                scalar1=-1.0, scalar2=None,
                                                op0=ALU.mult, op1=ALU.bypass)
                                v.drain()
                                v.tensor_tensor(out=acc[:, :], in0=acc[:, :],
                                                in1=w, op=ALU.max)
                                v.drain()
                            else:
                                v.scalar_tensor_tensor(out=acc[:, :], in0=w,
                                                       scalar=-1.0, in1=acc[:, :],
                                                       op0=ALU.mult, op1=ALU.max)
                                v.drain()
                                v.scalar_tensor_tensor(out=acc[:, :], in0=w,
                                                       scalar=1.0, in1=acc[:, :],
                                                       op0=ALU.mult, op1=ALU.max)
                                v.drain()
                            k.inc(v.nop(), rd, 1)

                k.part_reduce([acc], amaxr, pst2, ps_row, col, col2, pam,
                              mode="absmax")

            with nc.Block() as block:
                @block.vector
                def _(v):
                    v.tensor_scalar(out=amaxr[:, :], in0=amaxr[:, :],
                                    scalar1=1e-8, scalar2=None,
                                    op0=ALU.max, op1=ALU.bypass)
                    v.drain()
                    v.tensor_scalar(out=TMP, in0=G1,
                                    scalar1=1.0 / (32767.0 * 127.0),
                                    scalar2=None, op0=ALU.mult, op1=ALU.bypass)
                    v.drain()

            k.bcast_scalar_to_col(TMP, col2, pst2)
            t2 = k.sem()
            c2 = k.sem()
            t20, c20 = k.v(t2), k.v(c2)
            with nc.Block() as block:
                @block.tensor
                def _(t):
                    for s in range(32):
                        if s >= 2:
                            t.wait_ge(c2, c20 + s - 1)
                        k.inc(t.transpose(
                            pst2[:, (s % 2) * 512:(s % 2) * 512 + 1],
                            amaxr[0:1, s * 128:(s + 1) * 128],
                            k.ident[0:1, 0:1]), t2, 1)

                @block.vector
                def _(v):
                    for s in range(32):
                        v.wait_ge(t2, t20 + s + 1)
                        v.tensor_scalar(
                            out=deqc[:, s:s + 1],
                            in0=pst2[:, (s % 2) * 512:(s % 2) * 512 + 1],
                            scalar1=col2[:, 0:1], scalar2=None,
                            op0=ALU.mult, op1=ALU.bypass)
                        v.drain()
                        k.inc(v.nop(), c2, 1)

            with nc.Block() as block:
                @block.vector
                def _(v):
                    v.reciprocal(out=amaxr[:, :], in_=amaxr[:, :])
                    v.drain()
                    v.tensor_scalar(out=amaxr[:, :], in0=amaxr[:, :],
                                    scalar1=127.0, scalar2=None,
                                    op0=ALU.mult, op1=ALU.bypass)
                    v.drain()

            with contextlib.ExitStack() as esM:
                sfull_t = esM.enter_context(nc.sbuf_tensor("g_sfull", [128, 4096], F32))
                sfull = sfull_t.ap()
                bs = k.sem()
                bs0 = k.v(bs)
                with nc.Block() as block:
                    @block.tensor
                    def _(t):
                        for h in range(2):
                            t.wait_ge(bs, bs0 + 2 * h)
                            k.inc(k.bcast_row(t, amaxr[0:1, h * 2048:(h + 1) * 2048],
                                              psa, 2048), bs, 1)

                    @block.vector
                    def _(v):
                        for h in range(2):
                            v.wait_ge(bs, bs0 + 2 * h + 1)
                            v.tensor_copy(out=sfull[:, h * 2048:(h + 1) * 2048],
                                          in_=psa[:, :])
                            v.drain()
                            k.inc(v.nop(), bs, 1)

                with contextlib.ExitStack() as es3:
                    wst_t = es3.enter_context(nc.sbuf_tensor("g_wst", [128, 2048], F32))
                    mst_t = es3.enter_context(nc.sbuf_tensor("g_mst", [128, 2048], BF16))
                    hid_t = es3.enter_context(nc.sbuf_tensor("g_hid", [128, 2048], F32))
                    habs_t = es3.enter_context(nc.sbuf_tensor("g_habs", [128, 2048], F32))
                    hmax_t = es3.enter_context(nc.sbuf_tensor("g_hmax", [128, 2048], F32))
                    wst, mst = wst_t.ap(), mst_t.ap()
                    hid, habs, hmax = hid_t.ap(), habs_t.ap(), hmax_t.ap()
                    wsl = [wst[:, 0:1024], wst[:, 1024:2048]]
                    msl = [mst[:, 0:1024], mst[:, 1024:2048]]

                    ldw2 = [k.dsem(), k.dsem()]
                    qd = k.sem()
                    mmd = k.sem()
                    gel = k.sem()
                    spw = k.gsem()
                    mxd = k.sem()
                    ldw20 = [k.v(ldw2[0]), k.v(ldw2[1])]
                    qd0, mmd0, gel0, spw0, mxd0 = (
                        k.v(qd), k.v(mmd), k.v(gel), k.v(spw), k.v(mxd))

                    with nc.Block() as block:
                        @block.sync
                        def _(s):
                            for j in range(JT):
                                if j >= 2:
                                    s.wait_ge(qd, qd0 + j - 1)
                                k.inc(s.dma_start(
                                    out=wsl[j % 2].rearrange(
                                        "p (c j) -> p c j", c=8),
                                    in_=win[:, j * 128:(j + 1) * 128].rearrange(
                                        "(c p) j -> p c j", p=128)),
                                    ldw2[j % 2], 16)

                        @block.vector
                        def _(v):
                            def fold(m):
                                v.wait_ge(gel, gel0 + 2 * m + 2)
                                if m == 0:
                                    v.tensor_copy(out=hmax[:, :], in_=habs[:, :])
                                else:
                                    v.tensor_tensor(out=hmax[:, :],
                                                    in0=hmax[:, :],
                                                    in1=habs[:, :], op=ALU.max)
                                v.drain()
                                k.inc(v.nop(), mxd, 1)

                            for j in range(JT):
                                v.wait_ge(ldw2[j % 2],
                                          ldw20[j % 2] + 16 * (j // 2 + 1))
                                if j >= 2:
                                    v.wait_ge(mmd, mmd0 + j - 1)
                                for c in range(8):
                                    v.tensor_tensor(
                                        out=wsl[j % 2][:, c * 128:(c + 1) * 128],
                                        in0=wsl[j % 2][:, c * 128:(c + 1) * 128],
                                        in1=sfull[:, j * 128:(j + 1) * 128],
                                        op=ALU.mult)
                                v.drain()
                                v.tensor_scalar(out=wsl[j % 2], in0=wsl[j % 2],
                                                scalar1=MAGIC, scalar2=MAGIC,
                                                op0=ALU.add, op1=ALU.subtract)
                                v.drain()
                                v.tensor_scalar(out=msl[j % 2], in0=wsl[j % 2],
                                                scalar1=127.0, scalar2=-128.0,
                                                op0=ALU.min, op1=ALU.max)
                                v.drain()
                                k.inc(v.nop(), qd, 1)
                                if j >= 1:
                                    fold(j - 1)
                            fold(JT - 1)

                        @block.tensor
                        def _(t):
                            for j in range(JT):
                                ps = psa if j % 2 == 0 else psb
                                t.wait_ge(qd, qd0 + j + 1)
                                if j >= 2:
                                    t.wait_ge(gel, gel0 + 2 * (j - 1))
                                ins = None
                                for c in range(8):
                                    lhs = msl[j % 2][:, c * 128:(c + 1) * 128]
                                    for q in range(NT):
                                        pss = ps[:, q * 512:(q + 1) * 512]
                                        ins = t.matmul(
                                            pss, lhs,
                                            khi[:, c * 2048 + q * 512:c * 2048 + (q + 1) * 512],
                                            start=(c == 0), stop=False)
                                        ins = t.matmul(
                                            pss, lhs,
                                            klo[:, c * 2048 + q * 512:c * 2048 + (q + 1) * 512],
                                            start=False, stop=(c == 7))
                                k.inc(ins, mmd, 1)

                        @block.scalar
                        def _(s):
                            for j in range(JT):
                                ps = psa if j % 2 == 0 else psb
                                s.wait_ge(mmd, mmd0 + j + 1)
                                if j >= 1:
                                    s.wait_ge(spw, spw0 + 16 * j)
                                    s.wait_ge(mxd, mxd0 + j)
                                s.activation(out=hid[:, :], in_=ps[:, :],
                                             func=AF.Gelu,
                                             bias=binl[:, j:j + 1],
                                             scale=deqc[:, j:j + 1])
                                s.drain()
                                k.inc(s.nop(), gel, 1)
                                s.activation(out=habs[:, :], in_=hid[:, :],
                                             func=AF.Abs)
                                s.drain()
                                k.inc(s.nop(), gel, 1)

                        @block.gpsimd
                        def _(g):
                            for j in range(JT):
                                g.wait_ge(gel, gel0 + 2 * j + 1)
                                k.inc(g.dma_start(out=spill[j, :, :],
                                                  in_=hid[:, :]), spw, 16)
                                g.wait_ge(spw, spw0 + 16 * (j + 1))

                    k.part_reduce([hmax], amaxr[0:1, 0:2048], pst2, ps_row,
                                  col, col2, pam, mode="absmax")
                    with nc.Block() as block:
                        @block.vector
                        def _(v):
                            v.tensor_reduce(out=LM2, in_=amaxr[0:1, 0:2048],
                                            axis=AX.X, op=ALU.max)
                            v.drain()

        if upto == "max2":
            stm = k.gsem()
            with nc.Block() as block:
                @block.gpsimd
                def _(g):
                    k.inc(g.dma_start(out=lmax[0:1, :], in_=scal[0:1, 4:8]),
                          stm, 16)
                    g.wait_ge(stm, k.v(stm))
            return

        with nc.Block() as block:
            @block.vector
            def _(v):
                v.tensor_reduce(out=G2, in_=gmx[0:1, 8:16], axis=AX.X,
                                op=ALU.max)
                v.drain()
                v.tensor_scalar(out=G2, in0=G2, scalar1=1e-8, scalar2=None,
                                op0=ALU.max, op1=ALU.bypass)
                v.drain()
                v.reciprocal(out=TMP, in_=G2)
                v.drain()
                v.tensor_scalar(out=TMP, in0=TMP, scalar1=32767.0,
                                scalar2=None, op0=ALU.mult, op1=ALU.bypass)
                v.drain()

        k.bcast_scalar_to_col(TMP, s2col, pst2)
        with nc.Block() as block:
            @block.vector
            def _(v):
                v.tensor_scalar(out=TMP, in0=G2,
                                scalar1=1.0 / (32767.0 * 127.0), scalar2=None,
                                op0=ALU.mult, op1=ALU.bypass)
                v.drain()
        k.bcast_scalar_to_col(TMP, g2c, pst2)

        wout = dram["l0f_wout"]
        with contextlib.ExitStack() as es4:
            m2_t = es4.enter_context(nc.sbuf_tensor("g_m2", [128, 32768], BF16))
            wfb_t = es4.enter_context(nc.sbuf_tensor("g_wfb", [128, 2048], F32))
            m2, wfb = m2_t.ap(), wfb_t.ap()
            wbl = [wfb[:, 0:1024], wfb[:, 1024:2048]]

            ldb2 = [k.dsem(), k.dsem()]
            rdb = k.sem()
            ldb20 = [k.v(ldb2[0]), k.v(ldb2[1])]
            rdb0 = k.v(rdb)
            with nc.Block() as block:
                @block.sync
                def _(s):
                    for i in range(32):
                        if i >= 2:
                            s.wait_ge(rdb, rdb0 + i - 1)
                        k.inc(s.dma_start(out=wbl[i % 2],
                                          in_=wout[i * 128:(i + 1) * 128, :]),
                              ldb2[i % 2], 16)

                @block.vector
                def _(v):
                    for i in range(32):
                        v.wait_ge(ldb2[i % 2], ldb20[i % 2] + 16 * (i // 2 + 1))
                        w = wbl[i % 2]
                        v.tensor_reduce(out=col[:, :], in_=w, axis=AX.X,
                                        op=ALU.max, apply_absolute_value=True)
                        v.drain()
                        v.tensor_scalar(out=col[:, :], in0=col[:, :],
                                        scalar1=1e-8, scalar2=None,
                                        op0=ALU.max, op1=ALU.bypass)
                        v.drain()
                        v.tensor_tensor(out=deqc[:, i:i + 1], in0=col[:, :],
                                        in1=g2c[:, 0:1], op=ALU.mult)
                        v.drain()
                        v.reciprocal(out=col2[:, :], in_=col[:, :])
                        v.drain()
                        v.tensor_scalar(out=col2[:, :], in0=col2[:, :],
                                        scalar1=127.0, scalar2=None,
                                        op0=ALU.mult, op1=ALU.bypass)
                        v.drain()
                        v.tensor_scalar(out=w, in0=w, scalar1=col2[:, 0:1],
                                        scalar2=MAGIC, op0=ALU.mult,
                                        op1=ALU.add)
                        v.drain()
                        v.tensor_scalar(out=w, in0=w, scalar1=MAGIC,
                                        scalar2=127.0, op0=ALU.subtract,
                                        op1=ALU.min)
                        v.drain()
                        v.tensor_scalar(out=m2[:, i * 1024:(i + 1) * 1024],
                                        in0=w, scalar1=-128.0, scalar2=None,
                                        op0=ALU.max, op1=ALU.bypass)
                        v.drain()
                        k.inc(v.nop(), rdb, 1)

            with contextlib.ExitStack() as es5:
                h2s_t = es5.enter_context(nc.sbuf_tensor("g_h2s", [128, 1024], F32))
                kfh_t = es5.enter_context(nc.sbuf_tensor("g_kfh", [128, 1024], BF16))
                kfl_t = es5.enter_context(nc.sbuf_tensor("g_kfl", [128, 1024], BF16))
                t1_t = es5.enter_context(nc.sbuf_tensor("g_t1", [128, 512], F32))
                h2s_f = h2s_t.ap()
                kfh_f, kfl_f = kfh_t.ap(), kfl_t.ap()
                t1 = t1_t.ap()
                h2s = [h2s_f[:, 0:512], h2s_f[:, 512:1024]]
                kfh = [kfh_f[:, 0:512], kfh_f[:, 512:1024]]
                kfl = [kfl_f[:, 0:512], kfl_f[:, 512:1024]]

                for tt in range(4):
                    tsl = slice(tt * 512, (tt + 1) * 512)
                    ldh2 = [k.dsem(), k.dsem()]
                    kqd = k.sem()
                    mme = k.sem()
                    ldh20 = [k.v(ldh2[0]), k.v(ldh2[1])]
                    kqd0, mme0 = k.v(kqd), k.v(mme)
                    with nc.Block() as block:
                        @block.sync
                        def _(s):
                            for j in range(JT):
                                if j >= 2:
                                    s.wait_ge(kqd, kqd0 + j - 1)
                                k.inc(s.dma_start(out=h2s[j % 2],
                                                  in_=spill[j, :, tsl]),
                                      ldh2[j % 2], 16)

                        @block.vector
                        def _(v):
                            for j in range(JT):
                                v.wait_ge(ldh2[j % 2],
                                          ldh20[j % 2] + 16 * (j // 2 + 1))
                                if j >= 2:
                                    v.wait_ge(mme, mme0 + j - 1)
                                v.tensor_scalar(out=t1[:, :], in0=h2s[j % 2],
                                                scalar1=s2col[:, 0:1],
                                                scalar2=MAGIC,
                                                op0=ALU.mult, op1=ALU.add)
                                v.drain()
                                v.tensor_scalar(out=t1[:, :], in0=t1[:, :],
                                                scalar1=MAGIC, scalar2=32767.0,
                                                op0=ALU.subtract, op1=ALU.min)
                                v.drain()
                                v.tensor_scalar(out=t1[:, :], in0=t1[:, :],
                                                scalar1=-32768.0,
                                                scalar2=deqc[:, j:j + 1],
                                                op0=ALU.max, op1=ALU.mult)
                                v.drain()
                                v.tensor_copy(out=kfh[j % 2], in_=t1[:, :])
                                v.drain()
                                v.tensor_tensor(out=kfl[j % 2], in0=t1[:, :],
                                                in1=kfh[j % 2],
                                                op=ALU.subtract)
                                v.drain()
                                k.inc(v.nop(), kqd, 1)

                        @block.tensor
                        def _(t):
                            for j in range(JT):
                                t.wait_ge(kqd, kqd0 + j + 1)
                                ins = None
                                for i in range(8):
                                    lhs = m2[:, j * 1024 + i * 128:j * 1024 + (i + 1) * 128]
                                    ps = (psa if i < 4 else psb)
                                    pss = ps[:, (i % 4) * 512:(i % 4) * 512 + 512]
                                    ins = t.matmul(pss, lhs, kfh[j % 2],
                                                   start=(j == 0), stop=False)
                                    ins = t.matmul(pss, lhs, kfl[j % 2],
                                                   start=False,
                                                   stop=(j == JT - 1))
                                k.inc(ins, mme, 1)

                    with nc.Block() as block:
                        @block.vector
                        def _(v):
                            for i in range(8):
                                ps = (psa if i < 4 else psb)
                                pss = ps[:, (i % 4) * 512:(i % 4) * 512 + 512]
                                v.tensor_scalar(out=t1[:, :], in0=pss,
                                                scalar1=boutc[:, i:i + 1],
                                                scalar2=None,
                                                op0=ALU.add, op1=ALU.bypass)
                                v.drain()
                                v.tensor_tensor(out=xt[i][:, tsl],
                                                in0=xt[i][:, tsl],
                                                in1=t1[:, :], op=ALU.add)
                                v.drain()


def amaxr_row_tmp(amaxr):
    return amaxr[0:1, 0:2048]


def emit_ln_absmax(k, xt, lmax_scal, rows4, pst2, pcols, col, col2, pam,
                   sq_bufs, eps_ap=None):
    nc = k.nc
    rowA, rowB, rowC, rowD = rows4
    k.ln_rows(xt, rowA, rowB, rowC, sq_bufs, pcols, eps_ap=eps_ap)
    k.part_reduce(xt, rowC, pst2, pcols, col, col2, pam, mode="max")
    k.part_reduce(xt, rowD, pst2, pcols, col, col2, pam, mode="min")
    with nc.Block() as block:
        @block.vector
        def _(v):
            v.tensor_tensor(out=rowC[:, :], in0=rowC[:, :], in1=rowA[:, :],
                            op=ALU.subtract)
            v.tensor_tensor(out=rowD[:, :], in0=rowA[:, :], in1=rowD[:, :],
                            op=ALU.subtract)
            v.drain()
            v.tensor_tensor(out=rowC[:, :], in0=rowC[:, :], in1=rowD[:, :],
                            op=ALU.max)
            v.drain()
            v.tensor_tensor(out=rowC[:, :], in0=rowC[:, :], in1=rowB[:, :],
                            op=ALU.mult)
            v.drain()
            v.tensor_reduce(out=lmax_scal, in_=rowC[:, :], axis=AX.X,
                            op=ALU.max)
            v.drain()



def build_l1_ln_tohbm(k, xt, dram):
    nc = k.nc
    with contextlib.ExitStack() as es:
        psA_t = es.enter_context(nc.psum_tensor(k.name("a_psA"), [128, 2048], F32))
        psB_t = es.enter_context(nc.psum_tensor(k.name("a_psB"), [128, 2048], F32))
        rowA_t = es.enter_context(nc.sbuf_tensor(k.name("a_rowA"), [1, 2048], F32))
        rowB_t = es.enter_context(nc.sbuf_tensor(k.name("a_rowB"), [1, 2048], F32))
        rowC_t = es.enter_context(nc.sbuf_tensor(k.name("a_rowC"), [1, 2048], F32))
        sq_t = es.enter_context(nc.sbuf_tensor(k.name("a_sq"), [128, 2048], F32))
        snh_t = es.enter_context(nc.sbuf_tensor(k.name("a_snh"), [128, 4096], BF16))
        snl_t = es.enter_context(nc.sbuf_tensor(k.name("a_snl"), [128, 4096], BF16))
        lnw_t = es.enter_context(nc.sbuf_tensor(k.name("a_lnw"), [128, 8], F32))
        lnb_t = es.enter_context(nc.sbuf_tensor(k.name("a_lnb"), [128, 8], F32))
        psa, psb = psA_t.ap(), psB_t.ap()
        rowA, rowB, rowC = rowA_t.ap(), rowB_t.ap(), rowC_t.ap()
        sq = sq_t.ap()
        snh, snl = snh_t.ap(), snl_t.ap()
        lnw, lnb = lnw_t.ap(), lnb_t.ap()

        ldc = k.dsem()
        with nc.Block() as block:
            @block.sync
            def _(s):
                k.inc(s.dma_start(out=lnw[:, :], in_=dram["l1a_lnw"][:, :]), ldc, 16)
                k.inc(s.dma_start(out=lnb[:, :], in_=dram["l1a_lnb"][:, :]), ldc, 16)
                s.wait_ge(ldc, k.v(ldc))

        k.ln_rows(xt, rowA, rowB, rowC, [sq], psa[0:1, :])

        mmb = k.sem()
        evs = k.sem()
        dms = [k.dsem(), k.dsem()]
        mmb0, evs0 = k.v(mmb), k.v(evs)
        dms0 = [k.v(dms[0]), k.v(dms[1])]
        with nc.Block() as block:
            @block.tensor
            def _(t):
                k.bcast_row(t, rowA, psa, 2048)
                k.inc(k.bcast_row(t, rowB, psb, 2048), mmb, 1)

            @block.vector
            def _(v):
                v.wait_ge(mmb, mmb0 + 1)
                for i in range(8):
                    sn = slice((i % 2) * 2048, (i % 2) * 2048 + 2048)
                    if i >= 2:
                        v.wait_ge(dms[i % 2], dms0[i % 2] + 32 * (i // 2))
                    v.tensor_tensor(out=sq[:, :], in0=xt[i][:, :],
                                    in1=psa[:, :], op=ALU.subtract)
                    v.drain()
                    v.tensor_tensor(out=sq[:, :], in0=sq[:, :],
                                    in1=psb[:, :], op=ALU.mult)
                    v.drain()
                    v.tensor_scalar(out=sq[:, :], in0=sq[:, :],
                                    scalar1=lnw[:, i:i + 1],
                                    scalar2=lnb[:, i:i + 1],
                                    op0=ALU.mult, op1=ALU.add)
                    v.drain()
                    v.tensor_copy(out=snh[:, sn], in_=sq[:, :])
                    v.drain()
                    v.tensor_tensor(out=snl[:, sn], in0=sq[:, :],
                                    in1=snh[:, sn], op=ALU.subtract)
                    v.drain()
                    k.inc(v.nop(), evs, 1)

            @block.sync
            def _(s):
                for i in range(8):
                    s.wait_ge(evs, evs0 + i + 1)
                    k.inc(s.dma_start(out=dram["l1a_xnh"][i, :, :],
                                      in_=snh[:, (i % 2) * 2048:(i % 2) * 2048 + 2048]),
                          dms[i % 2], 16)
                    k.inc(s.dma_start(out=dram["l1a_xnl"][i, :, :],
                                      in_=snl[:, (i % 2) * 2048:(i % 2) * 2048 + 2048]),
                          dms[i % 2], 16)
                s.wait_ge(dms[0], k.v(dms[0]))
                s.wait_ge(dms[1], k.v(dms[1]))


def spill_xt(k, xt, xsp):
    nc = k.nc
    spx = k.gsem()
    with nc.Block() as block:
        @block.gpsimd
        def _(g):
            for i in range(8):
                k.inc(g.dma_start(out=xsp[i, :, :], in_=xt[i][:, :]), spx, 16)
            g.wait_ge(spx, k.v(spx))


def load_xn(k, dram, xnh, xnl):
    nc = k.nc
    ldn = k.dsem()
    with nc.Block() as block:
        @block.sync
        def _(s):
            for i in range(8):
                sl = slice(i * 2048, (i + 1) * 2048)
                k.inc(s.dma_start(out=xnh[:, sl], in_=dram["l1a_xnh"][i, :, :]),
                      ldn, 16)
                k.inc(s.dma_start(out=xnl[:, sl], in_=dram["l1a_xnl"][i, :, :]),
                      ldn, 16)
            s.wait_ge(ldn, k.v(ldn))


def build_l1_lact(k, dram, xnh, xnl):
    nc = k.nc
    lactw = dram["l1a_lactw"]

    with contextlib.ExitStack() as es:
        psA_t = es.enter_context(nc.psum_tensor(k.name("a_psA"), [128, 2048], F32))
        psB_t = es.enter_context(nc.psum_tensor(k.name("a_psB"), [128, 2048], F32))
        lw_t = es.enter_context(nc.sbuf_tensor(k.name("a_lw"), [128, 2048], F32))
        pw_t = es.enter_context(nc.sbuf_tensor(k.name("a_pw"), [128, 1536], F32))
        b_gate_t = es.enter_context(nc.sbuf_tensor(k.name("a_gate"), [128, 2048], F32))
        b_sg_t = es.enter_context(nc.sbuf_tensor(k.name("a_sg"), [128, 2048], F32))
        b_g2_t = es.enter_context(nc.sbuf_tensor(k.name("a_g2"), [128, 2048], F32))
        b_dhd_t = es.enter_context(nc.sbuf_tensor(k.name("a_dhd"), [128, 2048], F32))
        b_h__t = es.enter_context(nc.sbuf_tensor(k.name("a_h_"), [128, 2048], F32))
        ctm_t = es.enter_context(nc.sbuf_tensor(k.name("a_ctm"), [128, 2048], F32))
        ttm_t = es.enter_context(nc.sbuf_tensor(k.name("a_ttm"), [128, 2048], F32))
        crec_t = es.enter_context(nc.sbuf_tensor(k.name("a_crec"), [128, 2048], F32))
        psa, psb = psA_t.ap(), psB_t.ap()
        lw, pw = lw_t.ap(), pw_t.ap()
        gate, sg, g2 = b_gate_t.ap(), b_sg_t.ap(), b_g2_t.ap()
        dhd, h_ = b_dhd_t.ap(), b_h__t.ap()
        ctm, ttm, crec = ctm_t.ap(), ttm_t.ap(), crec_t.ap()
        hid = ctm

        def W(m, c):
            return lw[:, (m * 2 + c) * 256:(m * 2 + c + 1) * 256]

        def PW(m, c):
            return pw[:, (m * 2 + c) * 256:(m * 2 + c + 1) * 256]

        for n in range(2):
            for h in range(4):
                _lact_nh(k, dram, xnh, xnl, lactw, n, h, psa, psb, lw, pw,
                         W, PW, gate, sg, g2, dhd, h_, hid, ctm, ttm, crec)


def _lact_nh(k, dram, xnh, xnl, lactw, n, h, psa, psb, lw, pw, W, PW,
             gate, sg, g2, dhd, h_, hid, ctm, ttm, crec):
    nc = k.nc
    tsl = slice(n * 1024, (n + 1) * 1024)

    def cfm(c):
        return crec[:, c * 1024:(c + 1) * 1024]

    ld = k.dsem()
    with nc.Block() as block:
        @block.sync
        def _(s):
            k.inc(s.dma_start(
                out=lw[:, :].rearrange("p (m c e) -> p m c e", m=4, c=2),
                in_=lactw[h].rearrange("m (c p) e -> p m c e", p=128)),
                ld, 16)
            s.wait_ge(ld, k.v(ld))

        @block.vector
        def _(v):
            for c in range(2):
                sl = slice((2 * h + c) * 2048 + n * 1024,
                           (2 * h + c) * 2048 + (n + 1) * 1024)
                v.tensor_tensor(out=crec[:, c * 1024:(c + 1) * 1024],
                                in0=xnh[:, sl], in1=xnl[:, sl], op=ALU.add)
                v.drain()

    mm = k.sem()
    ev = k.sem()
    mm0, ev0 = k.v(mm), k.v(ev)
    with nc.Block() as block:
        @block.tensor
        def _(t):
            for idx, (mat, neg) in enumerate([(0, False), (3, False), (1, True)]):
                ps = psa if idx % 2 == 0 else psb
                if idx == 2:
                    t.wait_ge(ev, ev0 + 1)
                ins = None
                for me in range(2):
                    for c in range(2):
                        for q in range(2):
                            ins = t.matmul(
                                ps[:, me * 1024 + q * 512:me * 1024 + (q + 1) * 512],
                                W(mat, c)[:, me * 128:(me + 1) * 128],
                                cfm(c)[:, q * 512:(q + 1) * 512],
                                start=(c == 0), stop=(c == 1))
                k.inc(ins, mm, 1)

        @block.vector
        def _(v):
            v.wait_ge(mm, mm0 + 1)
            v.tensor_copy(out=gate[:, :], in_=psa[:, :])
            v.drain()
            k.inc(v.nop(), ev, 1)
            v.wait_ge(mm, mm0 + 2)
            v.tensor_copy(out=g2[:, :], in_=psb[:, :])
            v.drain()
            v.wait_ge(mm, mm0 + 3)
            v.tensor_scalar(out=dhd[:, :], in0=psa[:, :], scalar1=-1.0,
                            scalar2=None, op0=ALU.mult, op1=ALU.bypass)
            v.drain()
            k.inc(v.nop(), ev, 1)

    sgd = k.sem()
    sgd0 = k.v(sgd)
    with nc.Block() as block:
        @block.scalar
        def _(s):
            s.activation(out=sg[:, :], in_=gate[:, :], func=AF.Sigmoid)
            s.drain()
            k.inc(s.nop(), sgd, 1)

        @block.vector
        def _(v):
            v.wait_ge(sgd, sgd0 + 1)
            v.tensor_tensor(out=h_[:, :], in0=gate[:, :], in1=sg[:, :],
                            op=ALU.mult)
            v.drain()
            v.tensor_tensor(out=hid[:, :], in0=h_[:, :], in1=g2[:, :],
                            op=ALU.mult)
            v.drain()
            v.tensor_scalar(out=ttm[:, :], in0=sg[:, :], scalar1=-1.0,
                            scalar2=1.0, op0=ALU.mult, op1=ALU.add)
            v.drain()
            v.tensor_tensor(out=ttm[:, :], in0=ttm[:, :], in1=sg[:, :],
                            op=ALU.mult)
            v.drain()
            v.tensor_tensor(out=ttm[:, :], in0=ttm[:, :], in1=gate[:, :],
                            op=ALU.mult)
            v.drain()
            v.tensor_tensor(out=ttm[:, :], in0=ttm[:, :], in1=sg[:, :],
                            op=ALU.add)
            v.drain()
            v.tensor_tensor(out=g2[:, :], in0=dhd[:, :], in1=g2[:, :],
                            op=ALU.mult)
            v.drain()
            v.tensor_tensor(out=dhd[:, :], in0=dhd[:, :], in1=h_[:, :],
                            op=ALU.mult)
            v.drain()
            v.tensor_tensor(out=gate[:, :], in0=g2[:, :], in1=ttm[:, :],
                            op=ALU.mult)
            v.drain()

    def emit_tm(src_fm, dst):
        tr = k.sem()
        rdt = k.sem()
        tr0, rdt0 = k.v(tr), k.v(rdt)
        with nc.Block() as block:
            @block.tensor
            def _(t):
                nn = 0
                for tc in range(8):
                    for c in range(2):
                        if nn >= 2:
                            t.wait_ge(rdt, rdt0 + nn - 1)
                        k.inc(t.transpose(
                            psb[:, (nn % 2) * 512:(nn % 2) * 512 + 128],
                            src_fm(c)[:, tc * 128:(tc + 1) * 128],
                            k.ident[:, :]), tr, 1)
                        nn += 1

            @block.vector
            def _(v):
                nn = 0
                for tc in range(8):
                    for c in range(2):
                        v.wait_ge(tr, tr0 + nn + 1)
                        v.tensor_copy(
                            out=dst[:, tc * 256 + c * 128:tc * 256 + (c + 1) * 128],
                            in_=psb[:, (nn % 2) * 512:(nn % 2) * 512 + 128])
                        v.drain()
                        k.inc(v.nop(), rdt, 1)
                        nn += 1

    def fmv(buf):
        return lambda c: buf[:, c * 1024:(c + 1) * 1024]

    emit_tm(fmv(hid), ttm)
    emit_tm(cfm, ctm)
    _dw_mm(k, psa, ttm, ctm, pw, 2, sign=+1.0, base_w=W(2, 0), base_w2=W(2, 1))
    emit_tm(fmv(gate), ttm)
    _dw_mm(k, psa, ctm, ttm, pw, 0, sign=-1.0, base_w=W(0, 0), base_w2=W(0, 1))
    emit_tm(fmv(dhd), ttm)
    _dw_mm(k, psa, ctm, ttm, pw, 1, sign=-1.0, base_w=W(3, 0), base_w2=W(3, 1))

    mm2 = k.sem()
    ev2 = k.sem()
    sil2 = k.sem()
    mm20, ev20, sil20 = k.v(mm2), k.v(ev2), k.v(sil2)
    with nc.Block() as block:
        @block.tensor
        def _(t):
            for idx, mat in enumerate([0, 1]):
                ps = psa if idx == 0 else psb
                ins = None
                for me in range(2):
                    for c in range(2):
                        for q in range(2):
                            ins = t.matmul(
                                ps[:, me * 1024 + q * 512:me * 1024 + (q + 1) * 512],
                                PW(mat, c)[:, me * 128:(me + 1) * 128],
                                cfm(c)[:, q * 512:(q + 1) * 512],
                                start=(c == 0), stop=(c == 1))
                k.inc(ins, mm2, 1)
            t.wait_ge(ev2, ev20 + 1)
            ins = None
            for md in range(2):
                for c in range(2):
                    for q in range(2):
                        ins = t.matmul(
                            psa[:, md * 1024 + q * 512:md * 1024 + (q + 1) * 512],
                            PW(2, c)[:, md * 128:(md + 1) * 128],
                            h_[:, c * 1024 + q * 512:c * 1024 + (q + 1) * 512],
                            start=(c == 0), stop=(c == 1))
            k.inc(ins, mm2, 1)

        @block.scalar
        def _(s):
            s.wait_ge(mm2, mm20 + 1)
            s.activation(out=sg[:, :], in_=psa[:, :], func=AF.Silu)
            s.drain()
            k.inc(s.nop(), sil2, 1)

        @block.vector
        def _(v):
            v.wait_ge(sil2, sil20 + 1)
            v.wait_ge(mm2, mm20 + 2)
            v.tensor_tensor(out=h_[:, :], in0=sg[:, :], in1=psb[:, :],
                            op=ALU.mult)
            v.drain()
            k.inc(v.nop(), ev2, 1)
            v.wait_ge(mm2, mm20 + 3)
            v.tensor_copy(out=crec[:, :], in_=psa[:, :])
            v.drain()
            k.inc(v.nop(), ev2, 1)

        @block.gpsimd
        def _(g):
            spq = k.gsem()
            g.wait_ge(ev2, k.v(ev2))
            for md in range(2):
                k.inc(g.dma_start(
                    out=dram["l1a_lactsp"][2 * h + md, :, tsl],
                    in_=crec[:, md * 1024:(md + 1) * 1024]), spq, 16)
            g.wait_ge(spq, k.v(spq))


def _dw_mm(k, psa, lhs_tm, rhs_tm, pw, pwi, sign, base_w, base_w2):
    nc = k.nc
    mm = k.sem()
    mm0 = k.v(mm)
    with nc.Block() as block:
        @block.tensor
        def _(t):
            ins = None
            for mc in range(2):
                for tc in range(8):
                    ins = t.matmul(
                        psa[:, mc * 512:mc * 512 + 256],
                        lhs_tm[:, tc * 256 + mc * 128:tc * 256 + (mc + 1) * 128],
                        rhs_tm[:, tc * 256:(tc + 1) * 256],
                        start=(tc == 0), stop=(tc == 7))
            k.inc(ins, mm, 1)

        @block.vector
        def _(v):
            v.wait_ge(mm, mm0 + 1)
            for mc, bw in ((0, base_w), (1, base_w2)):
                if sign > 0:
                    v.tensor_tensor(out=pw[:, (pwi * 2 + mc) * 256:(pwi * 2 + mc + 1) * 256],
                                    in0=bw, in1=psa[:, mc * 512:mc * 512 + 256],
                                    op=ALU.add)
                else:
                    v.tensor_tensor(out=pw[:, (pwi * 2 + mc) * 256:(pwi * 2 + mc + 1) * 256],
                                    in0=bw, in1=psa[:, mc * 512:mc * 512 + 256],
                                    op=ALU.subtract)
                v.drain()


def split_xn(k, xn, xnh, xnl):
    nc = k.nc
    with nc.Block() as block:
        @block.vector
        def _(v):
            for i in range(8):
                sl = slice(i * 2048, (i + 1) * 2048)
                v.tensor_copy(out=xnh[:, sl], in_=xn[:, sl])
                v.drain()
                v.tensor_tensor(out=xnl[:, sl], in0=xn[:, sl],
                                in1=xnh[:, sl], op=ALU.subtract)
                v.drain()


def build_l1_attn_pairs(k, dram, xnh, xnl):
    nc = k.nc
    qkvw = dram["l1a_qkvw"]
    qkvb = dram["l1a_qkvb"]

    with contextlib.ExitStack() as es:
        psA_t = es.enter_context(nc.psum_tensor(k.name("t_psA"), [128, 2048], F32))
        psB_t = es.enter_context(nc.psum_tensor(k.name("t_psB"), [128, 2048], F32))
        ws_t = es.enter_context(nc.sbuf_tensor(k.name("t_ws"), [128, 3072], F32))
        wsh_t = es.enter_context(nc.sbuf_tensor(k.name("t_wsh"), [128, 3072], BF16))
        wsl_t = es.enter_context(nc.sbuf_tensor(k.name("t_wsl"), [128, 3072], BF16))
        khe_t = es.enter_context(nc.sbuf_tensor(k.name("t_khe"), [128, 2048], BF16))
        kle_t = es.enter_context(nc.sbuf_tensor(k.name("t_kle"), [128, 2048], BF16))
        kho_t = es.enter_context(nc.sbuf_tensor(k.name("t_kho"), [128, 2048], BF16))
        klo_t = es.enter_context(nc.sbuf_tensor(k.name("t_klo"), [128, 2048], BF16))
        qhe_t = es.enter_context(nc.sbuf_tensor(k.name("t_qhe"), [128, 2048], BF16))
        qle_t = es.enter_context(nc.sbuf_tensor(k.name("t_qle"), [128, 2048], BF16))
        qho_t = es.enter_context(nc.sbuf_tensor(k.name("t_qho"), [128, 2048], BF16))
        qlo_t = es.enter_context(nc.sbuf_tensor(k.name("t_qlo"), [128, 2048], BF16))
        vhe_t = es.enter_context(nc.sbuf_tensor(k.name("t_vhe"), [128, 16 * 65], F32))
        vho_t = es.enter_context(nc.sbuf_tensor(k.name("t_vho"), [128, 16 * 65], F32))
        tmp_t = es.enter_context(nc.sbuf_tensor(k.name("t_tmp"), [128, 2048], F32))
        vtmp_t = es.enter_context(nc.sbuf_tensor(k.name("t_vtmp"), [128, 2048], F32))
        pb_t = es.enter_context(nc.sbuf_tensor(k.name("t_pb"), [128, 2048], F32))
        pam_t = es.enter_context(nc.sbuf_tensor(k.name("t_pam"), [128, 16], F32))
        bcol_t = es.enter_context(nc.sbuf_tensor(k.name("t_bcol"), [128, 24], F32))
        ones65_t = es.enter_context(nc.sbuf_tensor(k.name("t_o65"), [128, 128], F32))

        psa, psb = psA_t.ap(), psB_t.ap()
        ws, wsh, wsl = ws_t.ap(), wsh_t.ap(), wsl_t.ap()
        khe, kle, kho, klo = khe_t.ap(), kle_t.ap(), kho_t.ap(), klo_t.ap()
        qhe, qle, qho, qlo = qhe_t.ap(), qle_t.ap(), qho_t.ap(), qlo_t.ap()
        vhe, vho = vhe_t.ap(), vho_t.ap()
        tmp, vtmp, pb = tmp_t.ap(), vtmp_t.ap(), pb_t.ap()
        pam, bcol = pam_t.ap(), bcol_t.ap()
        ones65 = ones65_t.ap()

        ldb = k.dsem()
        with nc.Block() as block:
            @block.sync
            def _(s):
                k.inc(s.dma_start(out=bcol[:, :], in_=qkvb[:, :]), ldb, 16)
                s.wait_ge(ldb, k.v(ldb))

            @block.gpsimd
            def _(g):
                g.memset(ones65[:, :], 0.0)
                g.drain()
                g.memset(ones65[64:65, :], 1.0)
                g.memset(khe[64:65, :], 1.0)
                g.memset(kle[64:65, :], 0.0)
                g.memset(kho[64:65, :], 1.0)
                g.memset(klo[64:65, :], 0.0)
                g.memset(vho[:, :], 0.0)
                g.memset(vhe[:, :], 0.0)
                g.drain()
                g.memset(vho[:, :].rearrange("p (t c) -> p t c", c=65)[:, :, 64:65], 1.0)
                g.memset(vhe[:, :].rearrange("p (t c) -> p t c", c=65)[:, :, 64:65], 1.0)
                g.drain()

        for p in range(8):
            _attn_pair(k, p, dram, psa, psb, xnh, xnl, ws, wsh, wsl,
                       khe, kle, kho, klo, qhe, qle, qho, qlo, vhe, vho,
                       tmp, vtmp, pb, pam, bcol, ones65)


def _attn_pair(k, p, dram, psa, psb, xnh, xnl, ws, wsh, wsl,
               khe, kle, kho, klo, qhe, qle, qho, qlo, vhe, vho,
               tmp, vtmp, pb, pam, bcol, ones65):
    nc = k.nc
    qkvw = dram["l1a_qkvw"]

    ldw = k.dsem()
    spl = k.sem()
    ldw0, spl0 = k.v(ldw), k.v(spl)
    with nc.Block() as block:
        @block.sync
        def _(s):
            for hh in range(2):
                k.inc(s.dma_start(
                    out=ws[:, :].rearrange("p (t c h m) -> p t c h m",
                                           t=3, c=8, h=2)[:, :, :, hh, :],
                    in_=qkvw[2 * p + hh].rearrange("t (c p) m -> p t c m",
                                                   p=128)), ldw, 16)

        @block.vector
        def _(v):
            v.wait_ge(ldw, ldw0 + 32)
            v.tensor_copy(out=wsh[:, :], in_=ws[:, :])
            v.drain()
            v.tensor_tensor(out=wsl[:, :], in0=ws[:, :], in1=wsh[:, :],
                            op=ALU.subtract)
            v.drain()

    mmq = k.sem()
    evq = k.sem()
    mmq0, evq0 = k.v(mmq), k.v(evq)
    with nc.Block() as block:
        @block.tensor
        def _(t):
            for ti in range(3):
                ps = psa if ti % 2 == 0 else psb
                if ti == 2:
                    t.wait_ge(evq, evq0 + 1)
                ins = None
                for c in range(8):
                    lh = wsh[:, ti * 1024 + c * 128:ti * 1024 + (c + 1) * 128]
                    ll = wsl[:, ti * 1024 + c * 128:ti * 1024 + (c + 1) * 128]
                    for q in range(NT):
                        xh = xnh[:, c * 2048 + q * 512:c * 2048 + (q + 1) * 512]
                        xl = xnl[:, c * 2048 + q * 512:c * 2048 + (q + 1) * 512]
                        pss = ps[:, q * 512:(q + 1) * 512]
                        ins = t.matmul(pss, lh, xh, start=(c == 0), stop=False)
                        ins = t.matmul(pss, lh, xl, start=False, stop=False)
                        ins = t.matmul(pss, ll, xh, start=False, stop=(c == 7))
                k.inc(ins, mmq, 1)

        @block.vector
        def _(v):
            v.wait_ge(mmq, mmq0 + 1)
            v.tensor_scalar(out=tmp[:, :], in0=psa[:, :],
                            scalar1=bcol[:, 3 * p + 0:3 * p + 1],
                            scalar2=0.125, op0=ALU.add, op1=ALU.mult)
            v.drain()
            v.tensor_copy(out=qhe[0:64, :], in_=tmp[0:64, :])
            v.tensor_copy(out=pb[64:128, :], in_=tmp[64:128, :])
            v.drain()
            v.tensor_tensor(out=qle[0:64, :], in0=tmp[0:64, :],
                            in1=qhe[0:64, :], op=ALU.subtract)
            v.drain()
            k.inc(v.nop(), evq, 1)
            v.wait_ge(mmq, mmq0 + 2)
            v.tensor_scalar(out=tmp[:, :], in0=psb[:, :],
                            scalar1=bcol[:, 3 * p + 1:3 * p + 2],
                            scalar2=None, op0=ALU.add, op1=ALU.bypass)
            v.drain()
            v.tensor_copy(out=khe[0:64, :], in_=tmp[0:64, :])
            v.drain()
            v.tensor_tensor(out=kle[0:64, :], in0=tmp[0:64, :],
                            in1=khe[0:64, :], op=ALU.subtract)
            v.drain()
            v.wait_ge(mmq, mmq0 + 3)
            v.tensor_scalar(out=vtmp[:, :], in0=psa[:, :],
                            scalar1=bcol[:, 3 * p + 2:3 * p + 3],
                            scalar2=None, op0=ALU.add, op1=ALU.bypass)
            v.drain()

    trv = k.sem()
    rdv = k.sem()
    trv0, rdv0 = k.v(trv), k.v(rdv)
    with nc.Block() as block:
        @block.tensor
        def _(t):
            n = 0
            for hh in range(2):
                for kt in range(16):
                    if n >= 2:
                        t.wait_ge(rdv, rdv0 + n - 1)
                    k.inc(t.transpose(
                        psb[:, (n % 2) * 512:(n % 2) * 512 + 64],
                        vtmp[64 * hh:64 * hh + 64, kt * 128:(kt + 1) * 128],
                        k.ident[64 * hh:64 * hh + 64, 64 * hh:64 * hh + 64]),
                        trv, 1)
                    n += 1

        @block.vector
        def _(v):
            n = 0
            for hh in range(2):
                for kt in range(16):
                    v.wait_ge(trv, trv0 + n + 1)
                    if hh == 0:
                        dst = vhe[:, kt * 65:kt * 65 + 64]
                    else:
                        dst = vho[:, kt * 65:kt * 65 + 64]
                    v.tensor_copy(out=dst,
                                  in_=psb[:, (n % 2) * 512:(n % 2) * 512 + 64])
                    v.drain()
                    k.inc(v.nop(), rdv, 1)
                    n += 1

    flp = k.dsem()
    fsp = k.sem()
    fsp0 = k.v(fsp)
    with nc.Block() as block:
        @block.sync
        def _(s):
            k.inc(s.dma_start(out=pb[0:64, :], in_=tmp[64:128, :]), flp, 16)
            k.inc(s.dma_start(out=tmp[0:64, :], in_=pb[64:128, :]), flp, 16)
            s.wait_ge(flp, k.v(flp))

        @block.vector
        def _(v):
            v.wait_ge(flp, k.v(flp))
            v.tensor_copy(out=kho[0:64, :], in_=pb[0:64, :])
            v.tensor_copy(out=qho[0:64, :], in_=tmp[0:64, :])
            v.drain()
            v.tensor_tensor(out=klo[0:64, :], in0=pb[0:64, :],
                            in1=kho[0:64, :], op=ALU.subtract)
            v.tensor_tensor(out=qlo[0:64, :], in0=tmp[0:64, :],
                            in1=qho[0:64, :], op=ALU.subtract)
            v.drain()
            k.inc(v.nop(), fsp, 1)

    for hh in range(2):
        _attn_head(k, p, hh, psa, psb, dram,
                   (khe, kle, qhe, qle, vhe) if hh == 0 else
                   (kho, klo, qho, qlo, vho),
                   tmp, vtmp, pb, pam, ones65)


def _attn_head(k, p, hh, psa, psb, dram, bufs, tmp, vtmp, pb, pam, ones65):
    nc = k.nc
    kh, kl, qh, ql, vh = bufs
    base = 0
    xrow = 64
    kspan = (0, 65)
    vw = 65

    mma = k.sem()
    rda = k.sem()
    mma0, rda0 = k.v(mma), k.v(rda)
    with nc.Block() as block:
        @block.tensor
        def _(t):
            for qt in range(16):
                ps = psa if qt % 2 == 0 else psb
                if qt >= 2:
                    t.wait_ge(rda, rda0 + qt - 1)
                ins = None
                for q in range(NT):
                    ins = t.matmul(ps[:, q * 512:(q + 1) * 512],
                                   qh[base:base + 64, qt * 128:(qt + 1) * 128],
                                   kh[base:base + 64, q * 512:(q + 1) * 512],
                                   start=True, stop=True)
                k.inc(ins, mma, 1)

        @block.vector
        def _(v):
            for qt in range(16):
                ps = psa if qt % 2 == 0 else psb
                v.wait_ge(mma, mma0 + qt + 1)
                v.tensor_reduce(out=pam[:, qt:qt + 1], in_=ps[:, :],
                                axis=AX.X, op=ALU.max)
                v.drain()
                v.tensor_scalar(out=pam[:, qt:qt + 1], in0=pam[:, qt:qt + 1],
                                scalar1=-1.0, scalar2=None,
                                op0=ALU.mult, op1=ALU.bypass)
                v.drain()
                k.inc(v.nop(), rda, 1)

    t2 = k.sem()
    c2 = k.sem()
    t20, c20 = k.v(t2), k.v(c2)
    with nc.Block() as block:
        @block.tensor
        def _(t):
            for s in range(16):
                if s >= 4:
                    t.wait_ge(c2, c20 + s - 3)
                k.inc(t.matmul(
                    psa[xrow:xrow + 1, (s % 4) * 512:(s % 4) * 512 + 128],
                    pam[:, s:s + 1], k.ident[:, :],
                    start=True, stop=True), t2, 1)

        @block.vector
        def _(v):
            for s in range(16):
                v.wait_ge(t2, t20 + s + 1)
                v.tensor_copy(out=vtmp[xrow:xrow + 1, s * 128:(s + 1) * 128],
                              in_=psa[xrow:xrow + 1, (s % 4) * 512:(s % 4) * 512 + 128])
                v.drain()
                k.inc(v.nop(), c2, 1)
            v.tensor_copy(out=qh[xrow:xrow + 1, :], in_=vtmp[xrow:xrow + 1, :])
            v.drain()
            v.tensor_tensor(out=ql[xrow:xrow + 1, :],
                            in0=vtmp[xrow:xrow + 1, :],
                            in1=qh[xrow:xrow + 1, :], op=ALU.subtract)
            v.drain()

    kb, kk = kspan
    mmb = k.sem()
    exd = k.sem()
    pvd = k.sem()
    mmb0, exd0, pvd0 = k.v(mmb), k.v(exd), k.v(pvd)
    ob, oe = 0, 65

    def emit_pv(t, nn):
        kt, qh2 = nn // 2, nn % 2
        t.wait_ge(exd, exd0 + nn + 1)
        ins = None
        for q in range(2):
            pcol = pb[:, (nn % 2) * 1024 + q * 512:(nn % 2) * 1024 + (q + 1) * 512]
            out = psa[ob:oe, qh2 * 1024 + q * 512:qh2 * 1024 + (q + 1) * 512]
            ins = t.matmul(out, vh[:, kt * vw:(kt + 1) * vw], pcol,
                           start=(kt == 0), stop=(kt == 15))
        k.inc(ins, pvd, 1)

    with nc.Block() as block:
        @block.tensor
        def _(t):
            nn = 0
            for kt in range(16):
                for qh2 in range(2):
                    psl = psb[:, (nn % 2) * 1024:(nn % 2) * 1024 + 1024]
                    if nn >= 2:
                        t.wait_ge(exd, exd0 + nn - 1)
                    ins = None
                    for q in range(2):
                        pss = psl[:, q * 512:(q + 1) * 512]
                        qsl = slice(qh2 * 1024 + q * 512, qh2 * 1024 + (q + 1) * 512)
                        ksl = slice(kt * 128, (kt + 1) * 128)
                        ins = t.matmul(pss, kh[kb:kb + kk, ksl],
                                       qh[kb:kb + kk, qsl], start=True, stop=False)
                        ins = t.matmul(pss, kh[kb:kb + kk, ksl],
                                       ql[kb:kb + kk, qsl], start=False, stop=False)
                        ins = t.matmul(pss, kl[kb:kb + kk, ksl],
                                       qh[kb:kb + kk, qsl], start=False, stop=True)
                    k.inc(ins, mmb, 1)
                    if nn >= 1:
                        emit_pv(t, nn - 1)
                    nn += 1
            emit_pv(t, 31)

        @block.scalar
        def _(s):
            nn = 0
            for kt in range(16):
                for qh2 in range(2):
                    psl = psb[:, (nn % 2) * 1024:(nn % 2) * 1024 + 1024]
                    s.wait_ge(mmb, mmb0 + nn + 1)
                    if nn >= 2:
                        s.wait_ge(pvd, pvd0 + nn - 1)
                    s.activation(out=pb[:, (nn % 2) * 1024:(nn % 2) * 1024 + 1024],
                                 in_=psl, func=AF.Exp)
                    s.drain()
                    k.inc(s.nop(), exd, 1)
                    nn += 1

    srow = 64
    drows = (0, 64)
    bcd = k.sem()
    aod = k.dsem()
    bcd0 = k.v(bcd)
    with nc.Block() as block:
        @block.vector
        def _(v):
            v.reciprocal(out=vtmp[srow:srow + 1, :], in_=psa[srow:srow + 1, :])
            v.drain()
            k.inc(v.nop(), bcd, 1)

        @block.tensor
        def _(t):
            t.wait_ge(bcd, bcd0 + 1)
            ins = None
            for q in range(NT):
                ins = t.matmul(psb[:, q * 512:(q + 1) * 512],
                               ones65[srow:srow + 1, :],
                               vtmp[srow:srow + 1, q * 512:(q + 1) * 512],
                               start=True, stop=True)
            k.inc(ins, bcd, 1)

        @block.vector
        def _(v):
            v.wait_ge(bcd, bcd0 + 2)
            d0, d1 = drows
            v.tensor_copy(out=tmp[d0:d1, :], in_=psa[d0:d1, :])
            v.drain()
            v.tensor_tensor(out=tmp[d0:d1, :], in0=tmp[d0:d1, :],
                            in1=psb[d0:d1, :], op=ALU.mult)
            v.drain()
            k.inc(v.nop(), bcd, 1)

        @block.sync
        def _(s):
            s.wait_ge(bcd, bcd0 + 3)
            k.inc(s.dma_start(out=dram["l1a_aosp"][p, 64 * hh:64 * hh + 64, :],
                              in_=tmp[0:64, :]), aod, 16)
            s.wait_ge(aod, k.v(aod))


def build_l1_attn_outproj(k, xt, dram):
    nc = k.nc
    wot = dram["l1a_wot"]
    wob = dram["l1a_wob"]
    xsp = dram["l1a_xspill"]
    aosp = dram["l1a_aosp"]
    lactsp = dram["l1a_lactsp"]

    with contextlib.ExitStack() as es:
        psA_t = es.enter_context(nc.psum_tensor(k.name("o_psA"), [128, 2048], F32))
        psB_t = es.enter_context(nc.psum_tensor(k.name("o_psB"), [128, 2048], F32))
        aoh_t = es.enter_context(nc.sbuf_tensor(k.name("o_aoh"), [128, 16384], BF16))
        aol_t = es.enter_context(nc.sbuf_tensor(k.name("o_aol"), [128, 16384], BF16))
        wblk_t = es.enter_context(nc.sbuf_tensor(k.name("o_wblk"), [128, 2048], F32))
        wbh_t = es.enter_context(nc.sbuf_tensor(k.name("o_wbh"), [128, 2048], BF16))
        wbl_t = es.enter_context(nc.sbuf_tensor(k.name("o_wbl"), [128, 2048], BF16))
        wbc_t = es.enter_context(nc.sbuf_tensor(k.name("o_wbc"), [128, 8], F32))
        tmp_t = es.enter_context(nc.sbuf_tensor(k.name("o_tmp"), [128, 2048], F32))
        lbA_t = es.enter_context(nc.sbuf_tensor(k.name("o_lbA"), [128, 2048], F32))
        lbB_t = es.enter_context(nc.sbuf_tensor(k.name("o_lbB"), [128, 2048], F32))
        psa, psb = psA_t.ap(), psB_t.ap()
        aoh, aol = aoh_t.ap(), aol_t.ap()
        wblk, wbh, wbl = wblk_t.ap(), wbh_t.ap(), wbl_t.ap()
        wbc, tmp = wbc_t.ap(), tmp_t.ap()
        lbA, lbB = lbA_t.ap(), lbB_t.ap()

        lda = [k.dsem(), k.dsem()]
        spa = k.sem()
        lda0 = [k.v(lda[0]), k.v(lda[1])]
        spa0 = k.v(spa)
        stg = [tmp, lbA]
        with nc.Block() as block:
            @block.sync
            def _(s):
                for i in range(8):
                    if i >= 2:
                        s.wait_ge(spa, spa0 + i - 1)
                    k.inc(s.dma_start(out=stg[i % 2][:, :], in_=aosp[i, :, :]),
                          lda[i % 2], 16)
                k.inc(s.dma_start(out=wbc[:, :], in_=wob[:, :]), lda[0], 16)
                s.wait_ge(lda[0], k.v(lda[0]))
                s.wait_ge(lda[1], k.v(lda[1]))

            @block.vector
            def _(v):
                for i in range(8):
                    blk = slice(i * 2048, (i + 1) * 2048)
                    v.wait_ge(lda[i % 2], lda0[i % 2] + 16 * (i // 2 + 1))
                    v.tensor_copy(out=aoh[:, blk], in_=stg[i % 2][:, :])
                    v.drain()
                    v.tensor_tensor(out=aol[:, blk], in0=stg[i % 2][:, :],
                                    in1=aoh[:, blk], op=ALU.subtract)
                    v.drain()
                    k.inc(v.nop(), spa, 1)

        ldw = [k.dsem(), k.dsem()]
        ldx = k.dsem()
        ldl = [k.gsem(), k.gsem()]
        spw = k.sem()
        mmo = k.sem()
        evo = k.sem()
        ldw0 = [k.v(ldw[0]), k.v(ldw[1])]
        ldl0 = [k.v(ldl[0]), k.v(ldl[1])]
        ldx0, spw0, mmo0, evo0 = k.v(ldx), k.v(spw), k.v(mmo), k.v(evo)
        lslot = [lbA, lbB]
        with nc.Block() as block:
            @block.sync
            def _(s):
                for dt in range(8):
                    k.inc(s.dma_start(out=xt[dt][:, :], in_=xsp[dt, :, :]),
                          ldx, 16)
                    if dt >= 2:
                        s.wait_ge(spw, spw0 + dt - 1)
                    k.inc(s.dma_start(
                        out=wblk[:, (dt % 2) * 1024:(dt % 2) * 1024 + 1024]
                            .rearrange("p (c m) -> p c m", c=8),
                        in_=wot[:, dt * 128:(dt + 1) * 128]
                            .rearrange("(c p) m -> p c m", p=128)),
                        ldw[dt % 2], 16)

            @block.gpsimd
            def _(g):
                for dt in range(8):
                    if dt >= 2:
                        g.wait_ge(evo, evo0 + dt - 1)
                    k.inc(g.dma_start(out=lslot[dt % 2][:, :],
                                      in_=lactsp[dt, :, :]), ldl[dt % 2], 16)

            @block.tensor
            def _(t):
                for dt in range(8):
                    ps = psa if dt % 2 == 0 else psb
                    t.wait_ge(spw, spw0 + dt + 1)
                    if dt >= 2:
                        t.wait_ge(evo, evo0 + dt - 1)
                    ins = None
                    for c in range(8):
                        lh = wbh[:, (dt % 2) * 1024 + c * 128:
                                 (dt % 2) * 1024 + (c + 1) * 128]
                        ll = wbl[:, (dt % 2) * 1024 + c * 128:
                                 (dt % 2) * 1024 + (c + 1) * 128]
                        for q in range(NT):
                            pss = ps[:, q * 512:(q + 1) * 512]
                            ah = aoh[:, c * 2048 + q * 512:c * 2048 + (q + 1) * 512]
                            al = aol[:, c * 2048 + q * 512:c * 2048 + (q + 1) * 512]
                            ins = t.matmul(pss, lh, ah, start=(c == 0), stop=False)
                            ins = t.matmul(pss, lh, al, start=False, stop=False)
                            ins = t.matmul(pss, ll, ah, start=False, stop=(c == 7))
                    k.inc(ins, mmo, 1)

            @block.vector
            def _(v):
                for dt in range(8):
                    sl = slice((dt % 2) * 1024, (dt % 2) * 1024 + 1024)
                    v.wait_ge(ldw[dt % 2], ldw0[dt % 2] + 16 * (dt // 2 + 1))
                    if dt >= 2:
                        v.wait_ge(mmo, mmo0 + dt - 1)
                    v.tensor_copy(out=wbh[:, sl], in_=wblk[:, sl])
                    v.drain()
                    v.tensor_tensor(out=wbl[:, sl], in0=wblk[:, sl],
                                    in1=wbh[:, sl], op=ALU.subtract)
                    v.drain()
                    k.inc(v.nop(), spw, 1)
                    if dt >= 1:
                        _outproj_evac(k, v, dt - 1, xt, psa, psb, tmp, wbc,
                                      lslot, mmo, mmo0, ldx, ldx0, ldl, ldl0,
                                      evo)
                _outproj_evac(k, v, 7, xt, psa, psb, tmp, wbc, lslot,
                              mmo, mmo0, ldx, ldx0, ldl, ldl0, evo)


def _outproj_evac(k, v, dt, xt, psa, psb, tmp, wbc, lslot,
                  mmo, mmo0, ldx, ldx0, ldl, ldl0, evo):
    ps = psa if dt % 2 == 0 else psb
    v.wait_ge(mmo, mmo0 + dt + 1)
    v.wait_ge(ldx, ldx0 + 16 * (dt + 1))
    v.wait_ge(ldl[dt % 2], ldl0[dt % 2] + 16 * (dt // 2 + 1))
    v.tensor_scalar(out=tmp[:, :], in0=ps[:, :],
                    scalar1=wbc[:, dt:dt + 1], scalar2=None,
                    op0=ALU.add, op1=ALU.bypass)
    v.drain()
    v.tensor_tensor(out=tmp[:, :], in0=tmp[:, :],
                    in1=lslot[dt % 2][:, :], op=ALU.add)
    v.drain()
    v.tensor_tensor(out=xt[dt][:, :], in0=xt[dt][:, :],
                    in1=tmp[:, :], op=ALU.add)
    v.drain()
    k.inc(v.nop(), evo, 1)



def l1_ffn_dram(nc):
    d = {}
    d["l1f_w1T"] = nc.dram_tensor("l1f_w1T", [1024, 4096], F32,
                                  kind="ExternalInput").ap()
    d["l1f_w2T"] = nc.dram_tensor("l1f_w2T", [4096, 1024], F32,
                                  kind="ExternalInput").ap()
    d["l1f_convw"] = nc.dram_tensor("l1f_convw", [128, 96], F32,
                                    kind="ExternalInput").ap()
    d["l1f_convb"] = nc.dram_tensor("l1f_convb", [128, 32], F32,
                                    kind="ExternalInput").ap()
    d["l1f_spill"] = nc.dram_tensor("l1f_spill", [32, 128, 2048], F32,
                                    kind="Internal").ap()
    return d


def l1_ffn_inputs(w1, conv_w, conv_b, w2):
    cw = np.ascontiguousarray(
        conv_w[:, 0, :].reshape(32, 128, 3).transpose(1, 0, 2).reshape(128, 96))
    cb = np.ascontiguousarray(conv_b.reshape(32, 128).T)
    return {
        "l1f_w1T": np.ascontiguousarray(w1.T),
        "l1f_w2T": np.ascontiguousarray(w2.T),
        "l1f_convw": cw,
        "l1f_convb": cb,
    }


def x_fm_layout(xb):
    return np.ascontiguousarray(xb.T.reshape(8, 128, 2048))


def x_fm_unlayout(xfm):
    return np.ascontiguousarray(xfm.reshape(1024, 2048).T)


def l0_ffn_dram(nc, upto="full"):
    d = {}
    d["l0f_win"] = nc.dram_tensor("l0f_win", [1024, 4096], F32,
                                  kind="ExternalInput").ap()
    d["l0f_binc"] = nc.dram_tensor("l0f_binc", [128, 32], F32,
                                   kind="ExternalInput").ap()
    d["l0f_gmax1"] = nc.dram_tensor("l0f_gmax1", [1, 8], F32,
                                    kind="ExternalInput").ap()
    d["l0f_spill"] = nc.dram_tensor("l0f_spill", [32, 128, 2048], F32,
                                    kind="Internal").ap()
    if upto == "full":
        d["l0f_wout"] = nc.dram_tensor("l0f_wout", [4096, 1024], F32,
                                       kind="ExternalInput").ap()
        d["l0f_boutc"] = nc.dram_tensor("l0f_boutc", [128, 8], F32,
                                        kind="ExternalInput").ap()
        d["l0f_gmax2"] = nc.dram_tensor("l0f_gmax2", [1, 8], F32,
                                        kind="ExternalInput").ap()
    else:
        d["l0f_lmax"] = nc.dram_tensor("l0f_lmax", [1, 4], F32,
                                       kind="ExternalOutput").ap()
    return d


def l0_ffn_inputs(W_in, b_in, W_out, b_out, upto="full"):
    d = {
        "l0f_win": np.ascontiguousarray(W_in),
        "l0f_binc": np.ascontiguousarray(b_in.reshape(32, 128).T),
    }
    if upto == "full":
        d["l0f_wout"] = np.ascontiguousarray(W_out)
        d["l0f_boutc"] = np.ascontiguousarray(b_out.reshape(8, 128).T)
    return d


def build_test_l0_ffn(upto="full"):
    nc = bass.Bass("TRN2", target_bir_lowering=False)
    x_in = nc.dram_tensor("x_in", [8, 128, 2048], F32, kind="ExternalInput").ap()
    x_out = nc.dram_tensor("x_out", [8, 128, 2048], F32, kind="ExternalOutput").ap()
    dram = l0_ffn_dram(nc, upto)
    k = K(nc)
    k.init_common()
    xt_t = [nc.alloc_sbuf_tensor(f"xt{i}", [128, 2048], F32) for i in range(8)]
    xt = [t.ap() for t in xt_t]
    k.load_x_fm(x_in, xt)
    build_l0_ffn(k, xt, dram, upto=upto)
    k.store_x_fm(x_out, xt)
    mybir.codegen_inst_isa_subclasses(nc)
    return nc


def l1_lact_dram(nc):
    d = {}
    d["l1a_lnw"] = nc.dram_tensor("l1a_lnw", [128, 8], F32, kind="ExternalInput").ap()
    d["l1a_lnb"] = nc.dram_tensor("l1a_lnb", [128, 8], F32, kind="ExternalInput").ap()
    d["l1a_lactw"] = nc.dram_tensor("l1a_lactw", [4, 4, 256, 256], F32,
                                    kind="ExternalInput").ap()
    d["l1a_xnh"] = nc.dram_tensor("l1a_xnh", [8, 128, 2048], BF16,
                                  kind="Internal").ap()
    d["l1a_xnl"] = nc.dram_tensor("l1a_xnl", [8, 128, 2048], BF16,
                                  kind="Internal").ap()
    return d


def l1_lact_inputs(ln_w, ln_b, w0, w1, w2):
    lact = np.stack([
        np.stack([w0[h].T, w1[h], w1[h].T, w2[h].T], axis=0) for h in range(4)
    ], axis=0)
    return {
        "l1a_lnw": np.ascontiguousarray(ln_w.reshape(8, 128).T),
        "l1a_lnb": np.ascontiguousarray(ln_b.reshape(8, 128).T),
        "l1a_lactw": np.ascontiguousarray(lact),
    }


def build_test_l1_lact():
    nc = bass.Bass("TRN2", target_bir_lowering=False)
    x_in = nc.dram_tensor("x_in", [8, 128, 2048], F32, kind="ExternalInput").ap()
    lact_out = nc.dram_tensor("lact_out", [8, 128, 2048], F32,
                              kind="ExternalOutput").ap()
    xn_out = nc.dram_tensor("xn_out", [8, 128, 2048], F32, kind="ExternalOutput").ap()
    dram = l1_lact_dram(nc)
    dram["l1a_lactsp"] = lact_out
    k = K(nc)
    k.init_common()
    with contextlib.ExitStack() as s1:
        xt = [s1.enter_context(nc.sbuf_tensor(k.name("xt"), [128, 2048], F32)).ap()
              for _ in range(8)]
        k.load_x_fm(x_in, xt)
        build_l1_ln_tohbm(k, xt, dram)
    with contextlib.ExitStack() as s2:
        xnh = s2.enter_context(nc.sbuf_tensor("xnh_all", [128, 16384], BF16)).ap()
        xnl = s2.enter_context(nc.sbuf_tensor("xnl_all", [128, 16384], BF16)).ap()
        load_xn(k, dram, xnh, xnl)
        build_l1_lact(k, dram, xnh, xnl)
        tmpb = s2.enter_context(nc.sbuf_tensor(k.name("tb"), [128, 2048], F32)).ap()
        st = k.gsem()
        ev = k.sem()
        ev0, st0 = k.v(ev), k.v(st)
        with nc.Block() as block:
            @block.vector
            def _(v):
                for i in range(8):
                    sl = slice(i * 2048, (i + 1) * 2048)
                    v.wait_ge(st, st0 + 16 * i)
                    v.tensor_tensor(out=tmpb[:, :], in0=xnh[:, sl],
                                    in1=xnl[:, sl], op=ALU.add)
                    v.drain()
                    k.inc(v.nop(), ev, 1)

            @block.gpsimd
            def _(g):
                for i in range(8):
                    g.wait_ge(ev, ev0 + i + 1)
                    k.inc(g.dma_start(out=xn_out[i, :, :], in_=tmpb[:, :]), st, 16)
                g.wait_ge(st, k.v(st))
    mybir.codegen_inst_isa_subclasses(nc)
    return nc


def l1_attn_dram(nc):
    d = l1_lact_dram(nc)
    d["l1a_qkvw"] = nc.dram_tensor("l1a_qkvw", [16, 3, 1024, 64], F32,
                                   kind="ExternalInput").ap()
    d["l1a_qkvb"] = nc.dram_tensor("l1a_qkvb", [128, 24], F32,
                                   kind="ExternalInput").ap()
    d["l1a_wot"] = nc.dram_tensor("l1a_wot", [1024, 1024], F32,
                                  kind="ExternalInput").ap()
    d["l1a_wob"] = nc.dram_tensor("l1a_wob", [128, 8], F32,
                                  kind="ExternalInput").ap()
    d["l1a_xspill"] = nc.dram_tensor("l1a_xspill", [8, 128, 2048], F32,
                                     kind="Internal").ap()
    d["l1a_lactsp"] = nc.dram_tensor("l1a_lactsp", [8, 128, 2048], F32,
                                     kind="Internal").ap()
    d["l1a_aosp"] = nc.dram_tensor("l1a_aosp", [8, 128, 2048], F32,
                                   kind="Internal").ap()
    return d


def l1_attn_inputs(ln_w, ln_b, in_proj_w, in_proj_b, out_proj_w, out_proj_b,
                   w0, w1, w2):
    d = l1_lact_inputs(ln_w, ln_b, w0, w1, w2)
    qkvw = np.zeros((16, 3, 1024, 64), np.float32)
    qkvb = np.zeros((128, 24), np.float32)
    for h in range(16):
        for t in range(3):
            rows = in_proj_w[t * 1024 + 64 * h:t * 1024 + 64 * (h + 1)]
            qkvw[h, t] = rows.T
    for p in range(8):
        for t in range(3):
            qkvb[0:64, 3 * p + t] = in_proj_b[t * 1024 + 64 * 2 * p:
                                              t * 1024 + 64 * (2 * p + 1)]
            qkvb[64:128, 3 * p + t] = in_proj_b[t * 1024 + 64 * (2 * p + 1):
                                                t * 1024 + 64 * (2 * p + 2)]
    d["l1a_qkvw"] = np.ascontiguousarray(qkvw)
    d["l1a_qkvb"] = qkvb
    d["l1a_wot"] = np.ascontiguousarray(out_proj_w.T)
    d["l1a_wob"] = np.ascontiguousarray(out_proj_b.reshape(8, 128).T)
    return d


def build_test_l1_attn():
    nc = bass.Bass("TRN2", target_bir_lowering=False)
    x_in = nc.dram_tensor("x_in", [8, 128, 2048], F32, kind="ExternalInput").ap()
    x_out = nc.dram_tensor("x_out", [8, 128, 2048], F32, kind="ExternalOutput").ap()
    dram = l1_attn_dram(nc)
    k = K(nc)
    k.init_common()
    with contextlib.ExitStack() as s1:
        xt = [s1.enter_context(nc.sbuf_tensor(k.name("xt"), [128, 2048], F32)).ap()
              for _ in range(8)]
        k.load_x_fm(x_in, xt)
        build_l1_ln_tohbm(k, xt, dram)
        spill_xt(k, xt, dram["l1a_xspill"])
    with contextlib.ExitStack() as s2:
        xnh = s2.enter_context(nc.sbuf_tensor("xnh_all", [128, 16384], BF16)).ap()
        xnl = s2.enter_context(nc.sbuf_tensor("xnl_all", [128, 16384], BF16)).ap()
        load_xn(k, dram, xnh, xnl)
        build_l1_lact(k, dram, xnh, xnl)
        build_l1_attn_pairs(k, dram, xnh, xnl)
    with contextlib.ExitStack() as s3:
        xt = [s3.enter_context(nc.sbuf_tensor(k.name("xt"), [128, 2048], F32)).ap()
              for _ in range(8)]
        build_l1_attn_outproj(k, xt, dram)
        k.store_x_fm(x_out, xt)
    mybir.codegen_inst_isa_subclasses(nc)
    return nc


def build_test_l1_ffn():
    nc = bass.Bass("TRN2", target_bir_lowering=False)
    x_in = nc.dram_tensor("x_in", [8, 128, 2048], F32, kind="ExternalInput").ap()
    x_out = nc.dram_tensor("x_out", [8, 128, 2048], F32, kind="ExternalOutput").ap()
    dram = l1_ffn_dram(nc)
    k = K(nc)
    k.init_common()
    xt_t = [nc.alloc_sbuf_tensor(f"xt{i}", [128, 2048], F32) for i in range(8)]
    xt = [t.ap() for t in xt_t]
    k.load_x_fm(x_in, xt)
    build_l1_ffn(k, xt, dram)
    k.store_x_fm(x_out, xt)
    mybir.codegen_inst_isa_subclasses(nc)
    return nc


def l0_attn_dram(nc):
    d = {}
    for nm in ("wqT", "wkT", "wvT", "wgT", "woT"):
        d["l0a_" + nm] = nc.dram_tensor("l0a_" + nm, [1024, 1024], F32,
                                        kind="ExternalInput").ap()
    d["l0a_lrwT"] = nc.dram_tensor("l0a_lrwT", [1024, 16], F32,
                                   kind="ExternalInput").ap()
    d["l0a_lrb"] = nc.dram_tensor("l0a_lrb", [1, 16], F32,
                                  kind="ExternalInput").ap()
    d["l0a_tidxb"] = nc.dram_tensor("l0a_tidxb", [1, 16], F32,
                                    kind="ExternalInput").ap()
    d["l0a_ltidx"] = nc.dram_tensor("l0a_ltidx", [1, 16], F32,
                                    kind="ExternalInput").ap()
    d["l0a_lnwq"] = nc.dram_tensor("l0a_lnwq", [128, 512], F32,
                                   kind="ExternalInput").ap()
    d["l0a_lnbq"] = nc.dram_tensor("l0a_lnbq", [128, 512], F32,
                                   kind="ExternalInput").ap()
    d["l0a_trilT"] = nc.dram_tensor("l0a_trilT", [128, 16], F32,
                                    kind="ExternalInput").ap()
    d["l0a_trilQ"] = nc.dram_tensor("l0a_trilQ", [128, 16], F32,
                                    kind="ExternalInput").ap()
    d["l0a_w1a"] = nc.dram_tensor("l0a_w1a", [65, 4096], F32,
                                  kind="ExternalInput").ap()
    d["l0a_w2a"] = nc.dram_tensor("l0a_w2a", [128, 2560], F32,
                                  kind="ExternalInput").ap()
    d["l0a_w2t"] = nc.dram_tensor("l0a_w2t", [64, 4096], F32,
                                  kind="ExternalInput").ap()
    d["l0a_b2r"] = nc.dram_tensor("l0a_b2r", [1, 1280], F32,
                                  kind="ExternalInput").ap()
    d["l0a_postw"] = nc.dram_tensor("l0a_postw", [128, 8], F32,
                                    kind="ExternalInput").ap()
    d["l0a_postb"] = nc.dram_tensor("l0a_postb", [128, 8], F32,
                                    kind="ExternalInput").ap()
    d["l0a_xnh"] = nc.dram_tensor("l0a_xnh", [8, 128, 2048], BF16,
                                  kind="Internal").ap()
    d["l0a_xnl"] = nc.dram_tensor("l0a_xnl", [8, 128, 2048], BF16,
                                  kind="Internal").ap()
    for nm in ("xq", "xk", "xvd", "xqw"):
        d["l0a_" + nm] = nc.dram_tensor("l0a_" + nm, [8, 128, 2048], F32,
                                        kind="Internal").ap()
    d["l0a_lr"] = nc.dram_tensor("l0a_lr", [16, 2048], F32,
                                 kind="Internal").ap()
    return d


def l0_attn_inputs(Wq, Wk, Wv, Wo, Wg, lr_w, lr_b, ltidx, norm_w, norm_b,
                   W1, b1, W2, b2, post_w, post_b):
    d = {}
    d["l0a_wqT"] = np.ascontiguousarray(Wq.T)
    d["l0a_wkT"] = np.ascontiguousarray(Wk.T)
    d["l0a_wvT"] = np.ascontiguousarray(Wv.T)
    d["l0a_wgT"] = np.ascontiguousarray(Wg.T)
    d["l0a_woT"] = np.ascontiguousarray(Wo.T)
    d["l0a_lrwT"] = np.ascontiguousarray(lr_w[:, 0, :].T)
    d["l0a_lrb"] = np.asarray(lr_b).reshape(1, 16)
    d["l0a_tidxb"] = (1.0 / np.arange(1, 17, dtype=np.float64)).astype(
        np.float32).reshape(1, 16)
    d["l0a_ltidx"] = np.asarray(ltidx).reshape(1, 16)
    lnwq = np.zeros((128, 512), np.float32)
    lnbq = np.zeros((128, 512), np.float32)
    trilT = np.zeros((128, 16), np.float32)
    trilQ = np.zeros((128, 16), np.float32)
    for j in range(2):
        for t in range(16):
            for dd in range(8):
                lnwq[64 * j + t, 64 * dd:64 * dd + 64] = norm_w[2 * dd + j]
                lnbq[64 * j + t, 64 * dd:64 * dd + 64] = norm_b[2 * dd + j]
            trilT[64 * j + t, t:] = 1.0
            trilQ[64 * j + t, :t + 1] = 1.0
    d["l0a_lnwq"] = lnwq
    d["l0a_lnbq"] = lnbq
    d["l0a_trilT"] = trilT
    d["l0a_trilQ"] = trilQ
    w1a = np.zeros((65, 4096), np.float32)
    w2a = np.zeros((128, 2560), np.float32)
    w2t = np.zeros((64, 4096), np.float32)
    b2r = np.zeros((1, 1280), np.float32)
    for h in range(16):
        w1a[0:64, h * 256:(h + 1) * 256] = W1[h]
        w1a[64, h * 256:(h + 1) * 256] = b1[h, 0]
        for c in range(2):
            w2a[:, (2 * h + c) * 80:(2 * h + c) * 80 + 64] = \
                W2[h, 128 * c:128 * (c + 1), :]
        w2t[:, h * 256:(h + 1) * 256] = W2[h].T
        b2r[0, h * 80:h * 80 + 64] = b2[h, 0]
    d["l0a_w1a"] = w1a
    d["l0a_w2a"] = w2a
    d["l0a_w2t"] = w2t
    d["l0a_b2r"] = b2r
    d["l0a_postw"] = np.ascontiguousarray(post_w.reshape(8, 128).T)
    d["l0a_postb"] = np.ascontiguousarray(post_b.reshape(8, 128).T)
    return d


def build_l0_ln_tohbm(k, xt, dram):
    nc = k.nc
    with contextlib.ExitStack() as es:
        psA_t = es.enter_context(nc.psum_tensor(k.name("n_psA"), [128, 2048], F32))
        psB_t = es.enter_context(nc.psum_tensor(k.name("n_psB"), [128, 2048], F32))
        rowA_t = es.enter_context(nc.sbuf_tensor(k.name("n_rowA"), [1, 2048], F32))
        rowB_t = es.enter_context(nc.sbuf_tensor(k.name("n_rowB"), [1, 2048], F32))
        rowC_t = es.enter_context(nc.sbuf_tensor(k.name("n_rowC"), [1, 2048], F32))
        sq_t = es.enter_context(nc.sbuf_tensor(k.name("n_sq"), [128, 2048], F32))
        snh_t = es.enter_context(nc.sbuf_tensor(k.name("n_snh"), [128, 4096], BF16))
        snl_t = es.enter_context(nc.sbuf_tensor(k.name("n_snl"), [128, 4096], BF16))
        psa, psb = psA_t.ap(), psB_t.ap()
        rowA, rowB, rowC = rowA_t.ap(), rowB_t.ap(), rowC_t.ap()
        sq = sq_t.ap()
        snh, snl = snh_t.ap(), snl_t.ap()

        k.ln_rows(xt, rowA, rowB, rowC, [sq], psa[0:1, :])

        mmb = k.sem()
        evs = k.sem()
        dms = [k.dsem(), k.dsem()]
        mmb0, evs0 = k.v(mmb), k.v(evs)
        dms0 = [k.v(dms[0]), k.v(dms[1])]
        with nc.Block() as block:
            @block.tensor
            def _(t):
                k.bcast_row(t, rowA, psa, 2048)
                k.inc(k.bcast_row(t, rowB, psb, 2048), mmb, 1)

            @block.vector
            def _(v):
                v.wait_ge(mmb, mmb0 + 1)
                for i in range(8):
                    sn = slice((i % 2) * 2048, (i % 2) * 2048 + 2048)
                    if i >= 2:
                        v.wait_ge(dms[i % 2], dms0[i % 2] + 32 * (i // 2))
                    v.tensor_tensor(out=sq[:, :], in0=xt[i][:, :],
                                    in1=psa[:, :], op=ALU.subtract)
                    v.drain()
                    v.tensor_tensor(out=sq[:, :], in0=sq[:, :],
                                    in1=psb[:, :], op=ALU.mult)
                    v.drain()
                    v.tensor_copy(out=snh[:, sn], in_=sq[:, :])
                    v.drain()
                    v.tensor_tensor(out=snl[:, sn], in0=sq[:, :],
                                    in1=snh[:, sn], op=ALU.subtract)
                    v.drain()
                    k.inc(v.nop(), evs, 1)

            @block.sync
            def _(s):
                for i in range(8):
                    s.wait_ge(evs, evs0 + i + 1)
                    k.inc(s.dma_start(out=dram["l0a_xnh"][i, :, :],
                                      in_=snh[:, (i % 2) * 2048:(i % 2) * 2048 + 2048]),
                          dms[i % 2], 16)
                    k.inc(s.dma_start(out=dram["l0a_xnl"][i, :, :],
                                      in_=snl[:, (i % 2) * 2048:(i % 2) * 2048 + 2048]),
                          dms[i % 2], 16)
                s.wait_ge(dms[0], k.v(dms[0]))
                s.wait_ge(dms[1], k.v(dms[1]))


def _l0_proj_one(k, dram, xnh, xnl, wkey, outkey, sub_xk, psa, psb,
                 wblk, wbh, wbl, stg, xkbuf):
    nc = k.nc
    wsrc = dram[wkey]
    dst = dram[outkey]
    ldw = [k.dsem(), k.dsem()]
    spw = k.sem()
    mmo = k.sem()
    evo = k.sem()
    sto = [k.gsem(), k.gsem()]
    ldw0 = [k.v(ldw[0]), k.v(ldw[1])]
    spw0, mmo0, evo0 = k.v(spw), k.v(mmo), k.v(evo)
    sto0 = [k.v(sto[0]), k.v(sto[1])]
    with nc.Block() as block:
        @block.sync
        def _(s):
            for wc in range(8):
                if wc >= 2:
                    s.wait_ge(spw, spw0 + wc - 1)
                k.inc(s.dma_start(
                    out=wblk[:, (wc % 2) * 1024:(wc % 2) * 1024 + 1024]
                        .rearrange("p (c m) -> p c m", c=8),
                    in_=wsrc[:, wc * 128:(wc + 1) * 128]
                        .rearrange("(c p) m -> p c m", p=128)),
                    ldw[wc % 2], 16)

        @block.gpsimd
        def _(g):
            for wc in range(8):
                g.wait_ge(evo, evo0 + wc + 1)
                k.inc(g.dma_start(out=dst[wc, :, :],
                                  in_=stg[:, (wc % 2) * 2048:(wc % 2) * 2048 + 2048]),
                      sto[wc % 2], 16)
            g.wait_ge(sto[0], k.v(sto[0]))
            g.wait_ge(sto[1], k.v(sto[1]))

        @block.tensor
        def _(t):
            for wc in range(8):
                ps = psa if wc % 2 == 0 else psb
                t.wait_ge(spw, spw0 + wc + 1)
                if wc >= 2:
                    t.wait_ge(evo, evo0 + wc - 1)
                ins = None
                for c in range(8):
                    lh = wbh[:, (wc % 2) * 1024 + c * 128:
                             (wc % 2) * 1024 + (c + 1) * 128]
                    ll = wbl[:, (wc % 2) * 1024 + c * 128:
                             (wc % 2) * 1024 + (c + 1) * 128]
                    for q in range(NT):
                        pss = ps[:, q * 512:(q + 1) * 512]
                        xh = xnh[:, c * 2048 + q * 512:c * 2048 + (q + 1) * 512]
                        xl = xnl[:, c * 2048 + q * 512:c * 2048 + (q + 1) * 512]
                        ins = t.matmul(pss, lh, xh, start=(c == 0), stop=False)
                        ins = t.matmul(pss, lh, xl, start=False, stop=False)
                        ins = t.matmul(pss, ll, xh, start=False, stop=(c == 7))
                k.inc(ins, mmo, 1)

        @block.vector
        def _(v):
            for wc in range(8):
                ps = psa if wc % 2 == 0 else psb
                sl = slice((wc % 2) * 2048, (wc % 2) * 2048 + 2048)
                v.wait_ge(ldw[wc % 2], ldw0[wc % 2] + 16 * (wc // 2 + 1))
                if wc >= 2:
                    v.wait_ge(mmo, mmo0 + wc - 1)
                wsl = slice((wc % 2) * 1024, (wc % 2) * 1024 + 1024)
                v.tensor_copy(out=wbh[:, wsl], in_=wblk[:, wsl])
                v.drain()
                v.tensor_tensor(out=wbl[:, wsl], in0=wblk[:, wsl],
                                in1=wbh[:, wsl], op=ALU.subtract)
                v.drain()
                k.inc(v.nop(), spw, 1)
                v.wait_ge(mmo, mmo0 + wc + 1)
                if wc >= 2:
                    v.wait_ge(sto[wc % 2], sto0[wc % 2] + 16 * (wc // 2))
                if sub_xk:
                    v.tensor_tensor(out=stg[:, sl], in0=ps[:, :],
                                    in1=xkbuf[:, wc * 2048:(wc + 1) * 2048],
                                    op=ALU.subtract)
                    v.drain()
                else:
                    v.tensor_copy(out=stg[:, sl], in_=ps[:, :])
                    v.drain()
                    if outkey == "l0a_xk":
                        v.tensor_copy(out=xkbuf[:, wc * 2048:(wc + 1) * 2048],
                                      in_=ps[:, :])
                        v.drain()
                k.inc(v.nop(), evo, 1)


def build_l0_proj(k, dram, xnh, xnl):
    nc = k.nc
    with contextlib.ExitStack() as es:
        psA_t = es.enter_context(nc.psum_tensor(k.name("p_psA"), [128, 2048], F32))
        psB_t = es.enter_context(nc.psum_tensor(k.name("p_psB"), [128, 2048], F32))
        wblk_t = es.enter_context(nc.sbuf_tensor(k.name("p_wblk"), [128, 2048], F32))
        wbh_t = es.enter_context(nc.sbuf_tensor(k.name("p_wbh"), [128, 2048], BF16))
        wbl_t = es.enter_context(nc.sbuf_tensor(k.name("p_wbl"), [128, 2048], BF16))
        stg_t = es.enter_context(nc.sbuf_tensor(k.name("p_stg"), [128, 4096], F32))
        xkb_t = es.enter_context(nc.sbuf_tensor(k.name("p_xkb"), [128, 16384], F32))
        psa, psb = psA_t.ap(), psB_t.ap()
        wblk, wbh, wbl = wblk_t.ap(), wbh_t.ap(), wbl_t.ap()
        stg, xkb = stg_t.ap(), xkb_t.ap()

        _l0_proj_one(k, dram, xnh, xnl, "l0a_wkT", "l0a_xk", False,
                     psa, psb, wblk, wbh, wbl, stg, xkb)
        _l0_proj_one(k, dram, xnh, xnl, "l0a_wvT", "l0a_xvd", True,
                     psa, psb, wblk, wbh, wbl, stg, xkb)
        _l0_proj_one(k, dram, xnh, xnl, "l0a_wqT", "l0a_xq", False,
                     psa, psb, wblk, wbh, wbl, stg, xkb)

        ldl = k.dsem()
        mml = k.sem()
        evl = k.sem()
        trl = k.sem()
        rdl = k.sem()
        stl = k.gsem()
        ldl0, mml0, evl0, trl0 = k.v(ldl), k.v(mml), k.v(evl), k.v(trl)
        rdl0 = k.v(rdl)
        lrt = stg[0:16, 0:2048]
        lrow = stg[0:1, 2048:2064]
        with nc.Block() as block:
            @block.sync
            def _(s):
                k.inc(s.dma_start(
                    out=wblk[:, 0:128].rearrange("p (c m) -> p c m", c=8),
                    in_=dram["l0a_lrwT"].rearrange("(c p) m -> p c m", p=128)),
                    ldl, 16)
                k.inc(s.dma_start(out=lrow, in_=dram["l0a_lrb"][:, :]), ldl, 16)
                s.wait_ge(ldl, k.v(ldl))

            @block.vector
            def _(v):
                v.wait_ge(ldl, k.v(ldl))
                v.tensor_copy(out=wbh[:, 0:128], in_=wblk[:, 0:128])
                v.drain()
                v.tensor_tensor(out=wbl[:, 0:128], in0=wblk[:, 0:128],
                                in1=wbh[:, 0:128], op=ALU.subtract)
                v.drain()
                k.inc(v.nop(), evl, 1)

        evl0 = k.v(evl)
        with nc.Block() as block:
            @block.tensor
            def _(t):
                for tl in range(16):
                    ps = psa if tl % 2 == 0 else psb
                    if tl >= 2:
                        t.wait_ge(trl, trl0 + tl - 1)
                    ins = None
                    for c in range(8):
                        lh = wbh[:, c * 16:(c + 1) * 16]
                        ll = wbl[:, c * 16:(c + 1) * 16]
                        xh = xnh[:, c * 2048 + tl * 128:c * 2048 + (tl + 1) * 128]
                        xl = xnl[:, c * 2048 + tl * 128:c * 2048 + (tl + 1) * 128]
                        ins = t.matmul(ps[:, 0:16], xh, lh, start=(c == 0), stop=False)
                        ins = t.matmul(ps[:, 0:16], xl, lh, start=False, stop=False)
                        ins = t.matmul(ps[:, 0:16], xh, ll, start=False,
                                       stop=False)
                        if c == 7:
                            ins = t.matmul(ps[:, 0:16], k.ones_row[0:1, :],
                                           lrow, start=False, stop=True)
                    k.inc(ins, mml, 1)
                    t.wait_ge(evl, evl0 + tl + 1)
                    if tl >= 2:
                        t.wait_ge(rdl, rdl0 + tl - 1)
                    k.inc(t.matmul(ps[0:16, 512:640], stg[:, 2176 + (tl % 2) * 16:
                                                          2176 + (tl % 2) * 16 + 16],
                                   k.ident[:, :], start=True, stop=True), trl, 1)

            @block.scalar
            def _(s):
                for tl in range(16):
                    ps = psa if tl % 2 == 0 else psb
                    s.wait_ge(mml, mml0 + tl + 1)
                    if tl >= 2:
                        s.wait_ge(trl, trl0 + tl - 1)
                    s.activation(out=stg[:, 2176 + (tl % 2) * 16:
                                         2176 + (tl % 2) * 16 + 16],
                                 in_=ps[:, 0:16], func=AF.Sigmoid)
                    s.drain()
                    k.inc(s.nop(), evl, 1)

            @block.vector
            def _(v):
                for tl in range(16):
                    ps = psa if tl % 2 == 0 else psb
                    v.wait_ge(trl, trl0 + tl + 1)
                    v.tensor_copy(out=lrt[:, tl * 128:(tl + 1) * 128],
                                  in_=ps[0:16, 512:640])
                    v.drain()
                    k.inc(v.nop(), rdl, 1)
                k.inc(v.nop(), evl, 1)

            @block.gpsimd
            def _(g):
                g.wait_ge(evl, k.v(evl))
                k.inc(g.dma_start(out=dram["l0a_lr"][:, :], in_=lrt), stl, 16)
                g.wait_ge(stl, k.v(stl))


def MM(t, *a, **kw):
    kw.setdefault("skip_group_check", True)
    return t.matmul(*a, **kw)


def _l0_chunk(k, dram, st, n, r, psa, psb, bufs):
    nc = k.nc
    (xkq, xvds, lrsg, xqws, w1a, w2a, w2t, b2r, x2f, x2b, gbuf, gz1f,
     gsc1, gsc2, gsc3, gz2q, gz1tmq, x2tmq, x1tmq, gfm, etamq, a1cT, a2cq,
     a2cT, lnzq, colp, lrq, lastq4, trilT, trilQ, lnwq, lnbq, negtr, ntqc,
     t15c) = bufs

    def R(h):
        return 64 * (h % 2)

    def pad(c, h):
        return (c * 8 + h // 2) * 128 + 64 * (h % 2)

    sT1 = k.sem(); sT2 = k.sem(); sT3 = k.sem(); sT4 = k.sem()
    sT5 = k.sem(); sT6 = k.sem(); sT7 = k.sem(); sT8 = k.sem()
    sA1 = k.sem(); sA2 = k.sem(); sA3 = k.sem()
    sV1 = k.sem(); sV2 = k.sem(); sV3 = k.sem(); sV4 = k.sem()
    sV5 = k.sem(); sV6 = k.sem(); sV7 = k.sem(); sV8 = k.sem()
    sV9 = k.sem(); sV10 = k.sem(); sV11 = k.sem(); sV12 = k.sem()
    sV13 = k.sem(); sV14 = k.sem(); sV15 = k.sem(); sV16 = k.sem()
    T10, T20, T30, T40 = k.v(sT1), k.v(sT2), k.v(sT3), k.v(sT4)
    T50, T60, T70, T80 = k.v(sT5), k.v(sT6), k.v(sT7), k.v(sT8)
    A10, A20, A30 = k.v(sA1), k.v(sA2), k.v(sA3)
    V10, V20, V30, V40 = k.v(sV1), k.v(sV2), k.v(sV3), k.v(sV4)
    V50, V60, V70, V80 = k.v(sV5), k.v(sV6), k.v(sV7), k.v(sV8)
    V90, V100, V110, V120 = k.v(sV9), k.v(sV10), k.v(sV11), k.v(sV12)
    V130, V140, V150, V160 = k.v(sV13), k.v(sV14), k.v(sV15), k.v(sV16)

    XVDQ = psa[:, 0:512]
    AT1 = psa[:, 512:640]
    ETM = psa[:, 640:768]
    LRC = psa[:, 768:776]
    Z2Q = psa[:, 1024:1536]
    GFM = psa[0:64, 0:256]
    W2U = psa[:, 1024:1152]
    DB2 = psa[0:1, 1152:1216]
    W1U = psa[0:65, 1536:1792]
    Z2B = psb[:, 0:1024]

    with nc.Block() as block:
        @block.tensor
        def _(t):
            for h in range(16):
                ins = MM(t, 
                    psa[R(h):R(h) + 64, (h // 2) * 64:(h // 2) * 64 + 64],
                    xvds[0:64, h * 128 + r * 16:h * 128 + r * 16 + 64],
                    k.ident[0:64, 0:64], start=(h in (0, 1)),
                    stop=(h in (14, 15)))
            k.inc(ins, sT1, 1)
            for h in range(16):
                for c in range(2):
                    ins = MM(t, 
                        psb[:, pad(c, h):pad(c, h) + 32],
                        w1a[0:65, h * 256 + c * 128:h * 256 + (c + 1) * 128],
                        xkq[0:65, h * 256 + r * 32:h * 256 + r * 32 + 32],
                        start=(h == 0 or h == 8), stop=False)
            k.inc(ins, sT2, 1)
            for h in range(16):
                xkw = xkq[0:64, h * 256 + r * 32:h * 256 + r * 32 + 64]
                xqsl = xkq[0:64, h * 256 + r * 32 + 16:h * 256 + r * 32 + 32]
                ins = MM(t, 
                    AT1[R(h):R(h) + 64, (h // 2) * 16:(h // 2) * 16 + 16],
                    xkw, xqsl, start=(h in (0, 1)), stop=False)
                lrsl = lrsg[0:1, h * 128 + r * 16:h * 128 + r * 16 + 16]
                lrw = lrsg[0:1, h * 128 + r * 16:h * 128 + r * 16 + 64]
                ins = MM(t, 
                    ETM[R(h):R(h) + 64, (h // 2) * 16:(h // 2) * 16 + 16],
                    k.ones_row[0:1, 0:64], lrsl, start=False, stop=False)
                ins = MM(t, 
                    LRC[R(h):R(h) + 64, h // 2:h // 2 + 1],
                    lrw, k.cpool[0:1, 3:4], start=False, stop=(h in (14, 15)))
            k.inc(ins, sT3, 1)
            t.wait_ge(sA1, A10 + 1)
            for h in range(16):
                d = h // 2
                ins = MM(t, 
                    Z2Q[R(h):R(h) + 64, d * 64:d * 64 + 64],
                    k.ones_row[0:1, 0:64],
                    b2r[0:1, h * 80:h * 80 + 64],
                    start=(h in (0, 1)), stop=False)
                for c in range(2):
                    ins = MM(t, 
                        Z2Q[R(h):R(h) + 64, d * 64:d * 64 + 64],
                        x2f[:, pad(c, h):pad(c, h) + 64],
                        w2a[:, (2 * h + c) * 80:(2 * h + c) * 80 + 64],
                        start=False, stop=(h in (14, 15) and c == 1))
            k.inc(ins, sT4, 1)
            t.wait_ge(sV2, V20 + 1)
            for h in range(16):
                ins = MM(t, 
                    GFM[0:64, h * 16:h * 16 + 16],
                    gz2q[R(h):R(h) + 16, (h // 2) * 64:(h // 2) * 64 + 64],
                    k.ident[R(h):R(h) + 16, R(h):R(h) + 16],
                    start=(h == 0), stop=(h == 15))
            k.inc(ins, sT5, 1)
            t.wait_ge(sV3, V30 + 1)
            for h in range(16):
                for c in range(2):
                    ins = MM(t, 
                        psb[:, pad(c, h) + 32:pad(c, h) + 48],
                        w2t[0:64, h * 256 + c * 128:h * 256 + (c + 1) * 128],
                        gfm[0:64, h * 16:h * 16 + 16],
                        start=False, stop=True)
            k.inc(ins, sT6, 1)
            t.wait_ge(sV4, V40 + 1)
            s11 = [psa[:, 256:384], psa[:, 512:640],
                   psa[:, 1024:1152], psa[:, 1536:1664]]
            for i in range(16):
                if i >= 4:
                    t.wait_ge(sV6, V60 + i - 3)
                ins = MM(t, 
                    s11[i % 4],
                    gz1f[:, i * 128:(i + 1) * 128],
                    k.ident[:, :], start=True, stop=True)
                k.inc(ins, sT7, 1)
            t.wait_ge(sV5, V50 + 1)
            t.wait_ge(sV6, V60 + 16)
            for h in range(16):
                d = h // 2
                for c in range(2):
                    ins = MM(t, 
                        psb[:, pad(c, h) + 16:pad(c, h) + 32],
                        gz1tmq[R(h):R(h) + 16,
                               (c * 8 + d) * 128:(c * 8 + d) * 128 + 128],
                        a1cT[R(h):R(h) + 16, d * 16:d * 16 + 16],
                        start=False, stop=(h in (7, 15)))
            k.inc(ins, sT8, 1)
            t.wait_ge(sA2, A20 + 1)
            t.wait_ge(sV7, V70 + 1)
            for h in range(16):
                d = h // 2
                ins = MM(t, 
                    Z2B[R(h):R(h) + 64, d * 128:d * 128 + 80],
                    k.ones_row[0:1, 0:64],
                    b2r[0:1, h * 80:h * 80 + 80],
                    start=(h in (0, 1, 8, 9)), stop=False)
                for c in range(2):
                    ins = MM(t, 
                        Z2B[R(h):R(h) + 64, d * 128:d * 128 + 80],
                        x2b[:, pad(c, h):pad(c, h) + 64],
                        w2a[:, (2 * h + c) * 80:(2 * h + c) * 80 + 80],
                        start=False, stop=False)
            k.inc(ins, sT1, 1)
            t.wait_ge(sV8, V80 + 1)
            for h in range(16):
                d = h // 2
                ins = MM(t, 
                    psa[R(h):R(h) + 16, 512 + d * 16:512 + d * 16 + 16],
                    a2cq[R(h):R(h) + 16, d * 16:d * 16 + 16],
                    k.ident[R(h):R(h) + 16, R(h):R(h) + 16],
                    start=(h in (0, 1)), stop=(h in (14, 15)))
            k.inc(ins, sT2, 1)
            t.wait_ge(sV9, V90 + 1)
            for h in range(16):
                d = h // 2
                ins = MM(t, 
                    Z2B[R(h):R(h) + 16, d * 128:d * 128 + 64],
                    a2cT[R(h):R(h) + 16, d * 16:d * 16 + 16],
                    gz2q[R(h):R(h) + 16, d * 64:d * 64 + 64],
                    start=False, stop=(h in (6, 7, 14, 15)))
            k.inc(ins, sT3, 1)
            t.wait_ge(sV10, V100 + 1)
            for d in range(8):
                ins = MM(t, 
                    psa[0:64, d * 128:d * 128 + 128],
                    lnzq[:, d * 64:d * 64 + 64],
                    k.ident[:, :], start=(d == 0 or d == 4),
                    stop=(d == 3 or d == 7))
            k.inc(ins, sT4, 1)
            t.wait_ge(sV16, V160 + 1)
            for h in range(16):
                ins = MM(t, 
                    psa[R(h):R(h) + 64, (h // 2) * 64:(h // 2) * 64 + 64],
                    xkq[0:64, h * 256 + r * 32:h * 256 + r * 32 + 64],
                    k.ident[0:64, 0:64], start=(h in (0, 1)),
                    stop=(h in (14, 15)))
            k.inc(ins, sT5, 1)
            t.wait_ge(sV11, V110 + 1)
            for h in range(16):
                d = h // 2
                if h >= 2:
                    t.wait_ge(sV12, V120 + h - 1)
                w1slot = (psa[0:65, 1536:1792] if h % 2 == 0
                          else psa[0:65, 0:256])
                for c in range(2):
                    ins = MM(t, 
                        w1slot[0:65, c * 128:c * 128 + 128],
                        gsc3[R(h):R(h) + 16, d * 65:d * 65 + 65],
                        gz1tmq[R(h):R(h) + 16,
                               (c * 8 + d) * 128:(c * 8 + d) * 128 + 128],
                        start=(c == 0), stop=(c == 1))
                k.inc(ins, sT6, 1)
            t.wait_ge(sV12, V120 + 16)
            s19 = [psa[:, 0:128], psa[:, 512:640],
                   psa[:, 1024:1152], psa[:, 1536:1664]]
            for i in range(16):
                if i >= 4:
                    t.wait_ge(sV13, V130 + i - 3)
                ins = MM(t, 
                    s19[i % 4],
                    x2f[:, i * 128:(i + 1) * 128],
                    k.ident[:, :], start=True, stop=True)
                k.inc(ins, sT7, 1)
            t.wait_ge(sV13, V130 + 17)
            for h in range(16):
                d = h // 2
                if h >= 2:
                    t.wait_ge(sV14, V140 + h - 1)
                w2slot = psa[:, 1024:1152] if h % 2 == 0 else psa[:, 512:640]
                dbslot = (psa[0:1, 1152:1216] if h % 2 == 0
                          else psa[0:1, 640:704])
                ins = MM(t, 
                    w2slot[:, 0:64],
                    x2tmq[R(h):R(h) + 16, d * 260:d * 260 + 128],
                    gz2q[R(h):R(h) + 16, d * 64:d * 64 + 64],
                    start=True, stop=False)
                ins = MM(t, 
                    dbslot[0:1, 0:64],
                    x2tmq[R(h):R(h) + 16, d * 260 + 256:d * 260 + 257],
                    gz2q[R(h):R(h) + 16, d * 64:d * 64 + 64],
                    start=False, stop=False)
                ins = MM(t, 
                    w2slot[:, 64:128],
                    x2tmq[R(h):R(h) + 16, d * 260 + 128:d * 260 + 256],
                    gz2q[R(h):R(h) + 16, d * 64:d * 64 + 64],
                    start=False, stop=True)
                k.inc(ins, sT8, 1)
            t.wait_ge(sV12, V120 + 16)
            for h in range(16):
                t.wait_ge(sV14, V140 + h + 1)
                if h >= 2:
                    t.wait_ge(sV15, V150 + h - 1)
                base = 1536 if h % 2 == 0 else 0
                for c in range(2):
                    ins = MM(t, 
                        psa[0:64, base + c * 128:base + c * 128 + 128],
                        w2a[:, (2 * h + c) * 80:(2 * h + c) * 80 + 64],
                        k.ident[:, :], start=(c == 0), stop=(c == 1))
                k.inc(ins, sT1, 1)

        @block.scalar
        def _(s):
            z1view = psb[:, :].rearrange("p (b z) -> p b z", z=64)
            x2fv = x2f.rearrange("p (b z) -> p b z", z=64)
            x2bv = x2b.rearrange("p (b z) -> p b z", z=64)
            s.wait_ge(sT2, T20 + 1)
            s.activation(out=x2fv[:, :, 0:16], in_=z1view[:, :, 0:16],
                         func=AF.Gelu_apprx_tanh)
            s.drain()
            k.inc(s.nop(), sA1, 1)
            s.wait_ge(sV1, V10 + 1)
            s.activation(out=gsc3.rearrange("p (b z) -> p b z", z=64)[:, :, 0:16],
                         in_=gsc2.rearrange("p (b z) -> p b z", z=64)[:, :, 0:16],
                         func=AF.Tanh)
            s.drain()
            k.inc(s.nop(), sA3, 1)
            s.wait_ge(sV1, V10 + 2)
            s.activation(out=colp[:, 16:24], in_=colp[:, 0:8], func=AF.Sqrt,
                         bias=k.cpool[:, 1:2])
            s.drain()
            k.inc(s.nop(), sA3, 1)
            s.wait_ge(sT8, T80 + 1)
            s.activation(out=x2bv[:, :, 0:16], in_=z1view[:, :, 16:32],
                         func=AF.Gelu_apprx_tanh)
            s.drain()
            k.inc(s.nop(), sA2, 1)
            s.wait_ge(sV1, V10 + 3)
            s.activation(out=colp[:, 24:32], in_=colp[:, 8:16], func=AF.Sqrt,
                         bias=k.cpool[:, 1:2])
            s.drain()
            k.inc(s.nop(), sA3, 1)

        @block.vector
        def _(v):
            zv = psb[:, :].rearrange("p (b z) -> p b z", z=64)[:, :, 0:16]
            g1v = gsc1.rearrange("p (b z) -> p b z", z=64)[:, :, 0:16]
            g2v = gsc2.rearrange("p (b z) -> p b z", z=64)[:, :, 0:16]
            g3v = gsc3.rearrange("p (b z) -> p b z", z=64)[:, :, 0:16]
            gbv = gbuf.rearrange("p (b z) -> p b z", z=64)[:, :, 0:16]
            v.wait_ge(sT3, T30 + 1)
            v.tensor_copy(out=lrq[:, :], in_=LRC)
            v.drain()
            v.tensor_copy(out=etamq[0:16, :], in_=ETM[0:16, :])
            v.tensor_copy(out=etamq[64:80, :], in_=ETM[64:80, :])
            v.drain()
            v.tensor_scalar(out=lastq4[:, :], in0=lrq[:, :], scalar1=t15c,
                            scalar2=None, op0=ALU.mult, op1=ALU.bypass)
            v.drain()
            A13 = a1cT.rearrange("p (d z) -> p d z", z=16)
            AT13 = AT1.rearrange("p (d z) -> p d z", z=16)
            v.tensor_scalar(out=A13, in0=AT13, scalar1=1.0, scalar2=None,
                            op0=ALU.add, op1=ALU.bypass)
            v.drain()
            for d in range(8):
                v.tensor_tensor(out=a1cT[:, d * 16:d * 16 + 16],
                                in0=a1cT[:, d * 16:d * 16 + 16],
                                in1=trilT[:, :], op=ALU.mult)
            v.drain()
            for d in range(8):
                v.tensor_scalar(out=a1cT[:, d * 16:d * 16 + 16],
                                in0=a1cT[:, d * 16:d * 16 + 16],
                                scalar1=lrq[:, d:d + 1],
                                scalar2=None, op0=ALU.mult, op1=ALU.bypass)
            v.drain()
            for d in range(8):
                v.tensor_tensor(out=a1cT[:, d * 16:d * 16 + 16],
                                in0=a1cT[:, d * 16:d * 16 + 16],
                                in1=negtr[:, :], op=ALU.mult)
            v.drain()
            k.inc(v.nop(), sV5, 1)
            v.wait_ge(sT2, T20 + 1)
            v.tensor_copy(out=gbv, in_=zv)
            v.drain()
            v.tensor_tensor(out=g1v, in0=gbv, in1=gbv, op=ALU.mult)
            v.drain()
            v.tensor_scalar(out=g2v, in0=g1v, scalar1=0.0356774081,
                            scalar2=0.79788456, op0=ALU.mult, op1=ALU.add)
            v.drain()
            v.tensor_tensor(out=g2v, in0=g2v, in1=gbv, op=ALU.mult)
            v.drain()
            k.inc(v.nop(), sV1, 1)
            v.wait_ge(sA3, A30 + 1)
            v.tensor_scalar(out=g1v, in0=g1v, scalar1=0.1070322243,
                            scalar2=0.79788456, op0=ALU.mult, op1=ALU.add)
            v.drain()
            v.tensor_tensor(out=g1v, in0=g1v, in1=gbv, op=ALU.mult)
            v.drain()
            v.tensor_tensor(out=g2v, in0=g3v, in1=g3v, op=ALU.mult)
            v.drain()
            v.tensor_scalar(out=g2v, in0=g2v, scalar1=-0.5, scalar2=0.5,
                            op0=ALU.mult, op1=ALU.add)
            v.drain()
            v.tensor_tensor(out=g1v, in0=g1v, in1=g2v, op=ALU.mult)
            v.drain()
            v.tensor_scalar(out=g2v, in0=g3v, scalar1=0.5, scalar2=0.5,
                            op0=ALU.mult, op1=ALU.add)
            v.drain()
            v.tensor_tensor(out=gbv, in0=g1v, in1=g2v, op=ALU.add)
            v.drain()
            v.wait_ge(sT4, T40 + 1)
            v.wait_ge(sT1, T10 + 1)
            Z3 = Z2Q.rearrange("p (d z) -> p d z", z=64)
            C13 = gsc1[:, 0:512].rearrange("p (d z) -> p d z", z=64)
            C23 = gsc2[:, 0:512].rearrange("p (d z) -> p d z", z=64)
            C33 = gsc3[:, 512:1024].rearrange("p (d z) -> p d z", z=64)
            v.tensor_reduce(out=colp[:, 0:8], in_=Z3, axis=AX.X, op=ALU.add)
            v.drain()
            v.tensor_scalar(out=colp[:, 0:8], in0=colp[:, 0:8],
                            scalar1=1.0 / 64, scalar2=None,
                            op0=ALU.mult, op1=ALU.bypass)
            v.drain()
            for d in range(8):
                v.tensor_scalar(out=gsc1[:, d * 64:d * 64 + 64],
                                in0=Z2Q[:, d * 64:d * 64 + 64],
                                scalar1=colp[:, d:d + 1],
                                scalar2=None, op0=ALU.subtract, op1=ALU.bypass)
            v.drain()
            v.tensor_tensor(out=C23, in0=C13, in1=C13, op=ALU.mult)
            v.drain()
            v.tensor_reduce(out=colp[:, 0:8], in_=C23, axis=AX.X, op=ALU.add)
            v.drain()
            v.tensor_scalar(out=colp[:, 0:8], in0=colp[:, 0:8],
                            scalar1=1.0 / 64, scalar2=None,
                            op0=ALU.mult, op1=ALU.bypass)
            v.drain()
            k.inc(v.nop(), sV1, 1)
            v.wait_ge(sA3, A30 + 2)
            v.reciprocal(out=colp[:, 16:24], in_=colp[:, 16:24])
            v.drain()
            for d in range(8):
                v.tensor_scalar(out=gsc1[:, d * 64:d * 64 + 64],
                                in0=gsc1[:, d * 64:d * 64 + 64],
                                scalar1=colp[:, 16 + d:17 + d], scalar2=None,
                                op0=ALU.mult, op1=ALU.bypass)
            v.drain()
            LW3 = lnwq.rearrange("p (d z) -> p d z", z=64)
            LB3 = lnbq.rearrange("p (d z) -> p d z", z=64)
            XV3 = XVDQ.rearrange("p (d z) -> p d z", z=64)
            v.tensor_tensor(out=C23, in0=C13, in1=LW3, op=ALU.mult)
            v.drain()
            v.tensor_tensor(out=C23, in0=C23, in1=LB3, op=ALU.add)
            v.drain()
            v.tensor_tensor(out=C23, in0=C23, in1=XV3, op=ALU.subtract)
            v.drain()
            v.tensor_tensor(out=C23, in0=C23, in1=LW3, op=ALU.mult)
            v.drain()
            v.tensor_reduce(out=colp[:, 0:8], in_=C23, axis=AX.X, op=ALU.add)
            v.drain()
            v.tensor_tensor(out=C33, in0=C23, in1=C13, op=ALU.mult)
            v.drain()
            v.tensor_reduce(out=colp[:, 8:16], in_=C33, axis=AX.X, op=ALU.add)
            v.drain()
            for d in range(8):
                v.tensor_scalar(out=gsc1[:, d * 64:d * 64 + 64],
                                in0=gsc1[:, d * 64:d * 64 + 64],
                                scalar1=colp[:, 8 + d:9 + d],
                                scalar2=1.0 / 64, op0=ALU.mult, op1=ALU.mult)
            v.drain()
            v.tensor_scalar(out=colp[:, 0:8], in0=colp[:, 0:8],
                            scalar1=1.0 / 64, scalar2=None,
                            op0=ALU.mult, op1=ALU.bypass)
            v.drain()
            for d in range(8):
                v.tensor_scalar(out=gsc2[:, d * 64:d * 64 + 64],
                                in0=gsc2[:, d * 64:d * 64 + 64],
                                scalar1=colp[:, d:d + 1],
                                scalar2=None, op0=ALU.subtract, op1=ALU.bypass)
            v.drain()
            v.tensor_tensor(out=C23, in0=C23, in1=C13, op=ALU.subtract)
            v.drain()
            for d in range(8):
                v.tensor_scalar(out=gz2q[:, d * 64:d * 64 + 64],
                                in0=gsc2[:, d * 64:d * 64 + 64],
                                scalar1=colp[:, 16 + d:17 + d], scalar2=None,
                                op0=ALU.mult, op1=ALU.bypass)
            v.drain()
            k.inc(v.nop(), sV2, 1)
            v.wait_ge(sT5, T50 + 1)
            v.tensor_copy(out=gfm[0:64, :], in_=GFM)
            v.drain()
            k.inc(v.nop(), sV3, 1)
            v.wait_ge(sT6, T60 + 1)
            v.tensor_tensor(
                out=gz1f.rearrange("p (b z) -> p b z", z=64)[:, :, 0:16],
                in0=psb[:, :].rearrange("p (b z) -> p b z", z=64)[:, :, 32:48],
                in1=gbv, op=ALU.mult)
            v.drain()
            k.inc(v.nop(), sV4, 1)
            s11 = [psa[:, 256:384], psa[:, 512:640],
                   psa[:, 1024:1152], psa[:, 1536:1664]]
            for i in range(16):
                v.wait_ge(sT7, T70 + i + 1)
                v.tensor_copy(out=gz1tmq[:, i * 128:(i + 1) * 128],
                              in_=s11[i % 4])
                v.drain()
                k.inc(v.nop(), sV6, 1)
            v.wait_ge(sA1, A10 + 1)
            for c in range(2):
                v.tensor_copy(
                    out=w2a.rearrange("p (d q c2 z) -> p d q c2 z",
                                      d=8, q=2, c2=2)[:, :, :, c, 64:80],
                    in_=x2f[:, c * 1024:(c + 1) * 1024]
                        .rearrange("p (d q z) -> p d q z", q=2, z=64)
                        [:, :, :, 0:16])
                v.drain()
            k.inc(v.nop(), sV7, 1)
            v.wait_ge(sT1, T10 + 2)
            A23 = a2cq.rearrange("p (d z) -> p d z", z=16)
            ZA3 = Z2B.rearrange("p (d z) -> p d z", z=128)[:, :, 64:80]
            ET3 = etamq.rearrange("p (d z) -> p d z", z=16)
            v.tensor_scalar(out=A23, in0=ZA3, scalar1=1.0, scalar2=None,
                            op0=ALU.add, op1=ALU.bypass)
            v.drain()
            for d in range(8):
                v.tensor_tensor(out=a2cq[:, d * 16:d * 16 + 16],
                                in0=a2cq[:, d * 16:d * 16 + 16],
                                in1=trilQ[:, :], op=ALU.mult)
            v.drain()
            v.tensor_tensor(out=A23, in0=A23, in1=ET3, op=ALU.mult)
            v.drain()
            v.tensor_scalar(out=A23, in0=A23, scalar1=ntqc,
                            scalar2=None, op0=ALU.mult, op1=ALU.bypass)
            v.drain()
            k.inc(v.nop(), sV8, 1)
            v.wait_ge(sT2, T20 + 2)
            v.tensor_copy(out=a2cT[0:16, :], in_=psa[0:16, 512:640])
            v.tensor_copy(out=a2cT[64:80, :], in_=psa[64:80, 512:640])
            v.drain()
            k.inc(v.nop(), sV9, 1)
            v.wait_ge(sT3, T30 + 2)
            ZB3 = Z2B.rearrange("p (d z) -> p d z", z=128)[:, :, 0:64]
            D13 = gsc1[:, 0:512].rearrange("p (d z) -> p d z", z=64)
            D23 = gsc2[:, 0:512].rearrange("p (d z) -> p d z", z=64)
            v.tensor_reduce(out=colp[:, 8:16], in_=ZB3, axis=AX.X, op=ALU.add)
            v.drain()
            v.tensor_scalar(out=colp[:, 8:16], in0=colp[:, 8:16],
                            scalar1=1.0 / 64, scalar2=None,
                            op0=ALU.mult, op1=ALU.bypass)
            v.drain()
            for d in range(8):
                v.tensor_scalar(out=gsc1[:, d * 64:d * 64 + 64],
                                in0=Z2B[:, d * 128:d * 128 + 64],
                                scalar1=colp[:, 8 + d:9 + d],
                                scalar2=None, op0=ALU.subtract, op1=ALU.bypass)
            v.drain()
            v.tensor_tensor(out=D23, in0=D13, in1=D13, op=ALU.mult)
            v.drain()
            v.tensor_reduce(out=colp[:, 8:16], in_=D23, axis=AX.X, op=ALU.add)
            v.drain()
            v.tensor_scalar(out=colp[:, 8:16], in0=colp[:, 8:16],
                            scalar1=1.0 / 64, scalar2=None,
                            op0=ALU.mult, op1=ALU.bypass)
            v.drain()
            k.inc(v.nop(), sV1, 1)
            v.wait_ge(sA3, A30 + 3)
            v.reciprocal(out=colp[:, 24:32], in_=colp[:, 24:32])
            v.drain()
            for d in range(8):
                v.tensor_scalar(out=gsc1[:, d * 64:d * 64 + 64],
                                in0=gsc1[:, d * 64:d * 64 + 64],
                                scalar1=colp[:, 24 + d:25 + d], scalar2=None,
                                op0=ALU.mult, op1=ALU.bypass)
            v.drain()
            LWj = lnwq.rearrange("p (d z) -> p d z", z=64)
            LBj = lnbq.rearrange("p (d z) -> p d z", z=64)
            LZj = lnzq.rearrange("p (d z) -> p d z", z=64)
            v.tensor_tensor(out=D13, in0=D13, in1=LWj, op=ALU.mult)
            v.drain()
            v.tensor_tensor(out=LZj, in0=D13, in1=LBj, op=ALU.add)
            v.drain()
            k.inc(v.nop(), sV10, 1)
            v.wait_ge(sT4, T40 + 2)
            for d in range(8):
                v.tensor_tensor(
                    out=xqws[0:64, d * 256:(d + 1) * 256]
                        .rearrange("p (j z) -> p j z", z=128)
                        [:, :, r * 16:r * 16 + 16],
                    in0=psa[0:64, d * 128:(d + 1) * 128]
                        .rearrange("p (j z) -> p j z", z=64)[:, :, 0:16],
                    in1=xkq[0:64, d * 512:(d + 1) * 512]
                        .rearrange("p (j z) -> p j z", z=256)
                        [:, :, r * 32 + 16:r * 32 + 32],
                    op=ALU.add)
                v.drain()
            k.inc(v.nop(), sV16, 1)
            v.wait_ge(sT5, T50 + 2)
            v.tensor_copy(
                out=x1tmq.rearrange("p (d z) -> p d z", z=65)[:, :, 0:64],
                in_=psa[:, 0:512].rearrange("p (d z) -> p d z", z=64))
            v.drain()
            for d in range(8):
                v.tensor_scalar(out=gsc3[:, d * 65:d * 65 + 65],
                                in0=x1tmq[:, d * 65:d * 65 + 65],
                                scalar1=lastq4[:, d:d + 1], scalar2=None,
                                op0=ALU.mult, op1=ALU.bypass)
                v.drain()
            k.inc(v.nop(), sV11, 1)
            for h in range(16):
                v.wait_ge(sT6, T60 + 1 + h + 1)
                w1slot = (psa[0:65, 1536:1792] if h % 2 == 0
                          else psa[0:65, 0:256])
                for c in range(2):
                    v.tensor_tensor(
                        out=w1a[0:65, h * 256 + c * 128:h * 256 + (c + 1) * 128],
                        in0=w1a[0:65, h * 256 + c * 128:h * 256 + (c + 1) * 128],
                        in1=w1slot[0:65, c * 128:c * 128 + 128],
                        op=ALU.subtract)
                v.drain()
                k.inc(v.nop(), sV12, 1)
            s19 = [psa[:, 0:128], psa[:, 512:640],
                   psa[:, 1024:1152], psa[:, 1536:1664]]
            for i in range(16):
                v.wait_ge(sT7, T70 + 16 + i + 1)
                c, d = i // 8, i % 8
                v.tensor_copy(out=x2tmq[:, d * 260 + c * 128:
                                        d * 260 + (c + 1) * 128],
                              in_=s19[i % 4])
                v.drain()
                k.inc(v.nop(), sV13, 1)
            v.tensor_copy(
                out=x2tmq.rearrange("p (d z) -> p d z", z=260)[:, :, 256:257],
                in_=lastq4[:, :].rearrange("p (d z) -> p d z", z=1))
            v.drain()
            for d in range(8):
                v.tensor_scalar(out=x2tmq[:, d * 260:d * 260 + 256],
                                in0=x2tmq[:, d * 260:d * 260 + 256],
                                scalar1=lastq4[:, d:d + 1], scalar2=None,
                                op0=ALU.mult, op1=ALU.bypass)
                v.drain()
            k.inc(v.nop(), sV13, 1)
            for h in range(16):
                v.wait_ge(sT8, T80 + 1 + h + 1)
                w2slot = psa[:, 1024:1152] if h % 2 == 0 else psa[:, 512:640]
                dbslot = (psa[0:1, 1152:1216] if h % 2 == 0
                          else psa[0:1, 640:704])
                for c in range(2):
                    sl = slice((2 * h + c) * 80, (2 * h + c) * 80 + 64)
                    v.tensor_tensor(out=w2a[:, sl], in0=w2a[:, sl],
                                    in1=w2slot[:, c * 64:c * 64 + 64],
                                    op=ALU.subtract)
                v.tensor_tensor(out=b2r[0:1, h * 80:h * 80 + 64],
                                in0=b2r[0:1, h * 80:h * 80 + 64],
                                in1=dbslot[0:1, 0:64], op=ALU.subtract)
                v.drain()
                k.inc(v.nop(), sV14, 1)
            for h in range(16):
                v.wait_ge(sT1, T10 + 2 + h + 1)
                base = 1536 if h % 2 == 0 else 0
                for c in range(2):
                    v.tensor_copy(
                        out=w2t[0:64, h * 256 + c * 128:h * 256 + (c + 1) * 128],
                        in_=psa[0:64, base + c * 128:base + c * 128 + 128])
                v.drain()
                k.inc(v.nop(), sV15, 1)


def build_l0_scan(k, dram, n_segs=16):
    nc = k.nc
    with contextlib.ExitStack() as es:
        def sb(nm, shape, dt=F32):
            return es.enter_context(nc.sbuf_tensor(k.name(nm), shape, dt)).ap()
        psa = es.enter_context(nc.psum_tensor(k.name("s_psA"), [128, 2048], F32)).ap()
        psb = es.enter_context(nc.psum_tensor(k.name("s_psB"), [128, 2048], F32)).ap()
        xkq = sb("s_xkq", [65, 4160])
        xvds = sb("s_xvd", [64, 2112])
        lrsg = sb("s_lrs", [1, 2112])
        xqws = sb("s_xqw", [64, 2048])
        w1a = sb("s_w1a", [65, 4096])
        w2a = sb("s_w2a", [128, 2560])
        w2t = sb("s_w2t", [64, 4096])
        b2r = sb("s_b2r", [1, 1280])
        x2f = sb("s_x2f", [128, 2048])
        x2b = sb("s_x2b", [128, 2048])
        gbuf = sb("s_gbf", [128, 2048])
        gz1f = sb("s_gz1", [128, 2048])
        gsc1 = sb("s_gs1", [128, 2048])
        gsc2 = sb("s_gs2", [128, 2048])
        gsc3 = sb("s_gs3", [128, 2048])
        gz2q = sb("s_gz2", [128, 512])
        gz1tmq = sb("s_g1t", [128, 2048])
        x2tmq = sb("s_x2t", [128, 2080])
        x1tmq = sb("s_x1t", [128, 520])
        gfm = sb("s_gfm", [64, 256])
        etamq = sb("s_etm", [128, 128])
        a1cT = sb("s_a1c", [128, 128])
        a2cq = sb("s_a2q", [128, 128])
        a2cT = sb("s_a2c", [128, 128])
        lnzq = sb("s_lnz", [128, 512])
        colp = sb("s_col", [128, 32])
        lrq = sb("s_lrq", [128, 8])
        lastq4 = sb("s_lq4", [128, 8])
        trilT = sb("s_trT", [128, 16])
        trilQ = sb("s_trQ", [128, 16])
        lnwq = sb("s_lnw", [128, 512])
        lnbq = sb("s_lnb", [128, 512])
        negtr = sb("s_ngr", [128, 16])
        ntqcb = sb("s_ntq", [128, 1])
        t15cb = sb("s_t15", [128, 1])
        trow = sb("s_trw", [1, 128])

        ldi = k.dsem()
        ev0s = k.sem()
        mmi = k.sem()
        ev0v = k.v(ev0s)
        mmi0 = k.v(mmi)
        with nc.Block() as block:
            @block.sync
            def _(s):
                k.inc(s.dma_start(out=w1a[0:65, :], in_=dram["l0a_w1a"][:, :]), ldi, 16)
                k.inc(s.dma_start(out=w2a[:, :], in_=dram["l0a_w2a"][:, :]), ldi, 16)
                k.inc(s.dma_start(out=w2t[0:64, :], in_=dram["l0a_w2t"][:, :]), ldi, 16)
                k.inc(s.dma_start(out=b2r[0:1, :], in_=dram["l0a_b2r"][:, :]), ldi, 16)
                k.inc(s.dma_start(out=trilT[:, :], in_=dram["l0a_trilT"][:, :]), ldi, 16)
                k.inc(s.dma_start(out=trilQ[:, :], in_=dram["l0a_trilQ"][:, :]), ldi, 16)
                k.inc(s.dma_start(out=lnwq[:, :], in_=dram["l0a_lnwq"][:, :]), ldi, 16)
                k.inc(s.dma_start(out=lnbq[:, :], in_=dram["l0a_lnbq"][:, :]), ldi, 16)
                k.inc(s.dma_start(out=trow[0:1, 0:16], in_=dram["l0a_tidxb"][:, :]),
                      ldi, 16)
                k.inc(s.dma_start(out=trow[0:1, 16:32], in_=dram["l0a_ltidx"][:, :]),
                      ldi, 16)
                s.wait_ge(ldi, k.v(ldi))

            @block.gpsimd
            def _(g):
                for buf in (x2f, x2b, gbuf, gz1f, gsc1, gsc2, gsc3, gz2q,
                            gz1tmq, x2tmq, x1tmq, gfm, etamq, a1cT, a2cq,
                            a2cT, lnzq, colp, lrq, lastq4, negtr, ntqcb,
                            t15cb, xqws, lrsg, xvds):
                    g.memset(buf[:, :], 0.0)
                g.memset(xkq[:, :], 0.0)
                g.memset(trow[0:1, 48:128], 0.0)
                g.drain()
                g.memset(xkq[64:65, :], 1.0)
                g.memset(x1tmq[:, :].rearrange("p (u z) -> p u z", z=65)
                         [:, :, 64:65], 1.0)
                g.drain()

        with nc.Block() as block:
            @block.vector
            def _(v):
                v.tensor_tensor(out=trow[0:1, 0:16], in0=trow[0:1, 0:16],
                                in1=trow[0:1, 16:32], op=ALU.add)
                v.drain()
                v.tensor_scalar(out=trow[0:1, 0:16], in0=trow[0:1, 0:16],
                                scalar1=0.0, scalar2=None,
                                op0=ALU.max, op1=ALU.bypass)
                v.drain()
                v.tensor_scalar(out=trow[0:1, 16:32], in0=trow[0:1, 0:16],
                                scalar1=1.0 / 64, scalar2=None,
                                op0=ALU.mult, op1=ALU.bypass)
                v.drain()
                v.tensor_scalar(out=trow[0:1, 32:48], in0=trow[0:1, 16:32],
                                scalar1=-1.0, scalar2=None,
                                op0=ALU.mult, op1=ALU.bypass)
                v.drain()
                k.inc(v.nop(), ev0s, 1)

            @block.tensor
            def _(t):
                t.wait_ge(ev0s, ev0v + 1)
                k.inc(MM(t, psa[:, 0:16], k.ones_row[0:1, :],
                               trow[0:1, 32:48], start=True, stop=False), mmi, 1)
                for j in range(2):
                    k.inc(MM(t, psa[64 * j:64 * j + 64, 16:17],
                                   trow[0:1, 32:96], k.cpool[0:1, 3:4],
                                   start=False, stop=False), mmi, 1)
                k.inc(MM(t, psa[:, 17:18], k.ones_row[0:1, :],
                               trow[0:1, 31:32], start=False, stop=True), mmi, 1)

            @block.vector
            def _(v2):
                v2.wait_ge(mmi, mmi0 + 4)
                v2.tensor_copy(out=negtr[:, :], in_=psa[:, 0:16])
                v2.drain()
                v2.tensor_copy(out=ntqcb[:, :], in_=psa[:, 16:17])
                v2.drain()
                v2.tensor_copy(out=t15cb[:, :], in_=psa[:, 17:18])
                v2.drain()

        bufs = (xkq, xvds, lrsg, xqws, w1a, w2a, w2t, b2r, x2f, x2b, gbuf,
                gz1f, gsc1, gsc2, gsc3, gz2q, gz1tmq, x2tmq, x1tmq, gfm,
                etamq, a1cT, a2cq, a2cT, lnzq, colp, lrq, lastq4, trilT,
                trilQ, lnwq, lnbq, negtr, ntqcb, t15cb)

        for seg in range(n_segs):
            lds = k.dsem()
            with nc.Block() as block:
                @block.sync
                def _(s):
                    for h in range(16):
                        sl = slice(seg * 128, (seg + 1) * 128)
                        k.inc(s.dma_start(
                            out=xkq[0:64, h * 256:(h + 1) * 256]
                                .rearrange("p (r z) -> p r z", z=32)[:, :, 0:16],
                            in_=dram["l0a_xk"][h // 2, (h % 2) * 64:(h % 2) * 64 + 64, sl]
                                .rearrange("p (r t) -> p r t", t=16)), lds, 16)
                        k.inc(s.dma_start(
                            out=xkq[0:64, h * 256:(h + 1) * 256]
                                .rearrange("p (r z) -> p r z", z=32)[:, :, 16:32],
                            in_=dram["l0a_xq"][h // 2, (h % 2) * 64:(h % 2) * 64 + 64, sl]
                                .rearrange("p (r t) -> p r t", t=16)), lds, 16)
                        k.inc(s.dma_start(
                            out=xvds[0:64, h * 128:(h + 1) * 128],
                            in_=dram["l0a_xvd"][h // 2, (h % 2) * 64:(h % 2) * 64 + 64, sl]),
                            lds, 16)
                    k.inc(s.dma_start(
                        out=lrsg[0:1, 0:2048].rearrange("p (h z) -> p h z", z=128),
                        in_=dram["l0a_lr"][:, seg * 128:(seg + 1) * 128]
                            .rearrange("(o h) t -> o h t", o=1)), lds, 16)
                    s.wait_ge(lds, k.v(lds))

            for r in range(8):
                n = seg * 8 + r
                _l0_chunk(k, dram, None, n, r, psa, psb, bufs)

            sts = k.gsem()
            with nc.Block() as block:
                @block.gpsimd
                def _(g):
                    for h in range(16):
                        sl = slice(seg * 128, (seg + 1) * 128)
                        k.inc(g.dma_start(
                            out=dram["l0a_xqw"][h // 2, (h % 2) * 64:(h % 2) * 64 + 64, sl],
                            in_=xqws[0:64, h * 128:(h + 1) * 128]), sts, 16)
                    g.wait_ge(sts, k.v(sts))


def build_l0_post(k, dram, x_in):
    nc = k.nc
    xt_out = []
    with contextlib.ExitStack() as es:
        def sb(nm, shape, dt=F32):
            return es.enter_context(nc.sbuf_tensor(k.name(nm), shape, dt)).ap()
        qt = [sb(f"q_qw{i}", [128, 2048]) for i in range(8)]
        psa = es.enter_context(nc.psum_tensor(k.name("q_psA"), [128, 2048], F32)).ap()
        psb = es.enter_context(nc.psum_tensor(k.name("q_psB"), [128, 2048], F32)).ap()

        ldq = k.dsem()
        with nc.Block() as block:
            @block.sync
            def _(s):
                for i in range(8):
                    k.inc(s.dma_start(out=qt[i][:, :], in_=dram["l0a_xqw"][i, :, :]),
                          ldq, 16)
                s.wait_ge(ldq, k.v(ldq))

        with contextlib.ExitStack() as es2:
            rowA = es2.enter_context(nc.sbuf_tensor(k.name("q_rA"), [1, 2048], F32)).ap()
            rowB = es2.enter_context(nc.sbuf_tensor(k.name("q_rB"), [1, 2048], F32)).ap()
            rowC = es2.enter_context(nc.sbuf_tensor(k.name("q_rC"), [1, 2048], F32)).ap()
            sq = es2.enter_context(nc.sbuf_tensor(k.name("q_sq"), [128, 2048], F32)).ap()
            pwc = es2.enter_context(nc.sbuf_tensor(k.name("q_pw"), [128, 8], F32)).ap()
            pbc = es2.enter_context(nc.sbuf_tensor(k.name("q_pb"), [128, 8], F32)).ap()
            ldc = k.dsem()
            with nc.Block() as block:
                @block.sync
                def _(s):
                    k.inc(s.dma_start(out=pwc[:, :], in_=dram["l0a_postw"][:, :]), ldc, 16)
                    k.inc(s.dma_start(out=pbc[:, :], in_=dram["l0a_postb"][:, :]), ldc, 16)
                    s.wait_ge(ldc, k.v(ldc))
            k.ln_rows(qt, rowA, rowB, rowC, [sq], psa[0:1, :],
                      eps_ap=k.cpool[:, 1:2])
            mmb = k.sem()
            mmb0 = k.v(mmb)
            with nc.Block() as block:
                @block.tensor
                def _(t):
                    k.bcast_row(t, rowA, psa, 2048)
                    k.inc(k.bcast_row(t, rowB, psb, 2048), mmb, 1)

                @block.vector
                def _(v):
                    v.wait_ge(mmb, mmb0 + 1)
                    for i in range(8):
                        v.tensor_tensor(out=qt[i][:, :], in0=qt[i][:, :],
                                        in1=psa[:, :], op=ALU.subtract)
                        v.drain()
                        v.tensor_tensor(out=qt[i][:, :], in0=qt[i][:, :],
                                        in1=psb[:, :], op=ALU.mult)
                        v.drain()
                        v.tensor_scalar(out=qt[i][:, :], in0=qt[i][:, :],
                                        scalar1=pwc[:, i:i + 1],
                                        scalar2=pbc[:, i:i + 1],
                                        op0=ALU.mult, op1=ALU.add)
                        v.drain()

        with contextlib.ExitStack() as es3:
            xnh = es3.enter_context(nc.sbuf_tensor(k.name("q_xnh"), [128, 16384], BF16)).ap()
            xnl = es3.enter_context(nc.sbuf_tensor(k.name("q_xnl"), [128, 16384], BF16)).ap()
            wblk = es3.enter_context(nc.sbuf_tensor(k.name("q_wb"), [128, 2048], F32)).ap()
            wbh = es3.enter_context(nc.sbuf_tensor(k.name("q_wh"), [128, 2048], BF16)).ap()
            wbl = es3.enter_context(nc.sbuf_tensor(k.name("q_wl"), [128, 2048], BF16)).ap()
            gob = es3.enter_context(nc.sbuf_tensor(k.name("q_go"), [128, 4096], F32)).ap()
            ldn = k.dsem()
            with nc.Block() as block:
                @block.sync
                def _(s):
                    for i in range(8):
                        sl = slice(i * 2048, (i + 1) * 2048)
                        k.inc(s.dma_start(out=xnh[:, sl],
                                          in_=dram["l0a_xnh"][i, :, :]), ldn, 16)
                        k.inc(s.dma_start(out=xnl[:, sl],
                                          in_=dram["l0a_xnl"][i, :, :]), ldn, 16)
                    s.wait_ge(ldn, k.v(ldn))

            ldw = [k.dsem(), k.dsem()]
            spw = k.sem()
            mmo = k.sem()
            gel = k.sem()
            evo = k.sem()
            ldw0 = [k.v(ldw[0]), k.v(ldw[1])]
            spw0, mmo0, evo0 = k.v(spw), k.v(mmo), k.v(evo)
            gel0 = k.v(gel)
            with nc.Block() as block:
                @block.sync
                def _(s):
                    for wc in range(8):
                        if wc >= 2:
                            s.wait_ge(spw, spw0 + wc - 1)
                        k.inc(s.dma_start(
                            out=wblk[:, (wc % 2) * 1024:(wc % 2) * 1024 + 1024]
                                .rearrange("p (c m) -> p c m", c=8),
                            in_=dram["l0a_wgT"][:, wc * 128:(wc + 1) * 128]
                                .rearrange("(c p) m -> p c m", p=128)),
                            ldw[wc % 2], 16)

                @block.tensor
                def _(t):
                    for wc in range(8):
                        ps = psa if wc % 2 == 0 else psb
                        t.wait_ge(spw, spw0 + wc + 1)
                        if wc >= 2:
                            t.wait_ge(evo, evo0 + wc - 1)
                        ins = None
                        for c in range(8):
                            lh = wbh[:, (wc % 2) * 1024 + c * 128:
                                     (wc % 2) * 1024 + (c + 1) * 128]
                            ll = wbl[:, (wc % 2) * 1024 + c * 128:
                                     (wc % 2) * 1024 + (c + 1) * 128]
                            for q in range(NT):
                                pss = ps[:, q * 512:(q + 1) * 512]
                                xh = xnh[:, c * 2048 + q * 512:c * 2048 + (q + 1) * 512]
                                xl = xnl[:, c * 2048 + q * 512:c * 2048 + (q + 1) * 512]
                                ins = t.matmul(pss, lh, xh, start=(c == 0), stop=False)
                                ins = t.matmul(pss, lh, xl, start=False, stop=False)
                                ins = t.matmul(pss, ll, xh, start=False, stop=(c == 7))
                        k.inc(ins, mmo, 1)

                @block.scalar
                def _(s):
                    for wc in range(8):
                        ps = psa if wc % 2 == 0 else psb
                        gsl = slice((wc % 2) * 2048, (wc % 2) * 2048 + 2048)
                        s.wait_ge(mmo, mmo0 + wc + 1)
                        if wc >= 2:
                            s.wait_ge(evo, evo0 + wc - 1)
                        s.activation(out=gob[:, gsl], in_=ps[:, :],
                                     func=AF.Gelu_apprx_tanh)
                        s.drain()
                        k.inc(s.nop(), gel, 1)

                @block.vector
                def _(v):
                    def gate_mult(wc):
                        gsl = slice((wc % 2) * 2048, (wc % 2) * 2048 + 2048)
                        v.wait_ge(gel, gel0 + wc + 1)
                        v.tensor_tensor(out=qt[wc][:, :], in0=qt[wc][:, :],
                                        in1=gob[:, gsl], op=ALU.mult)
                        v.drain()
                        k.inc(v.nop(), evo, 1)
                    for wc in range(8):
                        wsl = slice((wc % 2) * 1024, (wc % 2) * 1024 + 1024)
                        v.wait_ge(ldw[wc % 2], ldw0[wc % 2] + 16 * (wc // 2 + 1))
                        v.tensor_copy(out=wbh[:, wsl], in_=wblk[:, wsl])
                        v.drain()
                        v.tensor_tensor(out=wbl[:, wsl], in0=wblk[:, wsl],
                                        in1=wbh[:, wsl], op=ALU.subtract)
                        v.drain()
                        k.inc(v.nop(), spw, 1)
                        if wc >= 1:
                            gate_mult(wc - 1)
                    gate_mult(7)

        xt = [sb(f"q_xt{i}", [128, 2048]) for i in range(8)]
        with contextlib.ExitStack() as es4:
            wof = es4.enter_context(nc.sbuf_tensor(k.name("q_wo"), [128, 2048], F32)).ap()
            ldx = k.dsem()
            ldo = [k.dsem(), k.dsem()]
            mmw = k.sem()
            evw = k.sem()
            ldx0, mmw0, evw0 = k.v(ldx), k.v(mmw), k.v(evw)
            ldo0 = [k.v(ldo[0]), k.v(ldo[1])]
            with nc.Block() as block:
                @block.sync
                def _(s):
                    for dt in range(8):
                        k.inc(s.dma_start(out=xt[dt][:, :], in_=x_in[dt, :, :]),
                              ldx, 16)
                        if dt >= 2:
                            s.wait_ge(mmw, mmw0 + dt - 1)
                        k.inc(s.dma_start(
                            out=wof[:, (dt % 2) * 1024:(dt % 2) * 1024 + 1024]
                                .rearrange("p (c m) -> p c m", c=8),
                            in_=dram["l0a_woT"][:, dt * 128:(dt + 1) * 128]
                                .rearrange("(c p) m -> p c m", p=128)),
                            ldo[dt % 2], 16)

                @block.tensor
                def _(t):
                    for dt in range(8):
                        ps = psa if dt % 2 == 0 else psb
                        t.wait_ge(ldo[dt % 2], ldo0[dt % 2] + 16 * (dt // 2 + 1))
                        if dt >= 2:
                            t.wait_ge(evw, evw0 + dt - 1)
                        ins = None
                        for c in range(8):
                            lw = wof[:, (dt % 2) * 1024 + c * 128:
                                     (dt % 2) * 1024 + (c + 1) * 128]
                            for q in range(NT):
                                ins = t.matmul(ps[:, q * 512:(q + 1) * 512], lw,
                                               qt[c][:, q * 512:(q + 1) * 512],
                                               start=(c == 0), stop=(c == 7))
                        k.inc(ins, mmw, 1)

                @block.vector
                def _(v):
                    for dt in range(8):
                        ps = psa if dt % 2 == 0 else psb
                        v.wait_ge(mmw, mmw0 + dt + 1)
                        v.wait_ge(ldx, ldx0 + 16 * (dt + 1))
                        v.tensor_tensor(out=xt[dt][:, :], in0=xt[dt][:, :],
                                        in1=ps[:, :], op=ALU.add)
                        v.drain()
                        k.inc(v.nop(), evw, 1)
        xt_out = xt
        return es.pop_all(), xt_out, psa, psb


def build_test_l0_attn(n_segs=16, dbg_xqw=False):
    nc = bass.Bass("TRN2", target_bir_lowering=False)
    x_in = nc.dram_tensor("x_in", [8, 128, 2048], F32, kind="ExternalInput").ap()
    x_out = nc.dram_tensor("x_out", [8, 128, 2048], F32, kind="ExternalOutput").ap()
    dram = l0_attn_dram(nc)
    if dbg_xqw:
        dram["l0a_xqw"] = nc.dram_tensor("xqw_out", [8, 128, 2048], F32,
                                         kind="ExternalOutput").ap()
    k = K(nc)
    k.init_common()
    with contextlib.ExitStack() as s1:
        xt = [s1.enter_context(nc.sbuf_tensor(k.name("xt"), [128, 2048], F32)).ap()
              for _ in range(8)]
        k.load_x_fm(x_in, xt)
        build_l0_ln_tohbm(k, xt, dram)
    with contextlib.ExitStack() as s2:
        xnh = s2.enter_context(nc.sbuf_tensor("xnh0", [128, 16384], BF16)).ap()
        xnl = s2.enter_context(nc.sbuf_tensor("xnl0", [128, 16384], BF16)).ap()
        load_xn0(k, dram, xnh, xnl)
        build_l0_proj(k, dram, xnh, xnl)
    build_l0_scan(k, dram, n_segs=n_segs)
    if n_segs == 16:
        hold, xt, _psa, _psb = build_l0_post(k, dram, x_in)
        k.store_x_fm(x_out, xt)
        hold.close()
    mybir.codegen_inst_isa_subclasses(nc)
    return nc


def load_xn0(k, dram, xnh, xnl):
    nc = k.nc
    ldn = k.dsem()
    with nc.Block() as block:
        @block.sync
        def _(s):
            for i in range(8):
                sl = slice(i * 2048, (i + 1) * 2048)
                k.inc(s.dma_start(out=xnh[:, sl], in_=dram["l0a_xnh"][i, :, :]),
                      ldn, 16)
                k.inc(s.dma_start(out=xnl[:, sl], in_=dram["l0a_xnl"][i, :, :]),
                      ldn, 16)
            s.wait_ge(ldn, k.v(ldn))



def _build_launch_A():
    nc = bass.Bass("TRN2", target_bir_lowering=False)
    x_in = nc.dram_tensor("x_in", [8, 128, 2048], F32, kind="ExternalInput").ap()
    x_out = nc.dram_tensor("x_out", [8, 128, 2048], F32, kind="ExternalOutput").ap()
    lm_out = nc.dram_tensor("lmax1", [1, 1], F32, kind="ExternalOutput").ap()
    dram = l0_attn_dram(nc)
    k = K(nc)
    k.init_common()
    with contextlib.ExitStack() as s1:
        xt = [s1.enter_context(nc.sbuf_tensor(k.name("xt"), [128, 2048], F32)).ap()
              for _ in range(8)]
        k.load_x_fm(x_in, xt)
        build_l0_ln_tohbm(k, xt, dram)
    with contextlib.ExitStack() as s2:
        xnh = s2.enter_context(nc.sbuf_tensor("xnh0", [128, 16384], BF16)).ap()
        xnl = s2.enter_context(nc.sbuf_tensor("xnl0", [128, 16384], BF16)).ap()
        load_xn0(k, dram, xnh, xnl)
        build_l0_proj(k, dram, xnh, xnl)
    build_l0_scan(k, dram, n_segs=16)
    hold, xt, psa, psb = build_l0_post(k, dram, x_in)
    k.store_x_fm(x_out, xt)
    with contextlib.ExitStack() as s3:
        def sb(nm, shape, dt=F32):
            return s3.enter_context(nc.sbuf_tensor(k.name(nm), shape, dt)).ap()
        rows4 = [sb(f"m_r{i}", [1, 2048]) for i in range(4)]
        sq = sb("m_sq", [128, 2048])
        col = sb("m_c1", [128, 1])
        col2 = sb("m_c2", [128, 1])
        pam = sb("m_pam", [128, 16])
        lms = sb("m_lm", [1, 1])
        emit_ln_absmax(k, xt, lms, rows4, psb[:, 0:1024], psa[0:1, 0:2048],
                       col, col2, pam, [sq])
        stm = k.gsem()
        with nc.Block() as block:
            @block.gpsimd
            def _(g):
                k.inc(g.dma_start(out=lm_out[0:1, :], in_=lms[0:1, :]), stm, 16)
                g.wait_ge(stm, k.v(stm))
    hold.close()
    mybir.codegen_inst_isa_subclasses(nc)
    return nc


def _build_launch_B():
    nc = bass.Bass("TRN2", target_bir_lowering=False)
    x_in = nc.dram_tensor("x_in", [8, 128, 2048], F32, kind="ExternalInput").ap()
    dram = l0_ffn_dram(nc, "max2")
    k = K(nc)
    k.init_common()
    xt = [nc.alloc_sbuf_tensor(k.name("xt"), [128, 2048], F32).ap()
          for _ in range(8)]
    k.load_x_fm(x_in, xt)
    build_l0_ffn(k, xt, dram, upto="max2")
    mybir.codegen_inst_isa_subclasses(nc)
    return nc


def _build_launch_C():
    nc = bass.Bass("TRN2", target_bir_lowering=False)
    x_in = nc.dram_tensor("x_in", [8, 128, 2048], F32, kind="ExternalInput").ap()
    x_out = nc.dram_tensor("x_out", [8, 128, 2048], F32, kind="ExternalOutput").ap()
    dram = {}
    dram.update(l0_ffn_dram(nc, "full"))
    dram.update(l1_attn_dram(nc))
    dram.update(l1_ffn_dram(nc))
    k = K(nc)
    k.init_common()
    with contextlib.ExitStack() as s1:
        xt = [s1.enter_context(nc.sbuf_tensor(k.name("xt"), [128, 2048], F32)).ap()
              for _ in range(8)]
        k.load_x_fm(x_in, xt)
        build_l0_ffn(k, xt, dram, upto="full")
        build_l1_ln_tohbm(k, xt, dram)
        spill_xt(k, xt, dram["l1a_xspill"])
    with contextlib.ExitStack() as s2:
        xnh = s2.enter_context(nc.sbuf_tensor("xnh_all", [128, 16384], BF16)).ap()
        xnl = s2.enter_context(nc.sbuf_tensor("xnl_all", [128, 16384], BF16)).ap()
        load_xn(k, dram, xnh, xnl)
        build_l1_lact(k, dram, xnh, xnl)
        build_l1_attn_pairs(k, dram, xnh, xnl)
    with contextlib.ExitStack() as s3:
        xt = [s3.enter_context(nc.sbuf_tensor(k.name("xt"), [128, 2048], F32)).ap()
              for _ in range(8)]
        build_l1_attn_outproj(k, xt, dram)
        build_l1_ffn(k, xt, dram)
        k.store_x_fm(x_out, xt)
    mybir.codegen_inst_isa_subclasses(nc)
    return nc


_NC_CACHE = {}


def kernel(**inputs):
    from concourse.bass_utils import run_bass_kernel_spmd
    z = inputs
    x0 = np.asarray(z["L0_attn_hidden_states"])
    w_l0a = l0_attn_inputs(
        z["L0_attn_Wq"], z["L0_attn_Wk"], z["L0_attn_Wv"], z["L0_attn_Wo"],
        z["L0_attn_Wg"], z["L0_attn_lr_w"], z["L0_attn_lr_b"],
        z["L0_attn_learnable_token_idx"], z["L0_attn_norm_w"],
        z["L0_attn_norm_b"], z["L0_attn_W1"], z["L0_attn_b1"],
        z["L0_attn_W2"], z["L0_attn_b2"], z["L0_attn_post_w"],
        z["L0_attn_post_b"])
    w_l0f = l0_ffn_inputs(z["L0_ffn_W_in"], z["L0_ffn_b_in"],
                          z["L0_ffn_W_out"], z["L0_ffn_b_out"])
    w_l1a = l1_attn_inputs(z["L1_attn_ln_w"], z["L1_attn_ln_b"],
                           z["L1_attn_in_proj_w"], z["L1_attn_in_proj_b"],
                           z["L1_attn_out_proj_w"], z["L1_attn_out_proj_b"],
                           z["L1_attn_w0"], z["L1_attn_w1"], z["L1_attn_w2"])
    w_l1f = l1_ffn_inputs(z["L1_ffn_w1"], z["L1_ffn_conv_w"],
                          z["L1_ffn_conv_b"], z["L1_ffn_w2"])

    if "A" not in _NC_CACHE:
        _NC_CACHE["A"] = _build_launch_A()
        _NC_CACHE["B"] = _build_launch_B()
        _NC_CACHE["C"] = _build_launch_C()
    ncA, ncB, ncC = _NC_CACHE["A"], _NC_CACHE["B"], _NC_CACHE["C"]

    cores = list(range(8))
    maps = []
    for b in range(8):
        m = {"x_in": x_fm_layout(x0[b])}
        m.update(w_l0a)
        maps.append(m)
    resA = run_bass_kernel_spmd(ncA, maps, core_ids=cores)
    x1sh = [np.asarray(resA.results[b]["x_out"]) for b in range(8)]
    gm1 = np.concatenate([np.asarray(resA.results[b]["lmax1"]).reshape(1, 1)
                          for b in range(8)], axis=1)

    maps = []
    for b in range(8):
        m = {"x_in": x1sh[b], "l0f_gmax1": gm1}
        m.update({kk: vv for kk, vv in w_l0f.items() if kk in
                  ("l0f_win", "l0f_binc")})
        maps.append(m)
    resB = run_bass_kernel_spmd(ncB, maps, core_ids=cores)
    gm2 = np.concatenate([np.asarray(resB.results[b]["l0f_lmax"])[0:1, 1:2]
                          for b in range(8)], axis=1)

    maps = []
    for b in range(8):
        m = {"x_in": x1sh[b], "l0f_gmax1": gm1, "l0f_gmax2": gm2}
        m.update(w_l0f)
        m.update(w_l1a)
        m.update(w_l1f)
        maps.append(m)
    resC = run_bass_kernel_spmd(ncC, maps, core_ids=cores)
    out = np.stack([x_fm_unlayout(np.asarray(resC.results[b]["x_out"]))
                    for b in range(8)]).astype(np.float32)
    return out


# version 09
# speedup vs baseline: 1.4175x; 1.0198x over previous
import contextlib
import numpy as np
import concourse.bass as bass
import concourse.mybir as mybir
from concourse.bass_utils import run_bass_kernel_spmd

F32 = mybir.dt.float32
BF16 = mybir.dt.bfloat16
F8 = mybir.dt.float8e4
AF = mybir.ActivationFunctionType
ALU = mybir.AluOpType
AX = mybir.AxisListType

MAGIC = 12582912.0
T = 2048
D = 1024
NT = T // 512


class K:
    def __init__(self, nc):
        self.nc = nc
        self._sv = {}
        self._n = 0

    def name(self, p):
        self._n += 1
        return f"{p}_{self._n}"

    SEM_POOL = 40

    def gsem(self):
        if not hasattr(self, "_gi"):
            self._gi = 0
        self._gi += 1
        h = self.nc.alloc_semaphore(name=f"gs{self._gi}")
        self._sv[h] = 0
        return h

    def dsem(self, name=None):
        if not hasattr(self, "_dpool"):
            self._dpool = [self.nc.alloc_semaphore(name=f"dp{i}")
                           for i in range(16)]
            for h in self._dpool:
                self._sv[h] = 0
            self._dpi = 0
        h = self._dpool[self._dpi % 16]
        self._dpi += 1
        return h

    def sem(self, name=None):
        if not hasattr(self, "_pool"):
            self._pool = [self.nc.alloc_semaphore(name=f"sp{i}")
                          for i in range(self.SEM_POOL)]
            for h in self._pool:
                self._sv[h] = 0
            self._pi = 0
        h = self._pool[self._pi % self.SEM_POOL]
        self._pi += 1
        return h

    def inc(self, ins, sem, n=1):
        ins.then_inc(sem, n)
        self._sv[sem] += n
        return ins

    def v(self, sem):
        return self._sv[sem]

    def init_common(self):
        nc = self.nc
        self.ident = nc.alloc_sbuf_tensor("c_ident", [128, 128], F32).ap()
        self.ones_col = nc.alloc_sbuf_tensor("c_ones_col", [128, 1], F32).ap()
        self.ones_row = nc.alloc_sbuf_tensor("c_ones_row", [1, 128], F32).ap()
        self.cpool = nc.alloc_sbuf_tensor("c_cpool", [128, 4], F32).ap()
        with nc.Block() as block:
            @block.gpsimd
            def _(g):
                g.memset(self.cpool[:, 0:1], 1e-5)
                g.memset(self.cpool[:, 1:2], 1e-6)
                g.memset(self.cpool[:, 2:3], 0.5)
                g.memset(self.cpool[:, 3:4], 1.0)
                g.memset(self.ident[:, :], 1.0)
                g.memset(self.ones_col[:, :], 1.0)
                g.memset(self.ones_row[:, :], 1.0)
                g.drain()
                g.affine_select(
                    out=self.ident[:, :], in_=self.ident[:, :],
                    pattern=[[-1, 128]], compare_op=ALU.is_equal,
                    fill=0.0, channel_multiplier=1, base=0)
                g.drain()

    def load_x_fm(self, x_in, xt):
        nc, ld = self.nc, self.dsem()
        with nc.Block() as block:
            @block.sync
            def _(s):
                for i in range(8):
                    self.inc(s.dma_start(out=xt[i][:, :], in_=x_in[i, :, :]), ld, 16)
                s.wait_ge(ld, self.v(ld))

    def store_x_fm(self, x_out, xt):
        nc, st = self.nc, self.gsem()
        with nc.Block() as block:
            @block.gpsimd
            def _(g):
                for i in range(8):
                    self.inc(g.dma_start(out=x_out[i, :, :], in_=xt[i][:, :]), st, 16)
                g.wait_ge(st, self.v(st))

    def ln_rows(self, xt, mu_row, r_row, tmp_row, sq_bufs, ps_row, eps_ap=None,
                n_d=8):
        if eps_ap is None:
            eps_ap = self.cpool[:, 0:1]
        """mu_row = mean over d; r_row = 1/sqrt(var+eps); rows are [1,2048].

        ps_row: [1, >=2048] psum view (4 banks). sq_bufs: 2 scratch tiles.
        """
        nc = self.nc
        mm = self.sem()
        mm_t = self.v(mm) + 1
        with nc.Block() as block:
            @block.tensor
            def _(t):
                ins = None
                for q in range(NT):
                    for i in range(n_d):
                        ins = t.matmul(ps_row[0:1, q * 512:(q + 1) * 512],
                                       self.ones_col[:, :],
                                       xt[i][:, q * 512:(q + 1) * 512],
                                       start=(i == 0), stop=(i == n_d - 1))
                self.inc(ins, mm, 1)

            @block.vector
            def _(v):
                v.wait_ge(mm, mm_t)
                v.tensor_scalar(out=mu_row[:, :], in0=ps_row[0:1, 0:2048],
                                scalar1=1.0 / (128.0 * n_d), scalar2=None,
                                op0=ALU.mult, op1=ALU.bypass)
                v.drain()

        sq = self.sem()
        mm2 = self.sem()
        sq0, mm20 = self.v(sq), self.v(mm2)
        with nc.Block() as block:
            nb = len(sq_bufs)

            @block.scalar
            def _(s):
                for i in range(n_d):
                    if i >= nb:
                        s.wait_ge(mm2, mm20 + i - nb + 1)
                    s.activation(out=sq_bufs[i % nb][:, :], in_=xt[i][:, :],
                                 func=AF.Square)
                    s.drain()
                    self.inc(s.nop(), sq, 1)

            @block.tensor
            def _(t):
                for i in range(n_d):
                    t.wait_ge(sq, sq0 + i + 1)
                    ins = None
                    for q in range(NT):
                        ins = t.matmul(ps_row[0:1, q * 512:(q + 1) * 512],
                                       self.ones_col[:, :],
                                       sq_bufs[i % nb][:, q * 512:(q + 1) * 512],
                                       start=(i == 0), stop=(i == n_d - 1))
                    self.inc(ins, mm2, 1)

            mm2_t = self.v(mm2)

            @block.vector
            def _(v):
                v.wait_ge(mm2, mm2_t)
                v.tensor_scalar(out=tmp_row[:, :], in0=ps_row[0:1, 0:2048],
                                scalar1=1.0 / (128.0 * n_d), scalar2=None,
                                op0=ALU.mult, op1=ALU.bypass)
                v.tensor_tensor(out=r_row[:, :], in0=mu_row[:, :],
                                in1=mu_row[:, :], op=ALU.mult)
                v.drain()
                v.tensor_tensor(out=r_row[:, :], in0=tmp_row[:, :],
                                in1=r_row[:, :], op=ALU.subtract)
                v.drain()

        with nc.Block() as block:
            @block.scalar
            def _(s):
                s.activation(out=r_row[:, :], in_=r_row[:, :], func=AF.Sqrt,
                             bias=eps_ap[0:1, 0:1], scale=1.0)
                s.drain()

        with nc.Block() as block:
            @block.vector
            def _(v):
                v.reciprocal(out=r_row[:, :], in_=r_row[:, :])
                v.drain()

    def bcast_row(self, t_eng, row_ap, ps_tile, n_cols):
        ins = None
        for q in range(0, n_cols, 512):
            w = min(512, n_cols - q)
            ins = t_eng.matmul(ps_tile[:, q:q + w], self.ones_row[:, :],
                               row_ap[0:1, q:q + w], start=True, stop=True)
        return ins

    def part_reduce(self, tiles, out_row, pst2, pcols, red_col, acc_col,
                    pam_buf, mode="absmax"):
        red_op = ALU.min if mode == "min" else ALU.max
        use_abs = (mode == "absmax")
        width = tiles[0].shape[1]
        nsl = width // 128
        nc = self.nc
        n_tiles = len(tiles)
        tr = self.sem()
        rd = self.sem()
        tr0, rd0 = self.v(tr), self.v(rd)
        with nc.Block() as block:
            @block.tensor
            def _(t):
                n = 0
                for s in range(nsl):
                    for i in range(n_tiles):
                        if n >= 2:
                            t.wait_ge(rd, rd0 + n - 1)
                        self.inc(
                            t.transpose(pst2[:, (n % 2) * 512:(n % 2) * 512 + 128],
                                        tiles[i][:, s * 128:(s + 1) * 128],
                                        self.ident[:, :]), tr, 1)
                        n += 1

            @block.vector
            def _(v):
                n = 0
                for s in range(nsl):
                    for i in range(n_tiles):
                        v.wait_ge(tr, tr0 + n + 1)
                        dst = acc_col if i == 0 else red_col
                        v.tensor_reduce(
                            out=dst[:, :],
                            in_=pst2[:, (n % 2) * 512:(n % 2) * 512 + 128],
                            axis=AX.X, op=red_op,
                            apply_absolute_value=use_abs)
                        v.drain()
                        if i > 0:
                            v.tensor_tensor(out=acc_col[:, :], in0=acc_col[:, :],
                                            in1=red_col[:, :], op=red_op)
                            v.drain()
                        self.inc(v.nop(), rd, 1)
                        n += 1
                    v.tensor_copy(out=pam_buf[:, s:s + 1], in_=acc_col[:, :])
                    v.drain()

        t2 = self.sem()
        c2 = self.sem()
        t20, c20 = self.v(t2), self.v(c2)
        with nc.Block() as block:
            @block.tensor
            def _(t):
                for s in range(nsl):
                    if s >= 4:
                        t.wait_ge(c2, c20 + s - 3)
                    self.inc(
                        t.transpose(pcols[0:1, (s % 4) * 512:(s % 4) * 512 + 128],
                                    pam_buf[:, s:s + 1], self.ident[:, :]),
                        t2, 1)

            @block.vector
            def _(v):
                for s in range(nsl):
                    v.wait_ge(t2, t20 + s + 1)
                    v.tensor_copy(out=out_row[0:1, s * 128:(s + 1) * 128],
                                  in_=pcols[0:1, (s % 4) * 512:(s % 4) * 512 + 128])
                    v.drain()
                    self.inc(v.nop(), c2, 1)

    def bcast_scalar_to_col(self, scal_ap, col_out, ps_col):
        nc = self.nc
        bs = self.sem()
        bs0 = self.v(bs)
        with nc.Block() as block:
            @block.tensor
            def _(t):
                self.inc(t.matmul(ps_col[0:128, 0:1], self.ones_row[:, :],
                                  scal_ap, start=True, stop=True), bs, 1)

            @block.vector
            def _(v):
                v.wait_ge(bs, bs0 + 1)
                v.tensor_copy(out=col_out[:, 0:1], in_=ps_col[0:128, 0:1])
                v.drain()



def weight_meanabs(k, wT, n_chunks, chunk_cols, asum, col, pst2, scal_ap,
                   inv_count):
    nc = k.nc
    wf_cm = nc.sbuf_tensor(k.name("wma_wf"), [128, 2 * chunk_cols], F32)
    wf = wf_cm.__enter__().ap()
    ld2 = [k.dsem(), k.dsem()]
    rd = k.sem()
    ld20 = [k.v(ld2[0]), k.v(ld2[1])]
    rd0 = k.v(rd)
    with nc.Block() as block:
        @block.sync
        def _(s):
            for i in range(n_chunks):
                if i >= 2:
                    s.wait_ge(rd, rd0 + i - 1)
                k.inc(s.dma_start(
                    out=wf[:, (i % 2) * chunk_cols:(i % 2 + 1) * chunk_cols],
                    in_=wT[i * 128:(i + 1) * 128, :]), ld2[i % 2], 16)

        @block.vector
        def _(v):
            for i in range(n_chunks):
                v.wait_ge(ld2[i % 2], ld20[i % 2] + 16 * (i // 2 + 1))
                v.tensor_reduce(
                    out=asum[:, i:i + 1],
                    in_=wf[:, (i % 2) * chunk_cols:(i % 2 + 1) * chunk_cols],
                    axis=AX.X, op=ALU.add, apply_absolute_value=True)
                v.drain()
                k.inc(v.nop(), rd, 1)
            v.tensor_reduce(out=col[:, :], in_=asum[:, 0:n_chunks], axis=AX.X,
                            op=ALU.add)
            v.drain()

    wf_cm.__exit__(None, None, None)
    tr = k.sem()
    tr0 = k.v(tr)
    with nc.Block() as block:
        @block.tensor
        def _(t):
            k.inc(t.transpose(pst2[0:1, 0:128], col[:, :], k.ident[:, :]),
                  tr, 1)

        @block.vector
        def _(v):
            v.wait_ge(tr, tr0 + 1)
            v.tensor_reduce(out=scal_ap, in_=pst2[0:1, 0:128], axis=AX.X,
                            op=ALU.add)
            v.drain()
            v.tensor_scalar(out=scal_ap, in0=scal_ap,
                            scalar1=float(inv_count), scalar2=1e-5,
                            op0=ALU.mult, op1=ALU.max)
            v.drain()


def weight_quantize(k, wT, n_chunks, chunk_cols, wq, scal_ap, col, pst2,
                    lim, tmp_scal):
    nc = k.nc
    with nc.Block() as block:
        @block.vector
        def _(v):
            v.reciprocal(out=tmp_scal, in_=scal_ap)
            v.drain()

    k.bcast_scalar_to_col(tmp_scal, col, pst2)

    wf_cm = nc.sbuf_tensor(k.name("wq_wf"), [128, 2 * chunk_cols], F32)
    wf = wf_cm.__enter__().ap()
    ld2 = [k.dsem(), k.dsem()]
    rd = k.sem()
    ld20 = [k.v(ld2[0]), k.v(ld2[1])]
    rd0 = k.v(rd)
    with nc.Block() as block:
        @block.sync
        def _(s):
            for i in range(n_chunks):
                if i >= 2:
                    s.wait_ge(rd, rd0 + i - 1)
                k.inc(s.dma_start(
                    out=wf[:, (i % 2) * chunk_cols:(i % 2 + 1) * chunk_cols],
                    in_=wT[i * 128:(i + 1) * 128, :]), ld2[i % 2], 16)

        @block.vector
        def _(v):
            for i in range(n_chunks):
                v.wait_ge(ld2[i % 2], ld20[i % 2] + 16 * (i // 2 + 1))
                src = wf[:, (i % 2) * chunk_cols:(i % 2 + 1) * chunk_cols]
                dst = wq[:, i * chunk_cols:(i + 1) * chunk_cols]
                v.tensor_scalar(out=src, in0=src, scalar1=col[:, 0:1],
                                scalar2=MAGIC, op0=ALU.mult, op1=ALU.add)
                v.drain()
                v.tensor_scalar(out=src, in0=src, scalar1=MAGIC,
                                scalar2=float(lim), op0=ALU.subtract,
                                op1=ALU.min)
                v.drain()
                v.tensor_scalar(out=dst, in0=src, scalar1=float(-lim),
                                scalar2=None, op0=ALU.max, op1=ALU.bypass)
                v.drain()
                k.inc(v.nop(), rd, 1)
    wf_cm.__exit__(None, None, None)



def build_l1_ffn(k, xt, dram):
    nc = k.nc
    CT = 32

    w1T, w2T = dram["l1f_w1T"], dram["l1f_w2T"]
    convw, convb = dram["l1f_convw"], dram["l1f_convb"]
    spill = dram["l1f_spill"]

    with contextlib.ExitStack() as es0:
        wq_t = es0.enter_context(nc.sbuf_tensor("f_wq", [128, 32768], F8))
        asum_t = es0.enter_context(nc.sbuf_tensor("f_asum", [128, 32], F32))
        col_t = es0.enter_context(nc.sbuf_tensor("f_col", [128, 1], F32))
        col2_t = es0.enter_context(nc.sbuf_tensor("f_col2", [128, 1], F32))
        rowA_t = es0.enter_context(nc.sbuf_tensor("f_rowA", [1, 2048], F32))
        rowB_t = es0.enter_context(nc.sbuf_tensor("f_rowB", [1, 2048], F32))
        rowC_t = es0.enter_context(nc.sbuf_tensor("f_rowC", [1, 2048], F32))
        rowD_t = es0.enter_context(nc.sbuf_tensor("f_rowD", [1, 2048], F32))
        scal_t = es0.enter_context(nc.sbuf_tensor("f_scal", [1, 4], F32))
        pam_t = es0.enter_context(nc.sbuf_tensor("f_pam", [128, 16], F32))
        kq_t = es0.enter_context(nc.sbuf_tensor("f_kq", [128, 8 * 2048], BF16))
        cw_t = es0.enter_context(nc.sbuf_tensor("f_convw", [128, 96], F32))
        cb_t = es0.enter_context(nc.sbuf_tensor("f_convb", [128, 32], F32))
        psA_t = es0.enter_context(nc.psum_tensor("f_psA", [128, 2048], F32))
        psB_t = es0.enter_context(nc.psum_tensor("f_psB", [128, 2048], F32))

        wq, asum = wq_t.ap(), asum_t.ap()
        col, col2 = col_t.ap(), col2_t.ap()
        rowA, rowB = rowA_t.ap(), rowB_t.ap()
        rowC, rowD = rowC_t.ap(), rowD_t.ap()
        rowE, rowF = rowB, rowC
        scal, pam = scal_t.ap(), pam_t.ap()
        kq, cw, cb = kq_t.ap(), cw_t.ap(), cb_t.ap()
        psa, psb = psA_t.ap(), psB_t.ap()
        ps_row = psa[0:1, :]
        pst2 = psb[:, 0:1024]

        M1 = scal[0:1, 0:1]
        M2 = scal[0:1, 1:2]
        TMP = scal[0:1, 2:3]

        weight_meanabs(k, w1T, 8, 4096, asum, col, pst2, M1,
                       1.0 / (4096.0 * 1024.0))
        weight_quantize(k, w1T, 8, 4096, wq, M1, col, pst2, 1.0, TMP)

        ldc = k.dsem()
        with nc.Block() as block:
            @block.sync
            def _(s):
                k.inc(s.dma_start(out=cw[:, :], in_=convw[:, :]), ldc, 16)
                k.inc(s.dma_start(out=cb[:, :], in_=convb[:, :]), ldc, 16)
                s.wait_ge(ldc, k.v(ldc))

        with contextlib.ExitStack() as es1:
            sqa_t = es1.enter_context(nc.sbuf_tensor("f_sqa", [128, 2048], F32))
            sq_bufs = [sqa_t.ap()]

            k.ln_rows(xt, rowA, rowB, rowC, sq_bufs, ps_row)
            k.part_reduce(xt, rowC, pst2, ps_row, col, col2, pam, mode="max")
            k.part_reduce(xt, rowD, pst2, ps_row, col, col2, pam, mode="min")

            with nc.Block() as block:
                @block.vector
                def _(v):
                    v.tensor_tensor(out=rowC[:, :], in0=rowC[:, :],
                                    in1=rowA[:, :], op=ALU.subtract)
                    v.tensor_tensor(out=rowD[:, :], in0=rowA[:, :],
                                    in1=rowD[:, :], op=ALU.subtract)
                    v.drain()
                    v.tensor_tensor(out=rowC[:, :], in0=rowC[:, :],
                                    in1=rowD[:, :], op=ALU.max)
                    v.drain()
                    v.tensor_tensor(out=rowC[:, :], in0=rowC[:, :],
                                    in1=rowB[:, :], op=ALU.mult)
                    v.drain()
                    v.tensor_scalar(out=rowC[:, :], in0=rowC[:, :],
                                    scalar1=1e-5, scalar2=None,
                                    op0=ALU.max, op1=ALU.bypass)
                    v.drain()
                    v.reciprocal(out=rowD[:, :], in_=rowC[:, :])
                    v.drain()
                    v.tensor_scalar(out=rowD[:, :], in0=rowD[:, :],
                                    scalar1=127.0, scalar2=None,
                                    op0=ALU.mult, op1=ALU.bypass)
                    v.drain()
                    v.tensor_tensor(out=rowE[:, :], in0=rowB[:, :],
                                    in1=rowD[:, :], op=ALU.mult)
                    v.drain()
                    v.scalar_tensor_tensor(out=rowF[:, :], in0=rowC[:, :],
                                           scalar=M1, in1=rowC[:, :],
                                           op0=ALU.mult, op1=ALU.bypass)
                    v.drain()
                    v.tensor_scalar(out=rowF[:, :], in0=rowF[:, :],
                                    scalar1=1.0 / 127.0, scalar2=None,
                                    op0=ALU.mult, op1=ALU.bypass)
                    v.drain()

            mq = k.sem()
            mq0 = k.v(mq)
            with nc.Block() as block:
                @block.tensor
                def _(t):
                    k.bcast_row(t, rowA, psa, 2048)
                    k.inc(k.bcast_row(t, rowE, psb, 2048), mq, 1)

                @block.vector
                def _(v):
                    v.wait_ge(mq, mq0 + 1)
                    for i in range(8):
                        v.tensor_tensor(out=sq_bufs[0][:, :], in0=xt[i][:, :],
                                        in1=psa[:, :], op=ALU.subtract)
                        v.drain()
                        v.tensor_tensor(out=sq_bufs[0][:, :],
                                        in0=sq_bufs[0][:, :],
                                        in1=psb[:, :], op=ALU.mult)
                        v.drain()
                        v.tensor_scalar(out=sq_bufs[0][:, :],
                                        in0=sq_bufs[0][:, :],
                                        scalar1=MAGIC, scalar2=MAGIC,
                                        op0=ALU.add, op1=ALU.subtract)
                        v.drain()
                        v.tensor_scalar(out=kq[:, i * 2048:(i + 1) * 2048],
                                        in0=sq_bufs[0][:, :],
                                        scalar1=127.0, scalar2=-127.0,
                                        op0=ALU.min, op1=ALU.max)
                        v.drain()

        with contextlib.ExitStack() as es2:
            rs1b_t = es2.enter_context(nc.sbuf_tensor("f_rs1b", [128, 2048], F32))
            h_t = es2.enter_context(nc.sbuf_tensor("f_h", [128, 2048], F32))
            z1_t = es2.enter_context(nc.sbuf_tensor("f_z1", [128, 2048], F32))
            z2_t = es2.enter_context(nc.sbuf_tensor("f_z2", [128, 2048], F32))
            hmax_t = es2.enter_context(nc.sbuf_tensor("f_hmax", [128, 2048], F32))

            rs1b = rs1b_t.ap()
            h, z1, z2 = h_t.ap(), z1_t.ap(), z2_t.ap()
            z3, h2o, habs = z1, z2, z1
            hmax = hmax_t.ap()

            bq = k.sem()
            bq0 = k.v(bq)
            with nc.Block() as block:
                @block.tensor
                def _(t):
                    k.inc(k.bcast_row(t, rowF, psa, 2048), bq, 1)

                @block.vector
                def _(v):
                    v.wait_ge(bq, bq0 + 1)
                    v.tensor_copy(out=rs1b[:, :], in_=psa[:, :])
                    v.drain()

            mmd = k.sem()
            cvd = k.sem()
            sil = k.sem()
            sabs = k.sem()
            spw = k.gsem()
            mmd0, cvd0, sil0, sabs0, spw0 = (
                k.v(mmd), k.v(cvd), k.v(sil), k.v(sabs), k.v(spw))
            with nc.Block() as block:
                @block.tensor
                def _(t):
                    for j in range(CT):
                        ps = psa if j % 2 == 0 else psb
                        if j >= 2:
                            t.wait_ge(cvd, cvd0 + j - 1)
                        ins = None
                        for i in range(8):
                            lhs = wq[:, i * 4096 + j * 128:i * 4096 + (j + 1) * 128]
                            for q in range(NT):
                                ins = t.matmul(
                                    ps[:, q * 512:(q + 1) * 512], lhs,
                                    kq[:, i * 2048 + q * 512:i * 2048 + (q + 1) * 512],
                                    start=(i == 0), stop=(i == 7))
                        k.inc(ins, mmd, 1)

                @block.vector
                def _(v):
                    for j in range(CT):
                        ps = psa if j % 2 == 0 else psb
                        v.wait_ge(mmd, mmd0 + j + 1)
                        if j >= 1:
                            v.wait_ge(sabs, sabs0 + 2 * j)
                            v.wait_ge(spw, spw0 + 16 * j)
                        v.tensor_tensor(out=h[:, :], in0=ps[:, :],
                                        in1=rs1b[:, :], op=ALU.mult)
                        v.drain()
                        k.inc(v.nop(), cvd, 1)
                        v.tensor_scalar(out=z1[:, :], in0=h[:, :],
                                        scalar1=cw[:, 3 * j + 1:3 * j + 2],
                                        scalar2=cb[:, j:j + 1],
                                        op0=ALU.mult, op1=ALU.add)
                        v.drain()
                        v.tensor_copy(out=z2[:, 0:1], in_=z1[:, 0:1])
                        v.scalar_tensor_tensor(out=z2[:, 1:2048],
                                               in0=h[:, 0:2047],
                                               scalar=cw[:, 3 * j:3 * j + 1],
                                               in1=z1[:, 1:2048],
                                               op0=ALU.mult, op1=ALU.add)
                        v.drain()
                        v.tensor_copy(out=z3[:, 2047:2048], in_=z2[:, 2047:2048])
                        v.scalar_tensor_tensor(out=z3[:, 0:2047],
                                               in0=h[:, 1:2048],
                                               scalar=cw[:, 3 * j + 2:3 * j + 3],
                                               in1=z2[:, 0:2047],
                                               op0=ALU.mult, op1=ALU.add)
                        v.drain()
                        k.inc(v.nop(), sil, 1)
                        v.wait_ge(sabs, sabs0 + 2 * j + 2)
                        if j == 0:
                            v.tensor_copy(out=hmax[:, :], in_=habs[:, :])
                        else:
                            v.tensor_tensor(out=hmax[:, :], in0=hmax[:, :],
                                            in1=habs[:, :], op=ALU.max)
                        v.drain()

                @block.scalar
                def _(s):
                    for j in range(CT):
                        s.wait_ge(sil, sil0 + j + 1)
                        if j >= 1:
                            s.wait_ge(spw, spw0 + 16 * j)
                        s.activation(out=h2o[:, :], in_=z3[:, :], func=AF.Silu)
                        s.drain()
                        k.inc(s.nop(), sabs, 1)
                        s.activation(out=habs[:, :], in_=h2o[:, :], func=AF.Abs)
                        s.drain()
                        k.inc(s.nop(), sabs, 1)

                @block.gpsimd
                def _(g):
                    for j in range(CT):
                        g.wait_ge(sabs, sabs0 + 2 * j + 1)
                        k.inc(g.dma_start(out=spill[j, :, :], in_=h2o[:, :]),
                              spw, 16)
                        g.wait_ge(spw, spw0 + 16 * (j + 1))

            k.part_reduce([hmax], rowC, pst2, ps_row, col, col2, pam, mode="absmax")

        weight_meanabs(k, w2T, 32, 1024, asum, col, pst2, M2,
                       1.0 / (4096.0 * 1024.0))
        weight_quantize(k, w2T, 32, 1024, wq, M2, col, pst2, 1.0, TMP)

        with nc.Block() as block:
            @block.vector
            def _(v):
                v.tensor_scalar(out=rowC[:, :], in0=rowC[:, :],
                                scalar1=1e-5, scalar2=None,
                                op0=ALU.max, op1=ALU.bypass)
                v.drain()
                v.reciprocal(out=rowD[:, :], in_=rowC[:, :])
                v.drain()
                v.tensor_scalar(out=rowD[:, :], in0=rowD[:, :],
                                scalar1=127.0, scalar2=None,
                                op0=ALU.mult, op1=ALU.bypass)
                v.scalar_tensor_tensor(out=rowC[:, :], in0=rowC[:, :],
                                       scalar=M2, in1=rowC[:, :],
                                       op0=ALU.mult, op1=ALU.bypass)
                v.drain()
                v.tensor_scalar(out=rowC[:, :], in0=rowC[:, :],
                                scalar1=1.0 / 127.0, scalar2=None,
                                op0=ALU.mult, op1=ALU.bypass)
                v.drain()

        with contextlib.ExitStack() as es3:
            h2s_t = es3.enter_context(nc.sbuf_tensor("f_h2s", [128, 1024], F32))
            k2_t = es3.enter_context(nc.sbuf_tensor("f_k2", [128, 1024], BF16))
            sb_t = es3.enter_context(nc.sbuf_tensor("f_sb", [128, 512], F32))
            rsb_t = es3.enter_context(nc.sbuf_tensor("f_rsb", [128, 512], F32))
            t1_t = es3.enter_context(nc.sbuf_tensor("f_t1", [128, 512], F32))

            h2s_f, k2_f = h2s_t.ap(), k2_t.ap()
            h2s = [h2s_f[:, 0:512], h2s_f[:, 512:1024]]
            k2 = [k2_f[:, 0:512], k2_f[:, 512:1024]]
            sbb, rsb, t1 = sb_t.ap(), rsb_t.ap(), t1_t.ap()

            for tt in range(4):
                tsl = slice(tt * 512, (tt + 1) * 512)
                bq = k.sem()
                bq0 = k.v(bq)
                with nc.Block() as block:
                    @block.tensor
                    def _(t):
                        k.bcast_row(t, rowD[0:1, tsl], psa[:, 0:512], 512)
                        k.inc(k.bcast_row(t, rowC[0:1, tsl], psa[:, 512:1024],
                                          512), bq, 1)

                    @block.vector
                    def _(v):
                        v.wait_ge(bq, bq0 + 1)
                        v.tensor_copy(out=sbb[:, :], in_=psa[:, 0:512])
                        v.tensor_copy(out=rsb[:, :], in_=psa[:, 512:1024])
                        v.drain()

                ldh2 = [k.dsem(), k.dsem()]
                kqd = k.sem()
                mme = k.sem()
                ldh20 = [k.v(ldh2[0]), k.v(ldh2[1])]
                kqd0, mme0 = k.v(kqd), k.v(mme)
                with nc.Block() as block:
                    @block.sync
                    def _(s):
                        for j in range(CT):
                            if j >= 2:
                                s.wait_ge(kqd, kqd0 + j - 1)
                            k.inc(s.dma_start(out=h2s[j % 2],
                                              in_=spill[j, :, tsl]),
                                  ldh2[j % 2], 16)

                    @block.vector
                    def _(v):
                        for j in range(CT):
                            v.wait_ge(ldh2[j % 2], ldh20[j % 2] + 16 * (j // 2 + 1))
                            if j >= 2:
                                v.wait_ge(mme, mme0 + j - 1)
                            v.tensor_tensor(out=t1[:, :], in0=h2s[j % 2],
                                            in1=sbb[:, :], op=ALU.mult)
                            v.drain()
                            v.tensor_scalar(out=t1[:, :], in0=t1[:, :],
                                            scalar1=MAGIC, scalar2=MAGIC,
                                            op0=ALU.add, op1=ALU.subtract)
                            v.drain()
                            v.tensor_scalar(out=k2[j % 2], in0=t1[:, :],
                                            scalar1=127.0, scalar2=-127.0,
                                            op0=ALU.min, op1=ALU.max)
                            v.drain()
                            k.inc(v.nop(), kqd, 1)

                    @block.tensor
                    def _(t):
                        for j in range(CT):
                            t.wait_ge(kqd, kqd0 + j + 1)
                            ins = None
                            for i in range(8):
                                lhs = wq[:, j * 1024 + i * 128:j * 1024 + (i + 1) * 128]
                                ps = (psa if i < 4 else psb)
                                ins = t.matmul(
                                    ps[:, (i % 4) * 512:(i % 4) * 512 + 512],
                                    lhs, k2[j % 2],
                                    start=(j == 0), stop=(j == CT - 1))
                            k.inc(ins, mme, 1)

                with nc.Block() as block:
                    @block.vector
                    def _(v):
                        for i in range(8):
                            ps = (psa if i < 4 else psb)
                            v.tensor_tensor(
                                out=t1[:, :],
                                in0=ps[:, (i % 4) * 512:(i % 4) * 512 + 512],
                                in1=rsb[:, :], op=ALU.mult)
                            v.drain()
                            v.tensor_tensor(out=xt[i][:, tsl],
                                            in0=xt[i][:, tsl],
                                            in1=t1[:, :], op=ALU.add)
                            v.drain()



def build_l0_ffn(k, xt, dram, upto="full"):
    nc = k.nc
    JT = 32

    win = dram["l0f_win"]
    bin_c = dram["l0f_binc"]
    gmax1 = dram["l0f_gmax1"]
    lmax = dram.get("l0f_lmax")
    spill = dram["l0f_spill"]

    with contextlib.ExitStack() as es0:
        psA_t = es0.enter_context(nc.psum_tensor("g_psA", [128, 2048], F32))
        psB_t = es0.enter_context(nc.psum_tensor("g_psB", [128, 2048], F32))
        amaxr_t = es0.enter_context(nc.sbuf_tensor("g_amaxr", [1, 4096], F32))
        scal_t = es0.enter_context(nc.sbuf_tensor("g_scal", [1, 8], F32))
        gmx_t = es0.enter_context(nc.sbuf_tensor("g_gmx", [1, 16], F32))
        col_t = es0.enter_context(nc.sbuf_tensor("g_col", [128, 1], F32))
        col2_t = es0.enter_context(nc.sbuf_tensor("g_col2", [128, 1], F32))
        s2col_t = es0.enter_context(nc.sbuf_tensor("g_s2col", [128, 1], F32))
        g2c_t = es0.enter_context(nc.sbuf_tensor("g_g2c", [128, 1], F32))
        pam_t = es0.enter_context(nc.sbuf_tensor("g_pam", [128, 32], F32))
        deqc_t = es0.enter_context(nc.sbuf_tensor("g_deqc", [128, 32], F32))
        binl_t = es0.enter_context(nc.sbuf_tensor("g_binl", [128, 32], F32))
        boutc_t = es0.enter_context(nc.sbuf_tensor("g_boutc", [128, 8], F32))

        psa, psb = psA_t.ap(), psB_t.ap()
        ps_row = psa[0:1, :]
        pst2 = psb[:, 0:1024]
        amaxr = amaxr_t.ap()
        scal, gmx = scal_t.ap(), gmx_t.ap()
        col, col2, s2col = col_t.ap(), col2_t.ap(), s2col_t.ap()
        g2c = g2c_t.ap()
        pam, deqc, binl, boutc = pam_t.ap(), deqc_t.ap(), binl_t.ap(), boutc_t.ap()

        G1 = scal[0:1, 0:1]
        G2 = scal[0:1, 1:2]
        SX = scal[0:1, 2:3]
        TMP = scal[0:1, 3:4]
        LM1 = scal[0:1, 4:5]
        LM2 = scal[0:1, 5:6]

        ldc = k.dsem()
        with nc.Block() as block:
            @block.gpsimd
            def _(g):
                g.memset(scal[0:1, :], 0.0)
                g.drain()

            @block.sync
            def _(s):
                k.inc(s.dma_start(out=binl[:, :], in_=bin_c[:, :]), ldc, 16)
                k.inc(s.dma_start(out=gmx[0:1, 0:8], in_=gmax1[0:1, :]), ldc, 16)
                if upto == "full":
                    k.inc(s.dma_start(out=gmx[0:1, 8:16],
                                      in_=dram["l0f_gmax2"][0:1, :]), ldc, 16)
                    k.inc(s.dma_start(out=boutc[:, :],
                                      in_=dram["l0f_boutc"][:, :]), ldc, 16)
                s.wait_ge(ldc, k.v(ldc))

        with nc.Block() as block:
            @block.vector
            def _(v):
                v.tensor_reduce(out=G1, in_=gmx[0:1, 0:8], axis=AX.X,
                                op=ALU.max)
                v.drain()
                v.tensor_scalar(out=G1, in0=G1, scalar1=1e-8, scalar2=None,
                                op0=ALU.max, op1=ALU.bypass)
                v.drain()
                v.reciprocal(out=SX, in_=G1)
                v.drain()
                v.tensor_scalar(out=SX, in0=SX, scalar1=32767.0, scalar2=None,
                                op0=ALU.mult, op1=ALU.bypass)
                v.drain()

        with contextlib.ExitStack() as esK:
            khi_t = esK.enter_context(nc.sbuf_tensor("g_khi", [128, 16384], BF16))
            klo_t = esK.enter_context(nc.sbuf_tensor("g_klo", [128, 16384], BF16))
            khi, klo = khi_t.ap(), klo_t.ap()

            with contextlib.ExitStack() as esL:
                rowA_t = esL.enter_context(nc.sbuf_tensor("g_rowA", [1, 2048], F32))
                rowB_t = esL.enter_context(nc.sbuf_tensor("g_rowB", [1, 2048], F32))
                sq_t = esL.enter_context(nc.sbuf_tensor("g_sq", [128, 2048], F32))
                sq2_t = esL.enter_context(nc.sbuf_tensor("g_sq2", [128, 2048], F32))
                rowA, rowB, sq = rowA_t.ap(), rowB_t.ap(), sq_t.ap()
                sq2 = sq2_t.ap()

                k.ln_rows(xt, rowA, rowB, amaxr_row_tmp(amaxr), [sq], ps_row)

                with nc.Block() as block:
                    @block.vector
                    def _(v):
                        v.scalar_tensor_tensor(out=rowB[:, :], in0=rowB[:, :],
                                               scalar=SX, in1=rowB[:, :],
                                               op0=ALU.mult, op1=ALU.bypass)
                        v.drain()

                mq = k.sem()
                mq0 = k.v(mq)
                with nc.Block() as block:
                    @block.tensor
                    def _(t):
                        k.bcast_row(t, rowA, psa, 2048)
                        k.inc(k.bcast_row(t, rowB, psb, 2048), mq, 1)

                    @block.vector
                    def _(v):
                        v.wait_ge(mq, mq0 + 1)
                        for i in range(8):
                            v.tensor_tensor(out=sq[:, :], in0=xt[i][:, :],
                                            in1=psa[:, :], op=ALU.subtract)
                            v.drain()
                            v.tensor_tensor(out=sq[:, :], in0=sq[:, :],
                                            in1=psb[:, :], op=ALU.mult)
                            v.drain()
                            v.tensor_scalar(out=sq[:, :], in0=sq[:, :],
                                            scalar1=MAGIC, scalar2=MAGIC,
                                            op0=ALU.add, op1=ALU.subtract)
                            v.drain()
                            v.tensor_scalar(out=sq[:, :], in0=sq[:, :],
                                            scalar1=32767.0, scalar2=-32768.0,
                                            op0=ALU.min, op1=ALU.max)
                            v.drain()
                            v.tensor_scalar(out=sq2[:, :], in0=sq[:, :],
                                            scalar1=1.0 / 256.0, scalar2=MAGIC,
                                            op0=ALU.mult, op1=ALU.add)
                            v.drain()
                            v.tensor_scalar(out=khi[:, i * 2048:(i + 1) * 2048],
                                            in0=sq2[:, :],
                                            scalar1=MAGIC, scalar2=256.0,
                                            op0=ALU.subtract, op1=ALU.mult)
                            v.drain()
                            v.tensor_tensor(out=klo[:, i * 2048:(i + 1) * 2048],
                                            in0=sq[:, :],
                                            in1=khi[:, i * 2048:(i + 1) * 2048],
                                            op=ALU.subtract)
                            v.drain()

            with contextlib.ExitStack() as es2:
                acc_t = es2.enter_context(nc.sbuf_tensor("g_acc", [128, 4096], F32))
                wfa_t = es2.enter_context(nc.sbuf_tensor("g_wfa", [128, 8192], F32))
                acc, wfa = acc_t.ap(), wfa_t.ap()
                ld2 = [k.dsem(), k.dsem()]
                rd = k.sem()
                ld20 = [k.v(ld2[0]), k.v(ld2[1])]
                rd0 = k.v(rd)
                with nc.Block() as block:
                    @block.sync
                    def _(s):
                        for i in range(8):
                            if i >= 2:
                                s.wait_ge(rd, rd0 + i - 1)
                            k.inc(s.dma_start(
                                out=wfa[:, (i % 2) * 4096:(i % 2 + 1) * 4096],
                                in_=win[i * 128:(i + 1) * 128, :]),
                                ld2[i % 2], 16)

                    @block.vector
                    def _(v):
                        for i in range(8):
                            v.wait_ge(ld2[i % 2], ld20[i % 2] + 16 * (i // 2 + 1))
                            w = wfa[:, (i % 2) * 4096:(i % 2 + 1) * 4096]
                            if i == 0:
                                v.tensor_scalar(out=acc[:, :], in0=w,
                                                scalar1=-1.0, scalar2=None,
                                                op0=ALU.mult, op1=ALU.bypass)
                                v.drain()
                                v.tensor_tensor(out=acc[:, :], in0=acc[:, :],
                                                in1=w, op=ALU.max)
                                v.drain()
                            else:
                                v.scalar_tensor_tensor(out=acc[:, :], in0=w,
                                                       scalar=-1.0, in1=acc[:, :],
                                                       op0=ALU.mult, op1=ALU.max)
                                v.drain()
                                v.scalar_tensor_tensor(out=acc[:, :], in0=w,
                                                       scalar=1.0, in1=acc[:, :],
                                                       op0=ALU.mult, op1=ALU.max)
                                v.drain()
                            k.inc(v.nop(), rd, 1)

                k.part_reduce([acc], amaxr, pst2, ps_row, col, col2, pam,
                              mode="absmax")

            with nc.Block() as block:
                @block.vector
                def _(v):
                    v.tensor_scalar(out=amaxr[:, :], in0=amaxr[:, :],
                                    scalar1=1e-8, scalar2=None,
                                    op0=ALU.max, op1=ALU.bypass)
                    v.drain()
                    v.tensor_scalar(out=TMP, in0=G1,
                                    scalar1=1.0 / (32767.0 * 127.0),
                                    scalar2=None, op0=ALU.mult, op1=ALU.bypass)
                    v.drain()

            k.bcast_scalar_to_col(TMP, col2, pst2)
            t2 = k.sem()
            c2 = k.sem()
            t20, c20 = k.v(t2), k.v(c2)
            with nc.Block() as block:
                @block.tensor
                def _(t):
                    for s in range(32):
                        if s >= 2:
                            t.wait_ge(c2, c20 + s - 1)
                        k.inc(t.transpose(
                            pst2[:, (s % 2) * 512:(s % 2) * 512 + 1],
                            amaxr[0:1, s * 128:(s + 1) * 128],
                            k.ident[0:1, 0:1]), t2, 1)

                @block.vector
                def _(v):
                    for s in range(32):
                        v.wait_ge(t2, t20 + s + 1)
                        v.tensor_scalar(
                            out=deqc[:, s:s + 1],
                            in0=pst2[:, (s % 2) * 512:(s % 2) * 512 + 1],
                            scalar1=col2[:, 0:1], scalar2=None,
                            op0=ALU.mult, op1=ALU.bypass)
                        v.drain()
                        k.inc(v.nop(), c2, 1)

            with nc.Block() as block:
                @block.vector
                def _(v):
                    v.reciprocal(out=amaxr[:, :], in_=amaxr[:, :])
                    v.drain()
                    v.tensor_scalar(out=amaxr[:, :], in0=amaxr[:, :],
                                    scalar1=127.0, scalar2=None,
                                    op0=ALU.mult, op1=ALU.bypass)
                    v.drain()

            with contextlib.ExitStack() as esM:
                sfull_t = esM.enter_context(nc.sbuf_tensor("g_sfull", [128, 4096], F32))
                sfull = sfull_t.ap()
                bs = k.sem()
                bs0 = k.v(bs)
                with nc.Block() as block:
                    @block.tensor
                    def _(t):
                        for h in range(2):
                            t.wait_ge(bs, bs0 + 2 * h)
                            k.inc(k.bcast_row(t, amaxr[0:1, h * 2048:(h + 1) * 2048],
                                              psa, 2048), bs, 1)

                    @block.vector
                    def _(v):
                        for h in range(2):
                            v.wait_ge(bs, bs0 + 2 * h + 1)
                            v.tensor_copy(out=sfull[:, h * 2048:(h + 1) * 2048],
                                          in_=psa[:, :])
                            v.drain()
                            k.inc(v.nop(), bs, 1)

                with contextlib.ExitStack() as es3:
                    wst_t = es3.enter_context(nc.sbuf_tensor("g_wst", [128, 2048], F32))
                    mst_t = es3.enter_context(nc.sbuf_tensor("g_mst", [128, 2048], BF16))
                    hid_t = es3.enter_context(nc.sbuf_tensor("g_hid", [128, 2048], F32))
                    habs_t = es3.enter_context(nc.sbuf_tensor("g_habs", [128, 2048], F32))
                    hmax_t = es3.enter_context(nc.sbuf_tensor("g_hmax", [128, 2048], F32))
                    wst, mst = wst_t.ap(), mst_t.ap()
                    hid, habs, hmax = hid_t.ap(), habs_t.ap(), hmax_t.ap()
                    wsl = [wst[:, 0:1024], wst[:, 1024:2048]]
                    msl = [mst[:, 0:1024], mst[:, 1024:2048]]

                    ldw2 = [k.dsem(), k.dsem()]
                    qd = k.sem()
                    mmd = k.sem()
                    gel = k.sem()
                    spw = k.gsem()
                    mxd = k.sem()
                    ldw20 = [k.v(ldw2[0]), k.v(ldw2[1])]
                    qd0, mmd0, gel0, spw0, mxd0 = (
                        k.v(qd), k.v(mmd), k.v(gel), k.v(spw), k.v(mxd))

                    with nc.Block() as block:
                        @block.sync
                        def _(s):
                            for j in range(JT):
                                if j >= 2:
                                    s.wait_ge(qd, qd0 + j - 1)
                                k.inc(s.dma_start(
                                    out=wsl[j % 2].rearrange(
                                        "p (c j) -> p c j", c=8),
                                    in_=win[:, j * 128:(j + 1) * 128].rearrange(
                                        "(c p) j -> p c j", p=128)),
                                    ldw2[j % 2], 16)

                        @block.vector
                        def _(v):
                            def fold(m):
                                v.wait_ge(gel, gel0 + 2 * m + 2)
                                if m == 0:
                                    v.tensor_copy(out=hmax[:, :], in_=habs[:, :])
                                else:
                                    v.tensor_tensor(out=hmax[:, :],
                                                    in0=hmax[:, :],
                                                    in1=habs[:, :], op=ALU.max)
                                v.drain()
                                k.inc(v.nop(), mxd, 1)

                            for j in range(JT):
                                v.wait_ge(ldw2[j % 2],
                                          ldw20[j % 2] + 16 * (j // 2 + 1))
                                if j >= 2:
                                    v.wait_ge(mmd, mmd0 + j - 1)
                                for c in range(8):
                                    v.tensor_tensor(
                                        out=wsl[j % 2][:, c * 128:(c + 1) * 128],
                                        in0=wsl[j % 2][:, c * 128:(c + 1) * 128],
                                        in1=sfull[:, j * 128:(j + 1) * 128],
                                        op=ALU.mult)
                                v.drain()
                                v.tensor_scalar(out=wsl[j % 2], in0=wsl[j % 2],
                                                scalar1=MAGIC, scalar2=MAGIC,
                                                op0=ALU.add, op1=ALU.subtract)
                                v.drain()
                                v.tensor_scalar(out=msl[j % 2], in0=wsl[j % 2],
                                                scalar1=127.0, scalar2=-128.0,
                                                op0=ALU.min, op1=ALU.max)
                                v.drain()
                                k.inc(v.nop(), qd, 1)
                                if j >= 1:
                                    fold(j - 1)
                            fold(JT - 1)

                        @block.tensor
                        def _(t):
                            for j in range(JT):
                                ps = psa if j % 2 == 0 else psb
                                t.wait_ge(qd, qd0 + j + 1)
                                if j >= 2:
                                    t.wait_ge(gel, gel0 + 2 * (j - 1))
                                ins = None
                                for c in range(8):
                                    lhs = msl[j % 2][:, c * 128:(c + 1) * 128]
                                    for q in range(NT):
                                        pss = ps[:, q * 512:(q + 1) * 512]
                                        ins = t.matmul(
                                            pss, lhs,
                                            khi[:, c * 2048 + q * 512:c * 2048 + (q + 1) * 512],
                                            start=(c == 0), stop=False)
                                        ins = t.matmul(
                                            pss, lhs,
                                            klo[:, c * 2048 + q * 512:c * 2048 + (q + 1) * 512],
                                            start=False, stop=(c == 7))
                                k.inc(ins, mmd, 1)

                        @block.scalar
                        def _(s):
                            for j in range(JT):
                                ps = psa if j % 2 == 0 else psb
                                s.wait_ge(mmd, mmd0 + j + 1)
                                if j >= 1:
                                    s.wait_ge(spw, spw0 + 16 * j)
                                    s.wait_ge(mxd, mxd0 + j)
                                s.activation(out=hid[:, :], in_=ps[:, :],
                                             func=AF.Gelu,
                                             bias=binl[:, j:j + 1],
                                             scale=deqc[:, j:j + 1])
                                s.drain()
                                k.inc(s.nop(), gel, 1)
                                s.activation(out=habs[:, :], in_=hid[:, :],
                                             func=AF.Abs)
                                s.drain()
                                k.inc(s.nop(), gel, 1)

                        @block.gpsimd
                        def _(g):
                            for j in range(JT):
                                g.wait_ge(gel, gel0 + 2 * j + 1)
                                k.inc(g.dma_start(out=spill[j, :, :],
                                                  in_=hid[:, :]), spw, 16)
                                g.wait_ge(spw, spw0 + 16 * (j + 1))

                    k.part_reduce([hmax], amaxr[0:1, 0:2048], pst2, ps_row,
                                  col, col2, pam, mode="absmax")
                    with nc.Block() as block:
                        @block.vector
                        def _(v):
                            v.tensor_reduce(out=LM2, in_=amaxr[0:1, 0:2048],
                                            axis=AX.X, op=ALU.max)
                            v.drain()

        if upto == "max2":
            stm = k.gsem()
            with nc.Block() as block:
                @block.gpsimd
                def _(g):
                    k.inc(g.dma_start(out=lmax[0:1, :], in_=scal[0:1, 4:8]),
                          stm, 16)
                    g.wait_ge(stm, k.v(stm))
            return

        with nc.Block() as block:
            @block.vector
            def _(v):
                v.tensor_reduce(out=G2, in_=gmx[0:1, 8:16], axis=AX.X,
                                op=ALU.max)
                v.drain()
                v.tensor_scalar(out=G2, in0=G2, scalar1=1e-8, scalar2=None,
                                op0=ALU.max, op1=ALU.bypass)
                v.drain()
                v.reciprocal(out=TMP, in_=G2)
                v.drain()
                v.tensor_scalar(out=TMP, in0=TMP, scalar1=32767.0,
                                scalar2=None, op0=ALU.mult, op1=ALU.bypass)
                v.drain()

        k.bcast_scalar_to_col(TMP, s2col, pst2)
        with nc.Block() as block:
            @block.vector
            def _(v):
                v.tensor_scalar(out=TMP, in0=G2,
                                scalar1=1.0 / (32767.0 * 127.0), scalar2=None,
                                op0=ALU.mult, op1=ALU.bypass)
                v.drain()
        k.bcast_scalar_to_col(TMP, g2c, pst2)

        wout = dram["l0f_wout"]
        with contextlib.ExitStack() as es4:
            m2_t = es4.enter_context(nc.sbuf_tensor("g_m2", [128, 32768], BF16))
            wfb_t = es4.enter_context(nc.sbuf_tensor("g_wfb", [128, 2048], F32))
            m2, wfb = m2_t.ap(), wfb_t.ap()
            wbl = [wfb[:, 0:1024], wfb[:, 1024:2048]]

            ldb2 = [k.dsem(), k.dsem()]
            rdb = k.sem()
            ldb20 = [k.v(ldb2[0]), k.v(ldb2[1])]
            rdb0 = k.v(rdb)
            with nc.Block() as block:
                @block.sync
                def _(s):
                    for i in range(32):
                        if i >= 2:
                            s.wait_ge(rdb, rdb0 + i - 1)
                        k.inc(s.dma_start(out=wbl[i % 2],
                                          in_=wout[i * 128:(i + 1) * 128, :]),
                              ldb2[i % 2], 16)

                @block.vector
                def _(v):
                    for i in range(32):
                        v.wait_ge(ldb2[i % 2], ldb20[i % 2] + 16 * (i // 2 + 1))
                        w = wbl[i % 2]
                        v.tensor_reduce(out=col[:, :], in_=w, axis=AX.X,
                                        op=ALU.max, apply_absolute_value=True)
                        v.drain()
                        v.tensor_scalar(out=col[:, :], in0=col[:, :],
                                        scalar1=1e-8, scalar2=None,
                                        op0=ALU.max, op1=ALU.bypass)
                        v.drain()
                        v.tensor_tensor(out=deqc[:, i:i + 1], in0=col[:, :],
                                        in1=g2c[:, 0:1], op=ALU.mult)
                        v.drain()
                        v.reciprocal(out=col2[:, :], in_=col[:, :])
                        v.drain()
                        v.tensor_scalar(out=col2[:, :], in0=col2[:, :],
                                        scalar1=127.0, scalar2=None,
                                        op0=ALU.mult, op1=ALU.bypass)
                        v.drain()
                        v.tensor_scalar(out=w, in0=w, scalar1=col2[:, 0:1],
                                        scalar2=MAGIC, op0=ALU.mult,
                                        op1=ALU.add)
                        v.drain()
                        v.tensor_scalar(out=w, in0=w, scalar1=MAGIC,
                                        scalar2=127.0, op0=ALU.subtract,
                                        op1=ALU.min)
                        v.drain()
                        v.tensor_scalar(out=m2[:, i * 1024:(i + 1) * 1024],
                                        in0=w, scalar1=-128.0, scalar2=None,
                                        op0=ALU.max, op1=ALU.bypass)
                        v.drain()
                        k.inc(v.nop(), rdb, 1)

            with contextlib.ExitStack() as es5:
                h2s_t = es5.enter_context(nc.sbuf_tensor("g_h2s", [128, 1024], F32))
                kfh_t = es5.enter_context(nc.sbuf_tensor("g_kfh", [128, 1024], BF16))
                kfl_t = es5.enter_context(nc.sbuf_tensor("g_kfl", [128, 1024], BF16))
                t1_t = es5.enter_context(nc.sbuf_tensor("g_t1", [128, 512], F32))
                h2s_f = h2s_t.ap()
                kfh_f, kfl_f = kfh_t.ap(), kfl_t.ap()
                t1 = t1_t.ap()
                h2s = [h2s_f[:, 0:512], h2s_f[:, 512:1024]]
                kfh = [kfh_f[:, 0:512], kfh_f[:, 512:1024]]
                kfl = [kfl_f[:, 0:512], kfl_f[:, 512:1024]]

                for tt in range(4):
                    tsl = slice(tt * 512, (tt + 1) * 512)
                    ldh2 = [k.dsem(), k.dsem()]
                    kqd = k.sem()
                    mme = k.sem()
                    ldh20 = [k.v(ldh2[0]), k.v(ldh2[1])]
                    kqd0, mme0 = k.v(kqd), k.v(mme)
                    with nc.Block() as block:
                        @block.sync
                        def _(s):
                            for j in range(JT):
                                if j >= 2:
                                    s.wait_ge(kqd, kqd0 + j - 1)
                                k.inc(s.dma_start(out=h2s[j % 2],
                                                  in_=spill[j, :, tsl]),
                                      ldh2[j % 2], 16)

                        @block.vector
                        def _(v):
                            for j in range(JT):
                                v.wait_ge(ldh2[j % 2],
                                          ldh20[j % 2] + 16 * (j // 2 + 1))
                                if j >= 2:
                                    v.wait_ge(mme, mme0 + j - 1)
                                v.tensor_scalar(out=t1[:, :], in0=h2s[j % 2],
                                                scalar1=s2col[:, 0:1],
                                                scalar2=MAGIC,
                                                op0=ALU.mult, op1=ALU.add)
                                v.drain()
                                v.tensor_scalar(out=t1[:, :], in0=t1[:, :],
                                                scalar1=MAGIC, scalar2=32767.0,
                                                op0=ALU.subtract, op1=ALU.min)
                                v.drain()
                                v.tensor_scalar(out=t1[:, :], in0=t1[:, :],
                                                scalar1=-32768.0,
                                                scalar2=deqc[:, j:j + 1],
                                                op0=ALU.max, op1=ALU.mult)
                                v.drain()
                                v.tensor_copy(out=kfh[j % 2], in_=t1[:, :])
                                v.drain()
                                v.tensor_tensor(out=kfl[j % 2], in0=t1[:, :],
                                                in1=kfh[j % 2],
                                                op=ALU.subtract)
                                v.drain()
                                k.inc(v.nop(), kqd, 1)

                        @block.tensor
                        def _(t):
                            for j in range(JT):
                                t.wait_ge(kqd, kqd0 + j + 1)
                                ins = None
                                for i in range(8):
                                    lhs = m2[:, j * 1024 + i * 128:j * 1024 + (i + 1) * 128]
                                    ps = (psa if i < 4 else psb)
                                    pss = ps[:, (i % 4) * 512:(i % 4) * 512 + 512]
                                    ins = t.matmul(pss, lhs, kfh[j % 2],
                                                   start=(j == 0), stop=False)
                                    ins = t.matmul(pss, lhs, kfl[j % 2],
                                                   start=False,
                                                   stop=(j == JT - 1))
                                k.inc(ins, mme, 1)

                    with nc.Block() as block:
                        @block.vector
                        def _(v):
                            for i in range(8):
                                ps = (psa if i < 4 else psb)
                                pss = ps[:, (i % 4) * 512:(i % 4) * 512 + 512]
                                v.tensor_scalar(out=t1[:, :], in0=pss,
                                                scalar1=boutc[:, i:i + 1],
                                                scalar2=None,
                                                op0=ALU.add, op1=ALU.bypass)
                                v.drain()
                                v.tensor_tensor(out=xt[i][:, tsl],
                                                in0=xt[i][:, tsl],
                                                in1=t1[:, :], op=ALU.add)
                                v.drain()


def amaxr_row_tmp(amaxr):
    return amaxr[0:1, 0:2048]


def emit_ln_absmax(k, xt, lmax_scal, rows4, pst2, pcols, col, col2, pam,
                   sq_bufs, eps_ap=None):
    nc = k.nc
    rowA, rowB, rowC, rowD = rows4
    k.ln_rows(xt, rowA, rowB, rowC, sq_bufs, pcols, eps_ap=eps_ap)
    k.part_reduce(xt, rowC, pst2, pcols, col, col2, pam, mode="max")
    k.part_reduce(xt, rowD, pst2, pcols, col, col2, pam, mode="min")
    with nc.Block() as block:
        @block.vector
        def _(v):
            v.tensor_tensor(out=rowC[:, :], in0=rowC[:, :], in1=rowA[:, :],
                            op=ALU.subtract)
            v.tensor_tensor(out=rowD[:, :], in0=rowA[:, :], in1=rowD[:, :],
                            op=ALU.subtract)
            v.drain()
            v.tensor_tensor(out=rowC[:, :], in0=rowC[:, :], in1=rowD[:, :],
                            op=ALU.max)
            v.drain()
            v.tensor_tensor(out=rowC[:, :], in0=rowC[:, :], in1=rowB[:, :],
                            op=ALU.mult)
            v.drain()
            v.tensor_reduce(out=lmax_scal, in_=rowC[:, :], axis=AX.X,
                            op=ALU.max)
            v.drain()



def build_l1_ln_tohbm(k, xt, dram):
    nc = k.nc
    with contextlib.ExitStack() as es:
        psA_t = es.enter_context(nc.psum_tensor(k.name("a_psA"), [128, 2048], F32))
        psB_t = es.enter_context(nc.psum_tensor(k.name("a_psB"), [128, 2048], F32))
        rowA_t = es.enter_context(nc.sbuf_tensor(k.name("a_rowA"), [1, 2048], F32))
        rowB_t = es.enter_context(nc.sbuf_tensor(k.name("a_rowB"), [1, 2048], F32))
        rowC_t = es.enter_context(nc.sbuf_tensor(k.name("a_rowC"), [1, 2048], F32))
        sq_t = es.enter_context(nc.sbuf_tensor(k.name("a_sq"), [128, 2048], F32))
        snh_t = es.enter_context(nc.sbuf_tensor(k.name("a_snh"), [128, 4096], BF16))
        snl_t = es.enter_context(nc.sbuf_tensor(k.name("a_snl"), [128, 4096], BF16))
        lnw_t = es.enter_context(nc.sbuf_tensor(k.name("a_lnw"), [128, 8], F32))
        lnb_t = es.enter_context(nc.sbuf_tensor(k.name("a_lnb"), [128, 8], F32))
        psa, psb = psA_t.ap(), psB_t.ap()
        rowA, rowB, rowC = rowA_t.ap(), rowB_t.ap(), rowC_t.ap()
        sq = sq_t.ap()
        snh, snl = snh_t.ap(), snl_t.ap()
        lnw, lnb = lnw_t.ap(), lnb_t.ap()

        ldc = k.dsem()
        with nc.Block() as block:
            @block.sync
            def _(s):
                k.inc(s.dma_start(out=lnw[:, :], in_=dram["l1a_lnw"][:, :]), ldc, 16)
                k.inc(s.dma_start(out=lnb[:, :], in_=dram["l1a_lnb"][:, :]), ldc, 16)
                s.wait_ge(ldc, k.v(ldc))

        k.ln_rows(xt, rowA, rowB, rowC, [sq], psa[0:1, :])

        mmb = k.sem()
        evs = k.sem()
        dms = [k.dsem(), k.dsem()]
        mmb0, evs0 = k.v(mmb), k.v(evs)
        dms0 = [k.v(dms[0]), k.v(dms[1])]
        with nc.Block() as block:
            @block.tensor
            def _(t):
                k.bcast_row(t, rowA, psa, 2048)
                k.inc(k.bcast_row(t, rowB, psb, 2048), mmb, 1)

            @block.vector
            def _(v):
                v.wait_ge(mmb, mmb0 + 1)
                for i in range(8):
                    sn = slice((i % 2) * 2048, (i % 2) * 2048 + 2048)
                    if i >= 2:
                        v.wait_ge(dms[i % 2], dms0[i % 2] + 32 * (i // 2))
                    v.tensor_tensor(out=sq[:, :], in0=xt[i][:, :],
                                    in1=psa[:, :], op=ALU.subtract)
                    v.drain()
                    v.tensor_tensor(out=sq[:, :], in0=sq[:, :],
                                    in1=psb[:, :], op=ALU.mult)
                    v.drain()
                    v.tensor_scalar(out=sq[:, :], in0=sq[:, :],
                                    scalar1=lnw[:, i:i + 1],
                                    scalar2=lnb[:, i:i + 1],
                                    op0=ALU.mult, op1=ALU.add)
                    v.drain()
                    v.tensor_copy(out=snh[:, sn], in_=sq[:, :])
                    v.drain()
                    v.tensor_tensor(out=snl[:, sn], in0=sq[:, :],
                                    in1=snh[:, sn], op=ALU.subtract)
                    v.drain()
                    k.inc(v.nop(), evs, 1)

            @block.sync
            def _(s):
                for i in range(8):
                    s.wait_ge(evs, evs0 + i + 1)
                    k.inc(s.dma_start(out=dram["l1a_xnh"][i, :, :],
                                      in_=snh[:, (i % 2) * 2048:(i % 2) * 2048 + 2048]),
                          dms[i % 2], 16)
                    k.inc(s.dma_start(out=dram["l1a_xnl"][i, :, :],
                                      in_=snl[:, (i % 2) * 2048:(i % 2) * 2048 + 2048]),
                          dms[i % 2], 16)
                s.wait_ge(dms[0], k.v(dms[0]))
                s.wait_ge(dms[1], k.v(dms[1]))


def spill_xt(k, xt, xsp):
    nc = k.nc
    spx = k.gsem()
    with nc.Block() as block:
        @block.gpsimd
        def _(g):
            for i in range(8):
                k.inc(g.dma_start(out=xsp[i, :, :], in_=xt[i][:, :]), spx, 16)
            g.wait_ge(spx, k.v(spx))


def load_xn(k, dram, xnh, xnl):
    nc = k.nc
    ldn = k.dsem()
    with nc.Block() as block:
        @block.sync
        def _(s):
            for i in range(8):
                sl = slice(i * 2048, (i + 1) * 2048)
                k.inc(s.dma_start(out=xnh[:, sl], in_=dram["l1a_xnh"][i, :, :]),
                      ldn, 16)
                k.inc(s.dma_start(out=xnl[:, sl], in_=dram["l1a_xnl"][i, :, :]),
                      ldn, 16)
            s.wait_ge(ldn, k.v(ldn))


def build_l1_lact(k, dram, xnh, xnl):
    nc = k.nc
    lactw = dram["l1a_lactw"]

    with contextlib.ExitStack() as es:
        psA_t = es.enter_context(nc.psum_tensor(k.name("a_psA"), [128, 2048], F32))
        psB_t = es.enter_context(nc.psum_tensor(k.name("a_psB"), [128, 2048], F32))
        lw_t = es.enter_context(nc.sbuf_tensor(k.name("a_lw"), [128, 2048], F32))
        pw_t = es.enter_context(nc.sbuf_tensor(k.name("a_pw"), [128, 1536], F32))
        b_gate_t = es.enter_context(nc.sbuf_tensor(k.name("a_gate"), [128, 2048], F32))
        b_sg_t = es.enter_context(nc.sbuf_tensor(k.name("a_sg"), [128, 2048], F32))
        b_g2_t = es.enter_context(nc.sbuf_tensor(k.name("a_g2"), [128, 2048], F32))
        b_dhd_t = es.enter_context(nc.sbuf_tensor(k.name("a_dhd"), [128, 2048], F32))
        b_h__t = es.enter_context(nc.sbuf_tensor(k.name("a_h_"), [128, 2048], F32))
        ctm_t = es.enter_context(nc.sbuf_tensor(k.name("a_ctm"), [128, 2048], F32))
        ttm_t = es.enter_context(nc.sbuf_tensor(k.name("a_ttm"), [128, 2048], F32))
        crec_t = es.enter_context(nc.sbuf_tensor(k.name("a_crec"), [128, 2048], F32))
        psa, psb = psA_t.ap(), psB_t.ap()
        lw, pw = lw_t.ap(), pw_t.ap()
        gate, sg, g2 = b_gate_t.ap(), b_sg_t.ap(), b_g2_t.ap()
        dhd, h_ = b_dhd_t.ap(), b_h__t.ap()
        ctm, ttm, crec = ctm_t.ap(), ttm_t.ap(), crec_t.ap()
        hid = ctm

        def W(m, c):
            return lw[:, (m * 2 + c) * 256:(m * 2 + c + 1) * 256]

        def PW(m, c):
            return pw[:, (m * 2 + c) * 256:(m * 2 + c + 1) * 256]

        for n in range(2):
            for h in range(4):
                _lact_nh(k, dram, xnh, xnl, lactw, n, h, psa, psb, lw, pw,
                         W, PW, gate, sg, g2, dhd, h_, hid, ctm, ttm, crec)


def _lact_nh(k, dram, xnh, xnl, lactw, n, h, psa, psb, lw, pw, W, PW,
             gate, sg, g2, dhd, h_, hid, ctm, ttm, crec):
    nc = k.nc
    tsl = slice(n * 1024, (n + 1) * 1024)

    def cfm(c):
        return crec[:, c * 1024:(c + 1) * 1024]

    ld = k.dsem()
    with nc.Block() as block:
        @block.sync
        def _(s):
            k.inc(s.dma_start(
                out=lw[:, :].rearrange("p (m c e) -> p m c e", m=4, c=2),
                in_=lactw[h].rearrange("m (c p) e -> p m c e", p=128)),
                ld, 16)
            s.wait_ge(ld, k.v(ld))

        @block.vector
        def _(v):
            for c in range(2):
                sl = slice((2 * h + c) * 2048 + n * 1024,
                           (2 * h + c) * 2048 + (n + 1) * 1024)
                v.tensor_tensor(out=crec[:, c * 1024:(c + 1) * 1024],
                                in0=xnh[:, sl], in1=xnl[:, sl], op=ALU.add)
                v.drain()

    mm = k.sem()
    ev = k.sem()
    mm0, ev0 = k.v(mm), k.v(ev)
    with nc.Block() as block:
        @block.tensor
        def _(t):
            for idx, (mat, neg) in enumerate([(0, False), (3, False), (1, True)]):
                ps = psa if idx % 2 == 0 else psb
                if idx == 2:
                    t.wait_ge(ev, ev0 + 1)
                ins = None
                for me in range(2):
                    for c in range(2):
                        for q in range(2):
                            ins = t.matmul(
                                ps[:, me * 1024 + q * 512:me * 1024 + (q + 1) * 512],
                                W(mat, c)[:, me * 128:(me + 1) * 128],
                                cfm(c)[:, q * 512:(q + 1) * 512],
                                start=(c == 0), stop=(c == 1))
                k.inc(ins, mm, 1)

        @block.vector
        def _(v):
            v.wait_ge(mm, mm0 + 1)
            v.tensor_copy(out=gate[:, :], in_=psa[:, :])
            v.drain()
            k.inc(v.nop(), ev, 1)
            v.wait_ge(mm, mm0 + 2)
            v.tensor_copy(out=g2[:, :], in_=psb[:, :])
            v.drain()
            v.wait_ge(mm, mm0 + 3)
            v.tensor_scalar(out=dhd[:, :], in0=psa[:, :], scalar1=-1.0,
                            scalar2=None, op0=ALU.mult, op1=ALU.bypass)
            v.drain()
            k.inc(v.nop(), ev, 1)

    sgd = k.sem()
    sgd0 = k.v(sgd)
    with nc.Block() as block:
        @block.scalar
        def _(s):
            s.activation(out=sg[:, :], in_=gate[:, :], func=AF.Sigmoid)
            s.drain()
            k.inc(s.nop(), sgd, 1)

        @block.vector
        def _(v):
            v.wait_ge(sgd, sgd0 + 1)
            v.tensor_tensor(out=h_[:, :], in0=gate[:, :], in1=sg[:, :],
                            op=ALU.mult)
            v.drain()
            v.tensor_tensor(out=hid[:, :], in0=h_[:, :], in1=g2[:, :],
                            op=ALU.mult)
            v.drain()
            v.tensor_scalar(out=ttm[:, :], in0=sg[:, :], scalar1=-1.0,
                            scalar2=1.0, op0=ALU.mult, op1=ALU.add)
            v.drain()
            v.tensor_tensor(out=ttm[:, :], in0=ttm[:, :], in1=sg[:, :],
                            op=ALU.mult)
            v.drain()
            v.tensor_tensor(out=ttm[:, :], in0=ttm[:, :], in1=gate[:, :],
                            op=ALU.mult)
            v.drain()
            v.tensor_tensor(out=ttm[:, :], in0=ttm[:, :], in1=sg[:, :],
                            op=ALU.add)
            v.drain()
            v.tensor_tensor(out=g2[:, :], in0=dhd[:, :], in1=g2[:, :],
                            op=ALU.mult)
            v.drain()
            v.tensor_tensor(out=dhd[:, :], in0=dhd[:, :], in1=h_[:, :],
                            op=ALU.mult)
            v.drain()
            v.tensor_tensor(out=gate[:, :], in0=g2[:, :], in1=ttm[:, :],
                            op=ALU.mult)
            v.drain()

    def emit_tm(src_fm, dst):
        tr = k.sem()
        rdt = k.sem()
        tr0, rdt0 = k.v(tr), k.v(rdt)
        with nc.Block() as block:
            @block.tensor
            def _(t):
                nn = 0
                for tc in range(8):
                    for c in range(2):
                        if nn >= 2:
                            t.wait_ge(rdt, rdt0 + nn - 1)
                        k.inc(t.transpose(
                            psb[:, (nn % 2) * 512:(nn % 2) * 512 + 128],
                            src_fm(c)[:, tc * 128:(tc + 1) * 128],
                            k.ident[:, :]), tr, 1)
                        nn += 1

            @block.vector
            def _(v):
                nn = 0
                for tc in range(8):
                    for c in range(2):
                        v.wait_ge(tr, tr0 + nn + 1)
                        v.tensor_copy(
                            out=dst[:, tc * 256 + c * 128:tc * 256 + (c + 1) * 128],
                            in_=psb[:, (nn % 2) * 512:(nn % 2) * 512 + 128])
                        v.drain()
                        k.inc(v.nop(), rdt, 1)
                        nn += 1

    def fmv(buf):
        return lambda c: buf[:, c * 1024:(c + 1) * 1024]

    emit_tm(fmv(hid), ttm)
    emit_tm(cfm, ctm)
    _dw_mm(k, psa, ttm, ctm, pw, 2, sign=+1.0, base_w=W(2, 0), base_w2=W(2, 1))
    emit_tm(fmv(gate), ttm)
    _dw_mm(k, psa, ctm, ttm, pw, 0, sign=-1.0, base_w=W(0, 0), base_w2=W(0, 1))
    emit_tm(fmv(dhd), ttm)
    _dw_mm(k, psa, ctm, ttm, pw, 1, sign=-1.0, base_w=W(3, 0), base_w2=W(3, 1))

    mm2 = k.sem()
    ev2 = k.sem()
    sil2 = k.sem()
    mm20, ev20, sil20 = k.v(mm2), k.v(ev2), k.v(sil2)
    with nc.Block() as block:
        @block.tensor
        def _(t):
            for idx, mat in enumerate([0, 1]):
                ps = psa if idx == 0 else psb
                ins = None
                for me in range(2):
                    for c in range(2):
                        for q in range(2):
                            ins = t.matmul(
                                ps[:, me * 1024 + q * 512:me * 1024 + (q + 1) * 512],
                                PW(mat, c)[:, me * 128:(me + 1) * 128],
                                cfm(c)[:, q * 512:(q + 1) * 512],
                                start=(c == 0), stop=(c == 1))
                k.inc(ins, mm2, 1)
            t.wait_ge(ev2, ev20 + 1)
            ins = None
            for md in range(2):
                for c in range(2):
                    for q in range(2):
                        ins = t.matmul(
                            psa[:, md * 1024 + q * 512:md * 1024 + (q + 1) * 512],
                            PW(2, c)[:, md * 128:(md + 1) * 128],
                            h_[:, c * 1024 + q * 512:c * 1024 + (q + 1) * 512],
                            start=(c == 0), stop=(c == 1))
            k.inc(ins, mm2, 1)

        @block.scalar
        def _(s):
            s.wait_ge(mm2, mm20 + 1)
            s.activation(out=sg[:, :], in_=psa[:, :], func=AF.Silu)
            s.drain()
            k.inc(s.nop(), sil2, 1)

        @block.vector
        def _(v):
            v.wait_ge(sil2, sil20 + 1)
            v.wait_ge(mm2, mm20 + 2)
            v.tensor_tensor(out=h_[:, :], in0=sg[:, :], in1=psb[:, :],
                            op=ALU.mult)
            v.drain()
            k.inc(v.nop(), ev2, 1)
            v.wait_ge(mm2, mm20 + 3)
            v.tensor_copy(out=crec[:, :], in_=psa[:, :])
            v.drain()
            k.inc(v.nop(), ev2, 1)

        @block.gpsimd
        def _(g):
            spq = k.gsem()
            g.wait_ge(ev2, k.v(ev2))
            for md in range(2):
                k.inc(g.dma_start(
                    out=dram["l1a_lactsp"][2 * h + md, :, tsl],
                    in_=crec[:, md * 1024:(md + 1) * 1024]), spq, 16)
            g.wait_ge(spq, k.v(spq))


def _dw_mm(k, psa, lhs_tm, rhs_tm, pw, pwi, sign, base_w, base_w2):
    nc = k.nc
    mm = k.sem()
    mm0 = k.v(mm)
    with nc.Block() as block:
        @block.tensor
        def _(t):
            ins = None
            for mc in range(2):
                for tc in range(8):
                    ins = t.matmul(
                        psa[:, mc * 512:mc * 512 + 256],
                        lhs_tm[:, tc * 256 + mc * 128:tc * 256 + (mc + 1) * 128],
                        rhs_tm[:, tc * 256:(tc + 1) * 256],
                        start=(tc == 0), stop=(tc == 7))
            k.inc(ins, mm, 1)

        @block.vector
        def _(v):
            v.wait_ge(mm, mm0 + 1)
            for mc, bw in ((0, base_w), (1, base_w2)):
                if sign > 0:
                    v.tensor_tensor(out=pw[:, (pwi * 2 + mc) * 256:(pwi * 2 + mc + 1) * 256],
                                    in0=bw, in1=psa[:, mc * 512:mc * 512 + 256],
                                    op=ALU.add)
                else:
                    v.tensor_tensor(out=pw[:, (pwi * 2 + mc) * 256:(pwi * 2 + mc + 1) * 256],
                                    in0=bw, in1=psa[:, mc * 512:mc * 512 + 256],
                                    op=ALU.subtract)
                v.drain()


def split_xn(k, xn, xnh, xnl):
    nc = k.nc
    with nc.Block() as block:
        @block.vector
        def _(v):
            for i in range(8):
                sl = slice(i * 2048, (i + 1) * 2048)
                v.tensor_copy(out=xnh[:, sl], in_=xn[:, sl])
                v.drain()
                v.tensor_tensor(out=xnl[:, sl], in0=xn[:, sl],
                                in1=xnh[:, sl], op=ALU.subtract)
                v.drain()


def build_l1_attn_pairs(k, dram, xnh, xnl):
    nc = k.nc
    qkvw = dram["l1a_qkvw"]
    qkvb = dram["l1a_qkvb"]

    with contextlib.ExitStack() as es:
        psA_t = es.enter_context(nc.psum_tensor(k.name("t_psA"), [128, 2048], F32))
        psB_t = es.enter_context(nc.psum_tensor(k.name("t_psB"), [128, 2048], F32))
        ws_t = es.enter_context(nc.sbuf_tensor(k.name("t_ws"), [128, 3072], F32))
        wsh_t = es.enter_context(nc.sbuf_tensor(k.name("t_wsh"), [128, 3072], BF16))
        wsl_t = es.enter_context(nc.sbuf_tensor(k.name("t_wsl"), [128, 3072], BF16))
        khe_t = es.enter_context(nc.sbuf_tensor(k.name("t_khe"), [128, 2048], BF16))
        kle_t = es.enter_context(nc.sbuf_tensor(k.name("t_kle"), [128, 2048], BF16))
        kho_t = es.enter_context(nc.sbuf_tensor(k.name("t_kho"), [128, 2048], BF16))
        klo_t = es.enter_context(nc.sbuf_tensor(k.name("t_klo"), [128, 2048], BF16))
        qhe_t = es.enter_context(nc.sbuf_tensor(k.name("t_qhe"), [128, 2048], BF16))
        qle_t = es.enter_context(nc.sbuf_tensor(k.name("t_qle"), [128, 2048], BF16))
        qho_t = es.enter_context(nc.sbuf_tensor(k.name("t_qho"), [128, 2048], BF16))
        qlo_t = es.enter_context(nc.sbuf_tensor(k.name("t_qlo"), [128, 2048], BF16))
        vhe_t = es.enter_context(nc.sbuf_tensor(k.name("t_vhe"), [128, 16 * 65], F32))
        vho_t = es.enter_context(nc.sbuf_tensor(k.name("t_vho"), [128, 16 * 65], F32))
        tmp_t = es.enter_context(nc.sbuf_tensor(k.name("t_tmp"), [128, 2048], F32))
        vtmp_t = es.enter_context(nc.sbuf_tensor(k.name("t_vtmp"), [128, 2048], F32))
        pb_t = es.enter_context(nc.sbuf_tensor(k.name("t_pb"), [128, 2048], F32))
        pam_t = es.enter_context(nc.sbuf_tensor(k.name("t_pam"), [128, 16], F32))
        bcol_t = es.enter_context(nc.sbuf_tensor(k.name("t_bcol"), [128, 24], F32))
        ones65_t = es.enter_context(nc.sbuf_tensor(k.name("t_o65"), [128, 128], F32))

        psa, psb = psA_t.ap(), psB_t.ap()
        ws, wsh, wsl = ws_t.ap(), wsh_t.ap(), wsl_t.ap()
        khe, kle, kho, klo = khe_t.ap(), kle_t.ap(), kho_t.ap(), klo_t.ap()
        qhe, qle, qho, qlo = qhe_t.ap(), qle_t.ap(), qho_t.ap(), qlo_t.ap()
        vhe, vho = vhe_t.ap(), vho_t.ap()
        tmp, vtmp, pb = tmp_t.ap(), vtmp_t.ap(), pb_t.ap()
        pam, bcol = pam_t.ap(), bcol_t.ap()
        ones65 = ones65_t.ap()

        ldb = k.dsem()
        with nc.Block() as block:
            @block.sync
            def _(s):
                k.inc(s.dma_start(out=bcol[:, :], in_=qkvb[:, :]), ldb, 16)
                s.wait_ge(ldb, k.v(ldb))

            @block.gpsimd
            def _(g):
                g.memset(ones65[:, :], 0.0)
                g.drain()
                g.memset(ones65[64:65, :], 1.0)
                g.memset(khe[64:65, :], 1.0)
                g.memset(kle[64:65, :], 0.0)
                g.memset(kho[64:65, :], 1.0)
                g.memset(klo[64:65, :], 0.0)
                g.memset(vho[:, :], 0.0)
                g.memset(vhe[:, :], 0.0)
                g.drain()
                g.memset(vho[:, :].rearrange("p (t c) -> p t c", c=65)[:, :, 64:65], 1.0)
                g.memset(vhe[:, :].rearrange("p (t c) -> p t c", c=65)[:, :, 64:65], 1.0)
                g.drain()

        for p in range(8):
            _attn_pair(k, p, dram, psa, psb, xnh, xnl, ws, wsh, wsl,
                       khe, kle, kho, klo, qhe, qle, qho, qlo, vhe, vho,
                       tmp, vtmp, pb, pam, bcol, ones65)


def _attn_pair(k, p, dram, psa, psb, xnh, xnl, ws, wsh, wsl,
               khe, kle, kho, klo, qhe, qle, qho, qlo, vhe, vho,
               tmp, vtmp, pb, pam, bcol, ones65):
    nc = k.nc
    qkvw = dram["l1a_qkvw"]

    ldw = k.dsem()
    spl = k.sem()
    ldw0, spl0 = k.v(ldw), k.v(spl)
    with nc.Block() as block:
        @block.sync
        def _(s):
            for hh in range(2):
                k.inc(s.dma_start(
                    out=ws[:, :].rearrange("p (t c h m) -> p t c h m",
                                           t=3, c=8, h=2)[:, :, :, hh, :],
                    in_=qkvw[2 * p + hh].rearrange("t (c p) m -> p t c m",
                                                   p=128)), ldw, 16)

        @block.vector
        def _(v):
            v.wait_ge(ldw, ldw0 + 32)
            v.tensor_copy(out=wsh[:, :], in_=ws[:, :])
            v.drain()
            v.tensor_tensor(out=wsl[:, :], in0=ws[:, :], in1=wsh[:, :],
                            op=ALU.subtract)
            v.drain()

    mmq = k.sem()
    evq = k.sem()
    mmq0, evq0 = k.v(mmq), k.v(evq)
    with nc.Block() as block:
        @block.tensor
        def _(t):
            for ti in range(3):
                ps = psa if ti % 2 == 0 else psb
                if ti == 2:
                    t.wait_ge(evq, evq0 + 1)
                ins = None
                for c in range(8):
                    lh = wsh[:, ti * 1024 + c * 128:ti * 1024 + (c + 1) * 128]
                    ll = wsl[:, ti * 1024 + c * 128:ti * 1024 + (c + 1) * 128]
                    for q in range(NT):
                        xh = xnh[:, c * 2048 + q * 512:c * 2048 + (q + 1) * 512]
                        xl = xnl[:, c * 2048 + q * 512:c * 2048 + (q + 1) * 512]
                        pss = ps[:, q * 512:(q + 1) * 512]
                        ins = t.matmul(pss, lh, xh, start=(c == 0), stop=False)
                        ins = t.matmul(pss, lh, xl, start=False, stop=False)
                        ins = t.matmul(pss, ll, xh, start=False, stop=(c == 7))
                k.inc(ins, mmq, 1)

        @block.vector
        def _(v):
            v.wait_ge(mmq, mmq0 + 1)
            v.tensor_scalar(out=tmp[:, :], in0=psa[:, :],
                            scalar1=bcol[:, 3 * p + 0:3 * p + 1],
                            scalar2=0.125, op0=ALU.add, op1=ALU.mult)
            v.drain()
            v.tensor_copy(out=qhe[0:64, :], in_=tmp[0:64, :])
            v.tensor_copy(out=pb[64:128, :], in_=tmp[64:128, :])
            v.drain()
            v.tensor_tensor(out=qle[0:64, :], in0=tmp[0:64, :],
                            in1=qhe[0:64, :], op=ALU.subtract)
            v.drain()
            k.inc(v.nop(), evq, 1)
            v.wait_ge(mmq, mmq0 + 2)
            v.tensor_scalar(out=tmp[:, :], in0=psb[:, :],
                            scalar1=bcol[:, 3 * p + 1:3 * p + 2],
                            scalar2=None, op0=ALU.add, op1=ALU.bypass)
            v.drain()
            v.tensor_copy(out=khe[0:64, :], in_=tmp[0:64, :])
            v.drain()
            v.tensor_tensor(out=kle[0:64, :], in0=tmp[0:64, :],
                            in1=khe[0:64, :], op=ALU.subtract)
            v.drain()
            v.wait_ge(mmq, mmq0 + 3)
            v.tensor_scalar(out=vtmp[:, :], in0=psa[:, :],
                            scalar1=bcol[:, 3 * p + 2:3 * p + 3],
                            scalar2=None, op0=ALU.add, op1=ALU.bypass)
            v.drain()

    trv = k.sem()
    rdv = k.sem()
    trv0, rdv0 = k.v(trv), k.v(rdv)
    with nc.Block() as block:
        @block.tensor
        def _(t):
            n = 0
            for hh in range(2):
                for kt in range(16):
                    if n >= 2:
                        t.wait_ge(rdv, rdv0 + n - 1)
                    k.inc(t.transpose(
                        psb[:, (n % 2) * 512:(n % 2) * 512 + 64],
                        vtmp[64 * hh:64 * hh + 64, kt * 128:(kt + 1) * 128],
                        k.ident[64 * hh:64 * hh + 64, 64 * hh:64 * hh + 64]),
                        trv, 1)
                    n += 1

        @block.vector
        def _(v):
            n = 0
            for hh in range(2):
                for kt in range(16):
                    v.wait_ge(trv, trv0 + n + 1)
                    if hh == 0:
                        dst = vhe[:, kt * 65:kt * 65 + 64]
                    else:
                        dst = vho[:, kt * 65:kt * 65 + 64]
                    v.tensor_copy(out=dst,
                                  in_=psb[:, (n % 2) * 512:(n % 2) * 512 + 64])
                    v.drain()
                    k.inc(v.nop(), rdv, 1)
                    n += 1

    flp = k.dsem()
    fsp = k.sem()
    fsp0 = k.v(fsp)
    with nc.Block() as block:
        @block.sync
        def _(s):
            k.inc(s.dma_start(out=pb[0:64, :], in_=tmp[64:128, :]), flp, 16)
            k.inc(s.dma_start(out=tmp[0:64, :], in_=pb[64:128, :]), flp, 16)
            s.wait_ge(flp, k.v(flp))

        @block.vector
        def _(v):
            v.wait_ge(flp, k.v(flp))
            v.tensor_copy(out=kho[0:64, :], in_=pb[0:64, :])
            v.tensor_copy(out=qho[0:64, :], in_=tmp[0:64, :])
            v.drain()
            v.tensor_tensor(out=klo[0:64, :], in0=pb[0:64, :],
                            in1=kho[0:64, :], op=ALU.subtract)
            v.tensor_tensor(out=qlo[0:64, :], in0=tmp[0:64, :],
                            in1=qho[0:64, :], op=ALU.subtract)
            v.drain()
            k.inc(v.nop(), fsp, 1)

    for hh in range(2):
        _attn_head(k, p, hh, psa, psb, dram,
                   (khe, kle, qhe, qle, vhe) if hh == 0 else
                   (kho, klo, qho, qlo, vho),
                   tmp, vtmp, pb, pam, ones65)


def _attn_head(k, p, hh, psa, psb, dram, bufs, tmp, vtmp, pb, pam, ones65):
    nc = k.nc
    kh, kl, qh, ql, vh = bufs
    base = 0
    xrow = 64
    kspan = (0, 65)
    vw = 65

    mma = k.sem()
    rda = k.sem()
    mma0, rda0 = k.v(mma), k.v(rda)
    with nc.Block() as block:
        @block.tensor
        def _(t):
            for qt in range(16):
                ps = psa if qt % 2 == 0 else psb
                if qt >= 2:
                    t.wait_ge(rda, rda0 + qt - 1)
                ins = None
                for q in range(NT):
                    ins = t.matmul(ps[:, q * 512:(q + 1) * 512],
                                   qh[base:base + 64, qt * 128:(qt + 1) * 128],
                                   kh[base:base + 64, q * 512:(q + 1) * 512],
                                   start=True, stop=True)
                k.inc(ins, mma, 1)

        @block.vector
        def _(v):
            for qt in range(16):
                ps = psa if qt % 2 == 0 else psb
                v.wait_ge(mma, mma0 + qt + 1)
                v.tensor_reduce(out=pam[:, qt:qt + 1], in_=ps[:, :],
                                axis=AX.X, op=ALU.max)
                v.drain()
                v.tensor_scalar(out=pam[:, qt:qt + 1], in0=pam[:, qt:qt + 1],
                                scalar1=-1.0, scalar2=None,
                                op0=ALU.mult, op1=ALU.bypass)
                v.drain()
                k.inc(v.nop(), rda, 1)

    t2 = k.sem()
    c2 = k.sem()
    t20, c20 = k.v(t2), k.v(c2)
    with nc.Block() as block:
        @block.tensor
        def _(t):
            for s in range(16):
                if s >= 4:
                    t.wait_ge(c2, c20 + s - 3)
                k.inc(t.matmul(
                    psa[xrow:xrow + 1, (s % 4) * 512:(s % 4) * 512 + 128],
                    pam[:, s:s + 1], k.ident[:, :],
                    start=True, stop=True), t2, 1)

        @block.vector
        def _(v):
            for s in range(16):
                v.wait_ge(t2, t20 + s + 1)
                v.tensor_copy(out=vtmp[xrow:xrow + 1, s * 128:(s + 1) * 128],
                              in_=psa[xrow:xrow + 1, (s % 4) * 512:(s % 4) * 512 + 128])
                v.drain()
                k.inc(v.nop(), c2, 1)
            v.tensor_copy(out=qh[xrow:xrow + 1, :], in_=vtmp[xrow:xrow + 1, :])
            v.drain()
            v.tensor_tensor(out=ql[xrow:xrow + 1, :],
                            in0=vtmp[xrow:xrow + 1, :],
                            in1=qh[xrow:xrow + 1, :], op=ALU.subtract)
            v.drain()

    kb, kk = kspan
    mmb = k.sem()
    exd = k.sem()
    pvd = k.sem()
    mmb0, exd0, pvd0 = k.v(mmb), k.v(exd), k.v(pvd)
    ob, oe = 0, 65

    def emit_pv(t, nn):
        kt, qh2 = nn // 2, nn % 2
        t.wait_ge(exd, exd0 + nn + 1)
        ins = None
        for q in range(2):
            pcol = pb[:, (nn % 2) * 1024 + q * 512:(nn % 2) * 1024 + (q + 1) * 512]
            out = psa[ob:oe, qh2 * 1024 + q * 512:qh2 * 1024 + (q + 1) * 512]
            ins = t.matmul(out, vh[:, kt * vw:(kt + 1) * vw], pcol,
                           start=(kt == 0), stop=(kt == 15))
        k.inc(ins, pvd, 1)

    with nc.Block() as block:
        @block.tensor
        def _(t):
            nn = 0
            for kt in range(16):
                for qh2 in range(2):
                    psl = psb[:, (nn % 2) * 1024:(nn % 2) * 1024 + 1024]
                    if nn >= 2:
                        t.wait_ge(exd, exd0 + nn - 1)
                    ins = None
                    for q in range(2):
                        pss = psl[:, q * 512:(q + 1) * 512]
                        qsl = slice(qh2 * 1024 + q * 512, qh2 * 1024 + (q + 1) * 512)
                        ksl = slice(kt * 128, (kt + 1) * 128)
                        ins = t.matmul(pss, kh[kb:kb + kk, ksl],
                                       qh[kb:kb + kk, qsl], start=True, stop=False)
                        ins = t.matmul(pss, kh[kb:kb + kk, ksl],
                                       ql[kb:kb + kk, qsl], start=False, stop=False)
                        ins = t.matmul(pss, kl[kb:kb + kk, ksl],
                                       qh[kb:kb + kk, qsl], start=False, stop=True)
                    k.inc(ins, mmb, 1)
                    if nn >= 1:
                        emit_pv(t, nn - 1)
                    nn += 1
            emit_pv(t, 31)

        @block.scalar
        def _(s):
            nn = 0
            for kt in range(16):
                for qh2 in range(2):
                    psl = psb[:, (nn % 2) * 1024:(nn % 2) * 1024 + 1024]
                    s.wait_ge(mmb, mmb0 + nn + 1)
                    if nn >= 2:
                        s.wait_ge(pvd, pvd0 + nn - 1)
                    s.activation(out=pb[:, (nn % 2) * 1024:(nn % 2) * 1024 + 1024],
                                 in_=psl, func=AF.Exp)
                    s.drain()
                    k.inc(s.nop(), exd, 1)
                    nn += 1

    srow = 64
    drows = (0, 64)
    bcd = k.sem()
    aod = k.dsem()
    bcd0 = k.v(bcd)
    with nc.Block() as block:
        @block.vector
        def _(v):
            v.reciprocal(out=vtmp[srow:srow + 1, :], in_=psa[srow:srow + 1, :])
            v.drain()
            k.inc(v.nop(), bcd, 1)

        @block.tensor
        def _(t):
            t.wait_ge(bcd, bcd0 + 1)
            ins = None
            for q in range(NT):
                ins = t.matmul(psb[:, q * 512:(q + 1) * 512],
                               ones65[srow:srow + 1, :],
                               vtmp[srow:srow + 1, q * 512:(q + 1) * 512],
                               start=True, stop=True)
            k.inc(ins, bcd, 1)

        @block.vector
        def _(v):
            v.wait_ge(bcd, bcd0 + 2)
            d0, d1 = drows
            v.tensor_copy(out=tmp[d0:d1, :], in_=psa[d0:d1, :])
            v.drain()
            v.tensor_tensor(out=tmp[d0:d1, :], in0=tmp[d0:d1, :],
                            in1=psb[d0:d1, :], op=ALU.mult)
            v.drain()
            k.inc(v.nop(), bcd, 1)

        @block.sync
        def _(s):
            s.wait_ge(bcd, bcd0 + 3)
            k.inc(s.dma_start(out=dram["l1a_aosp"][p, 64 * hh:64 * hh + 64, :],
                              in_=tmp[0:64, :]), aod, 16)
            s.wait_ge(aod, k.v(aod))


def build_l1_attn_outproj(k, xt, dram):
    nc = k.nc
    wot = dram["l1a_wot"]
    wob = dram["l1a_wob"]
    xsp = dram["l1a_xspill"]
    aosp = dram["l1a_aosp"]
    lactsp = dram["l1a_lactsp"]

    with contextlib.ExitStack() as es:
        psA_t = es.enter_context(nc.psum_tensor(k.name("o_psA"), [128, 2048], F32))
        psB_t = es.enter_context(nc.psum_tensor(k.name("o_psB"), [128, 2048], F32))
        aoh_t = es.enter_context(nc.sbuf_tensor(k.name("o_aoh"), [128, 16384], BF16))
        aol_t = es.enter_context(nc.sbuf_tensor(k.name("o_aol"), [128, 16384], BF16))
        wblk_t = es.enter_context(nc.sbuf_tensor(k.name("o_wblk"), [128, 2048], F32))
        wbh_t = es.enter_context(nc.sbuf_tensor(k.name("o_wbh"), [128, 2048], BF16))
        wbl_t = es.enter_context(nc.sbuf_tensor(k.name("o_wbl"), [128, 2048], BF16))
        wbc_t = es.enter_context(nc.sbuf_tensor(k.name("o_wbc"), [128, 8], F32))
        tmp_t = es.enter_context(nc.sbuf_tensor(k.name("o_tmp"), [128, 2048], F32))
        lbA_t = es.enter_context(nc.sbuf_tensor(k.name("o_lbA"), [128, 2048], F32))
        lbB_t = es.enter_context(nc.sbuf_tensor(k.name("o_lbB"), [128, 2048], F32))
        psa, psb = psA_t.ap(), psB_t.ap()
        aoh, aol = aoh_t.ap(), aol_t.ap()
        wblk, wbh, wbl = wblk_t.ap(), wbh_t.ap(), wbl_t.ap()
        wbc, tmp = wbc_t.ap(), tmp_t.ap()
        lbA, lbB = lbA_t.ap(), lbB_t.ap()

        lda = [k.dsem(), k.dsem()]
        spa = k.sem()
        lda0 = [k.v(lda[0]), k.v(lda[1])]
        spa0 = k.v(spa)
        stg = [tmp, lbA]
        with nc.Block() as block:
            @block.sync
            def _(s):
                for i in range(8):
                    if i >= 2:
                        s.wait_ge(spa, spa0 + i - 1)
                    k.inc(s.dma_start(out=stg[i % 2][:, :], in_=aosp[i, :, :]),
                          lda[i % 2], 16)
                k.inc(s.dma_start(out=wbc[:, :], in_=wob[:, :]), lda[0], 16)
                s.wait_ge(lda[0], k.v(lda[0]))
                s.wait_ge(lda[1], k.v(lda[1]))

            @block.vector
            def _(v):
                for i in range(8):
                    blk = slice(i * 2048, (i + 1) * 2048)
                    v.wait_ge(lda[i % 2], lda0[i % 2] + 16 * (i // 2 + 1))
                    v.tensor_copy(out=aoh[:, blk], in_=stg[i % 2][:, :])
                    v.drain()
                    v.tensor_tensor(out=aol[:, blk], in0=stg[i % 2][:, :],
                                    in1=aoh[:, blk], op=ALU.subtract)
                    v.drain()
                    k.inc(v.nop(), spa, 1)

        ldw = [k.dsem(), k.dsem()]
        ldx = k.dsem()
        ldl = [k.gsem(), k.gsem()]
        spw = k.sem()
        mmo = k.sem()
        evo = k.sem()
        ldw0 = [k.v(ldw[0]), k.v(ldw[1])]
        ldl0 = [k.v(ldl[0]), k.v(ldl[1])]
        ldx0, spw0, mmo0, evo0 = k.v(ldx), k.v(spw), k.v(mmo), k.v(evo)
        lslot = [lbA, lbB]
        with nc.Block() as block:
            @block.sync
            def _(s):
                for dt in range(8):
                    k.inc(s.dma_start(out=xt[dt][:, :], in_=xsp[dt, :, :]),
                          ldx, 16)
                    if dt >= 2:
                        s.wait_ge(spw, spw0 + dt - 1)
                    k.inc(s.dma_start(
                        out=wblk[:, (dt % 2) * 1024:(dt % 2) * 1024 + 1024]
                            .rearrange("p (c m) -> p c m", c=8),
                        in_=wot[:, dt * 128:(dt + 1) * 128]
                            .rearrange("(c p) m -> p c m", p=128)),
                        ldw[dt % 2], 16)

            @block.gpsimd
            def _(g):
                for dt in range(8):
                    if dt >= 2:
                        g.wait_ge(evo, evo0 + dt - 1)
                    k.inc(g.dma_start(out=lslot[dt % 2][:, :],
                                      in_=lactsp[dt, :, :]), ldl[dt % 2], 16)

            @block.tensor
            def _(t):
                for dt in range(8):
                    ps = psa if dt % 2 == 0 else psb
                    t.wait_ge(spw, spw0 + dt + 1)
                    if dt >= 2:
                        t.wait_ge(evo, evo0 + dt - 1)
                    ins = None
                    for c in range(8):
                        lh = wbh[:, (dt % 2) * 1024 + c * 128:
                                 (dt % 2) * 1024 + (c + 1) * 128]
                        ll = wbl[:, (dt % 2) * 1024 + c * 128:
                                 (dt % 2) * 1024 + (c + 1) * 128]
                        for q in range(NT):
                            pss = ps[:, q * 512:(q + 1) * 512]
                            ah = aoh[:, c * 2048 + q * 512:c * 2048 + (q + 1) * 512]
                            al = aol[:, c * 2048 + q * 512:c * 2048 + (q + 1) * 512]
                            ins = t.matmul(pss, lh, ah, start=(c == 0), stop=False)
                            ins = t.matmul(pss, lh, al, start=False, stop=False)
                            ins = t.matmul(pss, ll, ah, start=False, stop=(c == 7))
                    k.inc(ins, mmo, 1)

            @block.vector
            def _(v):
                for dt in range(8):
                    sl = slice((dt % 2) * 1024, (dt % 2) * 1024 + 1024)
                    v.wait_ge(ldw[dt % 2], ldw0[dt % 2] + 16 * (dt // 2 + 1))
                    if dt >= 2:
                        v.wait_ge(mmo, mmo0 + dt - 1)
                    v.tensor_copy(out=wbh[:, sl], in_=wblk[:, sl])
                    v.drain()
                    v.tensor_tensor(out=wbl[:, sl], in0=wblk[:, sl],
                                    in1=wbh[:, sl], op=ALU.subtract)
                    v.drain()
                    k.inc(v.nop(), spw, 1)
                    if dt >= 1:
                        _outproj_evac(k, v, dt - 1, xt, psa, psb, tmp, wbc,
                                      lslot, mmo, mmo0, ldx, ldx0, ldl, ldl0,
                                      evo)
                _outproj_evac(k, v, 7, xt, psa, psb, tmp, wbc, lslot,
                              mmo, mmo0, ldx, ldx0, ldl, ldl0, evo)


def _outproj_evac(k, v, dt, xt, psa, psb, tmp, wbc, lslot,
                  mmo, mmo0, ldx, ldx0, ldl, ldl0, evo):
    ps = psa if dt % 2 == 0 else psb
    v.wait_ge(mmo, mmo0 + dt + 1)
    v.wait_ge(ldx, ldx0 + 16 * (dt + 1))
    v.wait_ge(ldl[dt % 2], ldl0[dt % 2] + 16 * (dt // 2 + 1))
    v.tensor_scalar(out=tmp[:, :], in0=ps[:, :],
                    scalar1=wbc[:, dt:dt + 1], scalar2=None,
                    op0=ALU.add, op1=ALU.bypass)
    v.drain()
    v.tensor_tensor(out=tmp[:, :], in0=tmp[:, :],
                    in1=lslot[dt % 2][:, :], op=ALU.add)
    v.drain()
    v.tensor_tensor(out=xt[dt][:, :], in0=xt[dt][:, :],
                    in1=tmp[:, :], op=ALU.add)
    v.drain()
    k.inc(v.nop(), evo, 1)



def l1_ffn_dram(nc):
    d = {}
    d["l1f_w1T"] = nc.dram_tensor("l1f_w1T", [1024, 4096], F32,
                                  kind="ExternalInput").ap()
    d["l1f_w2T"] = nc.dram_tensor("l1f_w2T", [4096, 1024], F32,
                                  kind="ExternalInput").ap()
    d["l1f_convw"] = nc.dram_tensor("l1f_convw", [128, 96], F32,
                                    kind="ExternalInput").ap()
    d["l1f_convb"] = nc.dram_tensor("l1f_convb", [128, 32], F32,
                                    kind="ExternalInput").ap()
    d["l1f_spill"] = nc.dram_tensor("l1f_spill", [32, 128, 2048], F32,
                                    kind="Internal").ap()
    return d


def l1_ffn_inputs(w1, conv_w, conv_b, w2):
    cw = np.ascontiguousarray(
        conv_w[:, 0, :].reshape(32, 128, 3).transpose(1, 0, 2).reshape(128, 96))
    cb = np.ascontiguousarray(conv_b.reshape(32, 128).T)
    return {
        "l1f_w1T": np.ascontiguousarray(w1.T),
        "l1f_w2T": np.ascontiguousarray(w2.T),
        "l1f_convw": cw,
        "l1f_convb": cb,
    }


def x_fm_layout(xb):
    return np.ascontiguousarray(xb.T.reshape(8, 128, 2048))


def x_fm_unlayout(xfm):
    return np.ascontiguousarray(xfm.reshape(1024, 2048).T)


def l0_ffn_dram(nc, upto="full"):
    d = {}
    d["l0f_win"] = nc.dram_tensor("l0f_win", [1024, 4096], F32,
                                  kind="ExternalInput").ap()
    d["l0f_binc"] = nc.dram_tensor("l0f_binc", [128, 32], F32,
                                   kind="ExternalInput").ap()
    d["l0f_gmax1"] = nc.dram_tensor("l0f_gmax1", [1, 8], F32,
                                    kind="ExternalInput").ap()
    d["l0f_spill"] = nc.dram_tensor("l0f_spill", [32, 128, 2048], F32,
                                    kind="Internal").ap()
    if upto == "full":
        d["l0f_wout"] = nc.dram_tensor("l0f_wout", [4096, 1024], F32,
                                       kind="ExternalInput").ap()
        d["l0f_boutc"] = nc.dram_tensor("l0f_boutc", [128, 8], F32,
                                        kind="ExternalInput").ap()
        d["l0f_gmax2"] = nc.dram_tensor("l0f_gmax2", [1, 8], F32,
                                        kind="ExternalInput").ap()
    else:
        d["l0f_lmax"] = nc.dram_tensor("l0f_lmax", [1, 4], F32,
                                       kind="ExternalOutput").ap()
    return d


def l0_ffn_inputs(W_in, b_in, W_out, b_out, upto="full"):
    d = {
        "l0f_win": np.ascontiguousarray(W_in),
        "l0f_binc": np.ascontiguousarray(b_in.reshape(32, 128).T),
    }
    if upto == "full":
        d["l0f_wout"] = np.ascontiguousarray(W_out)
        d["l0f_boutc"] = np.ascontiguousarray(b_out.reshape(8, 128).T)
    return d


def build_test_l0_ffn(upto="full"):
    nc = bass.Bass("TRN2", target_bir_lowering=False)
    x_in = nc.dram_tensor("x_in", [8, 128, 2048], F32, kind="ExternalInput").ap()
    x_out = nc.dram_tensor("x_out", [8, 128, 2048], F32, kind="ExternalOutput").ap()
    dram = l0_ffn_dram(nc, upto)
    k = K(nc)
    k.init_common()
    xt_t = [nc.alloc_sbuf_tensor(f"xt{i}", [128, 2048], F32) for i in range(8)]
    xt = [t.ap() for t in xt_t]
    k.load_x_fm(x_in, xt)
    build_l0_ffn(k, xt, dram, upto=upto)
    k.store_x_fm(x_out, xt)
    mybir.codegen_inst_isa_subclasses(nc)
    return nc


def l1_lact_dram(nc):
    d = {}
    d["l1a_lnw"] = nc.dram_tensor("l1a_lnw", [128, 8], F32, kind="ExternalInput").ap()
    d["l1a_lnb"] = nc.dram_tensor("l1a_lnb", [128, 8], F32, kind="ExternalInput").ap()
    d["l1a_lactw"] = nc.dram_tensor("l1a_lactw", [4, 4, 256, 256], F32,
                                    kind="ExternalInput").ap()
    d["l1a_xnh"] = nc.dram_tensor("l1a_xnh", [8, 128, 2048], BF16,
                                  kind="Internal").ap()
    d["l1a_xnl"] = nc.dram_tensor("l1a_xnl", [8, 128, 2048], BF16,
                                  kind="Internal").ap()
    return d


def l1_lact_inputs(ln_w, ln_b, w0, w1, w2):
    lact = np.stack([
        np.stack([w0[h].T, w1[h], w1[h].T, w2[h].T], axis=0) for h in range(4)
    ], axis=0)
    return {
        "l1a_lnw": np.ascontiguousarray(ln_w.reshape(8, 128).T),
        "l1a_lnb": np.ascontiguousarray(ln_b.reshape(8, 128).T),
        "l1a_lactw": np.ascontiguousarray(lact),
    }


def build_test_l1_lact():
    nc = bass.Bass("TRN2", target_bir_lowering=False)
    x_in = nc.dram_tensor("x_in", [8, 128, 2048], F32, kind="ExternalInput").ap()
    lact_out = nc.dram_tensor("lact_out", [8, 128, 2048], F32,
                              kind="ExternalOutput").ap()
    xn_out = nc.dram_tensor("xn_out", [8, 128, 2048], F32, kind="ExternalOutput").ap()
    dram = l1_lact_dram(nc)
    dram["l1a_lactsp"] = lact_out
    k = K(nc)
    k.init_common()
    with contextlib.ExitStack() as s1:
        xt = [s1.enter_context(nc.sbuf_tensor(k.name("xt"), [128, 2048], F32)).ap()
              for _ in range(8)]
        k.load_x_fm(x_in, xt)
        build_l1_ln_tohbm(k, xt, dram)
    with contextlib.ExitStack() as s2:
        xnh = s2.enter_context(nc.sbuf_tensor("xnh_all", [128, 16384], BF16)).ap()
        xnl = s2.enter_context(nc.sbuf_tensor("xnl_all", [128, 16384], BF16)).ap()
        load_xn(k, dram, xnh, xnl)
        build_l1_lact(k, dram, xnh, xnl)
        tmpb = s2.enter_context(nc.sbuf_tensor(k.name("tb"), [128, 2048], F32)).ap()
        st = k.gsem()
        ev = k.sem()
        ev0, st0 = k.v(ev), k.v(st)
        with nc.Block() as block:
            @block.vector
            def _(v):
                for i in range(8):
                    sl = slice(i * 2048, (i + 1) * 2048)
                    v.wait_ge(st, st0 + 16 * i)
                    v.tensor_tensor(out=tmpb[:, :], in0=xnh[:, sl],
                                    in1=xnl[:, sl], op=ALU.add)
                    v.drain()
                    k.inc(v.nop(), ev, 1)

            @block.gpsimd
            def _(g):
                for i in range(8):
                    g.wait_ge(ev, ev0 + i + 1)
                    k.inc(g.dma_start(out=xn_out[i, :, :], in_=tmpb[:, :]), st, 16)
                g.wait_ge(st, k.v(st))
    mybir.codegen_inst_isa_subclasses(nc)
    return nc


def l1_attn_dram(nc):
    d = l1_lact_dram(nc)
    d["l1a_qkvw"] = nc.dram_tensor("l1a_qkvw", [16, 3, 1024, 64], F32,
                                   kind="ExternalInput").ap()
    d["l1a_qkvb"] = nc.dram_tensor("l1a_qkvb", [128, 24], F32,
                                   kind="ExternalInput").ap()
    d["l1a_wot"] = nc.dram_tensor("l1a_wot", [1024, 1024], F32,
                                  kind="ExternalInput").ap()
    d["l1a_wob"] = nc.dram_tensor("l1a_wob", [128, 8], F32,
                                  kind="ExternalInput").ap()
    d["l1a_xspill"] = nc.dram_tensor("l1a_xspill", [8, 128, 2048], F32,
                                     kind="Internal").ap()
    d["l1a_lactsp"] = nc.dram_tensor("l1a_lactsp", [8, 128, 2048], F32,
                                     kind="Internal").ap()
    d["l1a_aosp"] = nc.dram_tensor("l1a_aosp", [8, 128, 2048], F32,
                                   kind="Internal").ap()
    return d


def l1_attn_inputs(ln_w, ln_b, in_proj_w, in_proj_b, out_proj_w, out_proj_b,
                   w0, w1, w2):
    d = l1_lact_inputs(ln_w, ln_b, w0, w1, w2)
    qkvw = np.zeros((16, 3, 1024, 64), np.float32)
    qkvb = np.zeros((128, 24), np.float32)
    for h in range(16):
        for t in range(3):
            rows = in_proj_w[t * 1024 + 64 * h:t * 1024 + 64 * (h + 1)]
            qkvw[h, t] = rows.T
    for p in range(8):
        for t in range(3):
            qkvb[0:64, 3 * p + t] = in_proj_b[t * 1024 + 64 * 2 * p:
                                              t * 1024 + 64 * (2 * p + 1)]
            qkvb[64:128, 3 * p + t] = in_proj_b[t * 1024 + 64 * (2 * p + 1):
                                                t * 1024 + 64 * (2 * p + 2)]
    d["l1a_qkvw"] = np.ascontiguousarray(qkvw)
    d["l1a_qkvb"] = qkvb
    d["l1a_wot"] = np.ascontiguousarray(out_proj_w.T)
    d["l1a_wob"] = np.ascontiguousarray(out_proj_b.reshape(8, 128).T)
    return d


def build_test_l1_attn():
    nc = bass.Bass("TRN2", target_bir_lowering=False)
    x_in = nc.dram_tensor("x_in", [8, 128, 2048], F32, kind="ExternalInput").ap()
    x_out = nc.dram_tensor("x_out", [8, 128, 2048], F32, kind="ExternalOutput").ap()
    dram = l1_attn_dram(nc)
    k = K(nc)
    k.init_common()
    with contextlib.ExitStack() as s1:
        xt = [s1.enter_context(nc.sbuf_tensor(k.name("xt"), [128, 2048], F32)).ap()
              for _ in range(8)]
        k.load_x_fm(x_in, xt)
        build_l1_ln_tohbm(k, xt, dram)
        spill_xt(k, xt, dram["l1a_xspill"])
    with contextlib.ExitStack() as s2:
        xnh = s2.enter_context(nc.sbuf_tensor("xnh_all", [128, 16384], BF16)).ap()
        xnl = s2.enter_context(nc.sbuf_tensor("xnl_all", [128, 16384], BF16)).ap()
        load_xn(k, dram, xnh, xnl)
        build_l1_lact(k, dram, xnh, xnl)
        build_l1_attn_pairs(k, dram, xnh, xnl)
    with contextlib.ExitStack() as s3:
        xt = [s3.enter_context(nc.sbuf_tensor(k.name("xt"), [128, 2048], F32)).ap()
              for _ in range(8)]
        build_l1_attn_outproj(k, xt, dram)
        k.store_x_fm(x_out, xt)
    mybir.codegen_inst_isa_subclasses(nc)
    return nc


def build_test_l1_ffn():
    nc = bass.Bass("TRN2", target_bir_lowering=False)
    x_in = nc.dram_tensor("x_in", [8, 128, 2048], F32, kind="ExternalInput").ap()
    x_out = nc.dram_tensor("x_out", [8, 128, 2048], F32, kind="ExternalOutput").ap()
    dram = l1_ffn_dram(nc)
    k = K(nc)
    k.init_common()
    xt_t = [nc.alloc_sbuf_tensor(f"xt{i}", [128, 2048], F32) for i in range(8)]
    xt = [t.ap() for t in xt_t]
    k.load_x_fm(x_in, xt)
    build_l1_ffn(k, xt, dram)
    k.store_x_fm(x_out, xt)
    mybir.codegen_inst_isa_subclasses(nc)
    return nc


def l0_attn_dram(nc):
    d = {}
    for nm in ("wqT", "wkT", "wvT", "wgT", "woT"):
        d["l0a_" + nm] = nc.dram_tensor("l0a_" + nm, [1024, 1024], F32,
                                        kind="ExternalInput").ap()
    d["l0a_lrwT"] = nc.dram_tensor("l0a_lrwT", [1024, 16], F32,
                                   kind="ExternalInput").ap()
    d["l0a_lrb"] = nc.dram_tensor("l0a_lrb", [1, 16], F32,
                                  kind="ExternalInput").ap()
    d["l0a_tidxb"] = nc.dram_tensor("l0a_tidxb", [1, 16], F32,
                                    kind="ExternalInput").ap()
    d["l0a_ltidx"] = nc.dram_tensor("l0a_ltidx", [1, 16], F32,
                                    kind="ExternalInput").ap()
    d["l0a_lnwq"] = nc.dram_tensor("l0a_lnwq", [128, 512], F32,
                                   kind="ExternalInput").ap()
    d["l0a_lnbq"] = nc.dram_tensor("l0a_lnbq", [128, 512], F32,
                                   kind="ExternalInput").ap()
    d["l0a_trilT"] = nc.dram_tensor("l0a_trilT", [128, 16], F32,
                                    kind="ExternalInput").ap()
    d["l0a_trilQ"] = nc.dram_tensor("l0a_trilQ", [128, 16], F32,
                                    kind="ExternalInput").ap()
    d["l0a_w1a"] = nc.dram_tensor("l0a_w1a", [65, 4096], F32,
                                  kind="ExternalInput").ap()
    d["l0a_w2a"] = nc.dram_tensor("l0a_w2a", [128, 2560], F32,
                                  kind="ExternalInput").ap()
    d["l0a_w2t"] = nc.dram_tensor("l0a_w2t", [64, 4096], F32,
                                  kind="ExternalInput").ap()
    d["l0a_b2r"] = nc.dram_tensor("l0a_b2r", [1, 1280], F32,
                                  kind="ExternalInput").ap()
    d["l0a_postw"] = nc.dram_tensor("l0a_postw", [128, 8], F32,
                                    kind="ExternalInput").ap()
    d["l0a_postb"] = nc.dram_tensor("l0a_postb", [128, 8], F32,
                                    kind="ExternalInput").ap()
    d["l0a_xnh"] = nc.dram_tensor("l0a_xnh", [8, 128, 2048], BF16,
                                  kind="Internal").ap()
    d["l0a_xnl"] = nc.dram_tensor("l0a_xnl", [8, 128, 2048], BF16,
                                  kind="Internal").ap()
    for nm in ("xq", "xk", "xvd", "xqw"):
        d["l0a_" + nm] = nc.dram_tensor("l0a_" + nm, [8, 128, 2048], F32,
                                        kind="Internal").ap()
    d["l0a_lr"] = nc.dram_tensor("l0a_lr", [16, 2048], F32,
                                 kind="Internal").ap()
    return d


def l0_attn_inputs(Wq, Wk, Wv, Wo, Wg, lr_w, lr_b, ltidx, norm_w, norm_b,
                   W1, b1, W2, b2, post_w, post_b):
    d = {}
    d["l0a_wqT"] = np.ascontiguousarray(Wq.T)
    d["l0a_wkT"] = np.ascontiguousarray(Wk.T)
    d["l0a_wvT"] = np.ascontiguousarray(Wv.T)
    d["l0a_wgT"] = np.ascontiguousarray(Wg.T)
    d["l0a_woT"] = np.ascontiguousarray(Wo.T)
    d["l0a_lrwT"] = np.ascontiguousarray(lr_w[:, 0, :].T)
    d["l0a_lrb"] = np.asarray(lr_b).reshape(1, 16)
    d["l0a_tidxb"] = (1.0 / np.arange(1, 17, dtype=np.float64)).astype(
        np.float32).reshape(1, 16)
    d["l0a_ltidx"] = np.asarray(ltidx).reshape(1, 16)
    lnwq = np.zeros((128, 512), np.float32)
    lnbq = np.zeros((128, 512), np.float32)
    trilT = np.zeros((128, 16), np.float32)
    trilQ = np.zeros((128, 16), np.float32)
    for j in range(2):
        for t in range(16):
            for dd in range(8):
                lnwq[64 * j + t, 64 * dd:64 * dd + 64] = norm_w[2 * dd + j]
                lnbq[64 * j + t, 64 * dd:64 * dd + 64] = norm_b[2 * dd + j]
            trilT[64 * j + t, t:] = 1.0
            trilQ[64 * j + t, :t + 1] = 1.0
    d["l0a_lnwq"] = lnwq
    d["l0a_lnbq"] = lnbq
    d["l0a_trilT"] = trilT
    d["l0a_trilQ"] = trilQ
    w1a = np.zeros((65, 4096), np.float32)
    w2a = np.zeros((128, 2560), np.float32)
    w2t = np.zeros((64, 4096), np.float32)
    b2r = np.zeros((1, 1280), np.float32)
    for h in range(16):
        w1a[0:64, h * 256:(h + 1) * 256] = W1[h]
        w1a[64, h * 256:(h + 1) * 256] = b1[h, 0]
        for c in range(2):
            w2a[:, (2 * h + c) * 80:(2 * h + c) * 80 + 64] = \
                W2[h, 128 * c:128 * (c + 1), :]
        w2t[:, h * 256:(h + 1) * 256] = W2[h].T
        b2r[0, h * 80:h * 80 + 64] = b2[h, 0]
    d["l0a_w1a"] = w1a
    d["l0a_w2a"] = w2a
    d["l0a_w2t"] = w2t
    d["l0a_b2r"] = b2r
    d["l0a_postw"] = np.ascontiguousarray(post_w.reshape(8, 128).T)
    d["l0a_postb"] = np.ascontiguousarray(post_b.reshape(8, 128).T)
    return d


def build_l0_ln_tohbm(k, xt, dram):
    nc = k.nc
    with contextlib.ExitStack() as es:
        psA_t = es.enter_context(nc.psum_tensor(k.name("n_psA"), [128, 2048], F32))
        psB_t = es.enter_context(nc.psum_tensor(k.name("n_psB"), [128, 2048], F32))
        rowA_t = es.enter_context(nc.sbuf_tensor(k.name("n_rowA"), [1, 2048], F32))
        rowB_t = es.enter_context(nc.sbuf_tensor(k.name("n_rowB"), [1, 2048], F32))
        rowC_t = es.enter_context(nc.sbuf_tensor(k.name("n_rowC"), [1, 2048], F32))
        sq_t = es.enter_context(nc.sbuf_tensor(k.name("n_sq"), [128, 2048], F32))
        snh_t = es.enter_context(nc.sbuf_tensor(k.name("n_snh"), [128, 4096], BF16))
        snl_t = es.enter_context(nc.sbuf_tensor(k.name("n_snl"), [128, 4096], BF16))
        psa, psb = psA_t.ap(), psB_t.ap()
        rowA, rowB, rowC = rowA_t.ap(), rowB_t.ap(), rowC_t.ap()
        sq = sq_t.ap()
        snh, snl = snh_t.ap(), snl_t.ap()

        k.ln_rows(xt, rowA, rowB, rowC, [sq], psa[0:1, :])

        mmb = k.sem()
        evs = k.sem()
        dms = [k.dsem(), k.dsem()]
        mmb0, evs0 = k.v(mmb), k.v(evs)
        dms0 = [k.v(dms[0]), k.v(dms[1])]
        with nc.Block() as block:
            @block.tensor
            def _(t):
                k.bcast_row(t, rowA, psa, 2048)
                k.inc(k.bcast_row(t, rowB, psb, 2048), mmb, 1)

            @block.vector
            def _(v):
                v.wait_ge(mmb, mmb0 + 1)
                for i in range(8):
                    sn = slice((i % 2) * 2048, (i % 2) * 2048 + 2048)
                    if i >= 2:
                        v.wait_ge(dms[i % 2], dms0[i % 2] + 32 * (i // 2))
                    v.tensor_tensor(out=sq[:, :], in0=xt[i][:, :],
                                    in1=psa[:, :], op=ALU.subtract)
                    v.drain()
                    v.tensor_tensor(out=sq[:, :], in0=sq[:, :],
                                    in1=psb[:, :], op=ALU.mult)
                    v.drain()
                    v.tensor_copy(out=snh[:, sn], in_=sq[:, :])
                    v.drain()
                    v.tensor_tensor(out=snl[:, sn], in0=sq[:, :],
                                    in1=snh[:, sn], op=ALU.subtract)
                    v.drain()
                    k.inc(v.nop(), evs, 1)

            @block.sync
            def _(s):
                for i in range(8):
                    s.wait_ge(evs, evs0 + i + 1)
                    k.inc(s.dma_start(out=dram["l0a_xnh"][i, :, :],
                                      in_=snh[:, (i % 2) * 2048:(i % 2) * 2048 + 2048]),
                          dms[i % 2], 16)
                    k.inc(s.dma_start(out=dram["l0a_xnl"][i, :, :],
                                      in_=snl[:, (i % 2) * 2048:(i % 2) * 2048 + 2048]),
                          dms[i % 2], 16)
                s.wait_ge(dms[0], k.v(dms[0]))
                s.wait_ge(dms[1], k.v(dms[1]))


def _l0_proj_one(k, dram, xnh, xnl, wkey, outkey, sub_xk, psa, psb,
                 wblk, wbh, wbl, stg, xkbuf):
    nc = k.nc
    wsrc = dram[wkey]
    dst = dram[outkey]
    ldw = [k.dsem(), k.dsem()]
    spw = k.sem()
    mmo = k.sem()
    evo = k.sem()
    sto = [k.gsem(), k.gsem()]
    ldw0 = [k.v(ldw[0]), k.v(ldw[1])]
    spw0, mmo0, evo0 = k.v(spw), k.v(mmo), k.v(evo)
    sto0 = [k.v(sto[0]), k.v(sto[1])]
    with nc.Block() as block:
        @block.sync
        def _(s):
            for wc in range(8):
                if wc >= 2:
                    s.wait_ge(spw, spw0 + wc - 1)
                k.inc(s.dma_start(
                    out=wblk[:, (wc % 2) * 1024:(wc % 2) * 1024 + 1024]
                        .rearrange("p (c m) -> p c m", c=8),
                    in_=wsrc[:, wc * 128:(wc + 1) * 128]
                        .rearrange("(c p) m -> p c m", p=128)),
                    ldw[wc % 2], 16)

        @block.gpsimd
        def _(g):
            for wc in range(8):
                g.wait_ge(evo, evo0 + wc + 1)
                k.inc(g.dma_start(out=dst[wc, :, :],
                                  in_=stg[:, (wc % 2) * 2048:(wc % 2) * 2048 + 2048]),
                      sto[wc % 2], 16)
            g.wait_ge(sto[0], k.v(sto[0]))
            g.wait_ge(sto[1], k.v(sto[1]))

        @block.tensor
        def _(t):
            for wc in range(8):
                ps = psa if wc % 2 == 0 else psb
                t.wait_ge(spw, spw0 + wc + 1)
                if wc >= 2:
                    t.wait_ge(evo, evo0 + wc - 1)
                ins = None
                for c in range(8):
                    lh = wbh[:, (wc % 2) * 1024 + c * 128:
                             (wc % 2) * 1024 + (c + 1) * 128]
                    ll = wbl[:, (wc % 2) * 1024 + c * 128:
                             (wc % 2) * 1024 + (c + 1) * 128]
                    for q in range(NT):
                        pss = ps[:, q * 512:(q + 1) * 512]
                        xh = xnh[:, c * 2048 + q * 512:c * 2048 + (q + 1) * 512]
                        xl = xnl[:, c * 2048 + q * 512:c * 2048 + (q + 1) * 512]
                        ins = t.matmul(pss, lh, xh, start=(c == 0), stop=False)
                        ins = t.matmul(pss, lh, xl, start=False, stop=False)
                        ins = t.matmul(pss, ll, xh, start=False, stop=(c == 7))
                k.inc(ins, mmo, 1)

        @block.vector
        def _(v):
            for wc in range(8):
                ps = psa if wc % 2 == 0 else psb
                sl = slice((wc % 2) * 2048, (wc % 2) * 2048 + 2048)
                v.wait_ge(ldw[wc % 2], ldw0[wc % 2] + 16 * (wc // 2 + 1))
                if wc >= 2:
                    v.wait_ge(mmo, mmo0 + wc - 1)
                wsl = slice((wc % 2) * 1024, (wc % 2) * 1024 + 1024)
                v.tensor_copy(out=wbh[:, wsl], in_=wblk[:, wsl])
                v.drain()
                v.tensor_tensor(out=wbl[:, wsl], in0=wblk[:, wsl],
                                in1=wbh[:, wsl], op=ALU.subtract)
                v.drain()
                k.inc(v.nop(), spw, 1)
                v.wait_ge(mmo, mmo0 + wc + 1)
                if wc >= 2:
                    v.wait_ge(sto[wc % 2], sto0[wc % 2] + 16 * (wc // 2))
                if sub_xk:
                    v.tensor_tensor(out=stg[:, sl], in0=ps[:, :],
                                    in1=xkbuf[:, wc * 2048:(wc + 1) * 2048],
                                    op=ALU.subtract)
                    v.drain()
                else:
                    v.tensor_copy(out=stg[:, sl], in_=ps[:, :])
                    v.drain()
                    if outkey == "l0a_xk":
                        v.tensor_copy(out=xkbuf[:, wc * 2048:(wc + 1) * 2048],
                                      in_=ps[:, :])
                        v.drain()
                k.inc(v.nop(), evo, 1)


def build_l0_proj(k, dram, xnh, xnl):
    nc = k.nc
    with contextlib.ExitStack() as es:
        psA_t = es.enter_context(nc.psum_tensor(k.name("p_psA"), [128, 2048], F32))
        psB_t = es.enter_context(nc.psum_tensor(k.name("p_psB"), [128, 2048], F32))
        wblk_t = es.enter_context(nc.sbuf_tensor(k.name("p_wblk"), [128, 2048], F32))
        wbh_t = es.enter_context(nc.sbuf_tensor(k.name("p_wbh"), [128, 2048], BF16))
        wbl_t = es.enter_context(nc.sbuf_tensor(k.name("p_wbl"), [128, 2048], BF16))
        stg_t = es.enter_context(nc.sbuf_tensor(k.name("p_stg"), [128, 4096], F32))
        xkb_t = es.enter_context(nc.sbuf_tensor(k.name("p_xkb"), [128, 16384], F32))
        psa, psb = psA_t.ap(), psB_t.ap()
        wblk, wbh, wbl = wblk_t.ap(), wbh_t.ap(), wbl_t.ap()
        stg, xkb = stg_t.ap(), xkb_t.ap()

        _l0_proj_one(k, dram, xnh, xnl, "l0a_wkT", "l0a_xk", False,
                     psa, psb, wblk, wbh, wbl, stg, xkb)
        _l0_proj_one(k, dram, xnh, xnl, "l0a_wvT", "l0a_xvd", True,
                     psa, psb, wblk, wbh, wbl, stg, xkb)
        _l0_proj_one(k, dram, xnh, xnl, "l0a_wqT", "l0a_xq", False,
                     psa, psb, wblk, wbh, wbl, stg, xkb)

        ldl = k.dsem()
        mml = k.sem()
        evl = k.sem()
        trl = k.sem()
        rdl = k.sem()
        stl = k.gsem()
        ldl0, mml0, evl0, trl0 = k.v(ldl), k.v(mml), k.v(evl), k.v(trl)
        rdl0 = k.v(rdl)
        lrt = stg[0:16, 0:2048]
        lrow = stg[0:1, 2048:2064]
        with nc.Block() as block:
            @block.sync
            def _(s):
                k.inc(s.dma_start(
                    out=wblk[:, 0:128].rearrange("p (c m) -> p c m", c=8),
                    in_=dram["l0a_lrwT"].rearrange("(c p) m -> p c m", p=128)),
                    ldl, 16)
                k.inc(s.dma_start(out=lrow, in_=dram["l0a_lrb"][:, :]), ldl, 16)
                s.wait_ge(ldl, k.v(ldl))

            @block.vector
            def _(v):
                v.wait_ge(ldl, k.v(ldl))
                v.tensor_copy(out=wbh[:, 0:128], in_=wblk[:, 0:128])
                v.drain()
                v.tensor_tensor(out=wbl[:, 0:128], in0=wblk[:, 0:128],
                                in1=wbh[:, 0:128], op=ALU.subtract)
                v.drain()
                k.inc(v.nop(), evl, 1)

        evl0 = k.v(evl)
        with nc.Block() as block:
            @block.tensor
            def _(t):
                for tl in range(16):
                    ps = psa if tl % 2 == 0 else psb
                    if tl >= 2:
                        t.wait_ge(trl, trl0 + tl - 1)
                    ins = None
                    for c in range(8):
                        lh = wbh[:, c * 16:(c + 1) * 16]
                        ll = wbl[:, c * 16:(c + 1) * 16]
                        xh = xnh[:, c * 2048 + tl * 128:c * 2048 + (tl + 1) * 128]
                        xl = xnl[:, c * 2048 + tl * 128:c * 2048 + (tl + 1) * 128]
                        ins = t.matmul(ps[:, 0:16], xh, lh, start=(c == 0), stop=False)
                        ins = t.matmul(ps[:, 0:16], xl, lh, start=False, stop=False)
                        ins = t.matmul(ps[:, 0:16], xh, ll, start=False,
                                       stop=False)
                        if c == 7:
                            ins = t.matmul(ps[:, 0:16], k.ones_row[0:1, :],
                                           lrow, start=False, stop=True)
                    k.inc(ins, mml, 1)
                    t.wait_ge(evl, evl0 + tl + 1)
                    if tl >= 2:
                        t.wait_ge(rdl, rdl0 + tl - 1)
                    k.inc(t.matmul(ps[0:16, 512:640], stg[:, 2176 + (tl % 2) * 16:
                                                          2176 + (tl % 2) * 16 + 16],
                                   k.ident[:, :], start=True, stop=True), trl, 1)

            @block.scalar
            def _(s):
                for tl in range(16):
                    ps = psa if tl % 2 == 0 else psb
                    s.wait_ge(mml, mml0 + tl + 1)
                    if tl >= 2:
                        s.wait_ge(trl, trl0 + tl - 1)
                    s.activation(out=stg[:, 2176 + (tl % 2) * 16:
                                         2176 + (tl % 2) * 16 + 16],
                                 in_=ps[:, 0:16], func=AF.Sigmoid)
                    s.drain()
                    k.inc(s.nop(), evl, 1)

            @block.vector
            def _(v):
                for tl in range(16):
                    ps = psa if tl % 2 == 0 else psb
                    v.wait_ge(trl, trl0 + tl + 1)
                    v.tensor_copy(out=lrt[:, tl * 128:(tl + 1) * 128],
                                  in_=ps[0:16, 512:640])
                    v.drain()
                    k.inc(v.nop(), rdl, 1)
                k.inc(v.nop(), evl, 1)

            @block.gpsimd
            def _(g):
                g.wait_ge(evl, k.v(evl))
                k.inc(g.dma_start(out=dram["l0a_lr"][:, :], in_=lrt), stl, 16)
                g.wait_ge(stl, k.v(stl))


def MM(t, *a, **kw):
    kw.setdefault("skip_group_check", True)
    return t.matmul(*a, **kw)


def _l0_chunk(k, dram, st, n, r, psa, psb, bufs):
    nc = k.nc
    (xkq, xvds, lrsg, xqws, w1a, w2a, w2t, b2r, x2f, x2b, gbuf, gz1f,
     gsc1, gsc2, gsc3, gz2q, gz1tmq, x2tmq, x1tmq, gfm, etamq, a1cT, a2cq,
     a2cT, lnzq, colp, lrq, lastq4, trilT, trilQ, lnwq, lnbq, negtr, ntqc,
     t15c, lx1b) = bufs

    def R(h):
        return 64 * (h % 2)

    def pad(c, h):
        return (c * 8 + h // 2) * 128 + 64 * (h % 2)

    sT1 = k.sem(); sT2 = k.sem(); sT3 = k.sem(); sT4 = k.sem()
    sT5 = k.sem(); sT6 = k.sem(); sT7 = k.sem(); sT8 = k.sem()
    sA1 = k.sem(); sA2 = k.sem(); sA3 = k.sem()
    sV1 = k.sem(); sV2 = k.sem(); sV3 = k.sem(); sV4 = k.sem()
    sV5 = k.sem(); sV6 = k.sem(); sV7 = k.sem(); sV8 = k.sem()
    sV9 = k.sem(); sV10 = k.sem(); sV11 = k.sem(); sV12 = k.sem()
    sV13 = k.sem(); sV14 = k.sem(); sV15 = k.sem(); sV16 = k.sem()
    T10, T20, T30, T40 = k.v(sT1), k.v(sT2), k.v(sT3), k.v(sT4)
    T50, T60, T70, T80 = k.v(sT5), k.v(sT6), k.v(sT7), k.v(sT8)
    A10, A20, A30 = k.v(sA1), k.v(sA2), k.v(sA3)
    V10, V20, V30, V40 = k.v(sV1), k.v(sV2), k.v(sV3), k.v(sV4)
    V50, V60, V70, V80 = k.v(sV5), k.v(sV6), k.v(sV7), k.v(sV8)
    V90, V100, V110, V120 = k.v(sV9), k.v(sV10), k.v(sV11), k.v(sV12)
    V130, V140, V150, V160 = k.v(sV13), k.v(sV14), k.v(sV15), k.v(sV16)

    XVDQ = psa[:, 0:512]
    AT1 = psa[:, 512:640]
    ETM = psa[:, 640:768]
    LRC = psa[:, 768:776]
    Z2Q = psa[:, 1024:1536]
    GFM = psa[0:64, 0:256]
    W2U = psa[:, 1024:1152]
    DB2 = psa[0:1, 1152:1216]
    W1U = psa[0:65, 1536:1792]
    Z2B = psb[:, 0:1024]

    with nc.Block() as block:
        @block.tensor
        def _(t):
            for h in range(16):
                ins = MM(t, 
                    psa[R(h):R(h) + 64, (h // 2) * 64:(h // 2) * 64 + 64],
                    xvds[0:64, h * 128 + r * 16:h * 128 + r * 16 + 64],
                    k.ident[0:64, 0:64], start=(h in (0, 1)),
                    stop=(h in (14, 15)))
            k.inc(ins, sT1, 1)
            for h in range(16):
                for c in range(2):
                    ins = MM(t, 
                        psb[:, pad(c, h):pad(c, h) + 32],
                        w1a[0:65, h * 256 + c * 128:h * 256 + (c + 1) * 128],
                        xkq[0:65, h * 256 + r * 32:h * 256 + r * 32 + 32],
                        start=(h == 0 or h == 8), stop=False)
            k.inc(ins, sT2, 1)
            for h in range(16):
                xkw = xkq[0:64, h * 256 + r * 32:h * 256 + r * 32 + 64]
                xqsl = xkq[0:64, h * 256 + r * 32 + 16:h * 256 + r * 32 + 32]
                ins = MM(t, 
                    AT1[R(h):R(h) + 64, (h // 2) * 16:(h // 2) * 16 + 16],
                    xkw, xqsl, start=(h in (0, 1)), stop=False)
                lrsl = lrsg[0:1, h * 128 + r * 16:h * 128 + r * 16 + 16]
                lrw = lrsg[0:1, h * 128 + r * 16:h * 128 + r * 16 + 64]
                ins = MM(t, 
                    ETM[R(h):R(h) + 64, (h // 2) * 16:(h // 2) * 16 + 16],
                    k.ones_row[0:1, 0:64], lrsl, start=False, stop=False)
                ins = MM(t, 
                    LRC[R(h):R(h) + 64, h // 2:h // 2 + 1],
                    lrw, k.cpool[0:1, 3:4], start=False, stop=(h in (14, 15)))
            k.inc(ins, sT3, 1)
            for h in range(16):
                ins = MM(t, 
                    psa[R(h):R(h) + 64, 1536 + (h // 2) * 64:
                        1536 + (h // 2) * 64 + 64],
                    xkq[0:64, h * 256 + r * 32:h * 256 + r * 32 + 64],
                    k.ident[0:64, 0:64], start=(h in (0, 1)),
                    stop=(h in (14, 15)))
            k.inc(ins, sT5, 1)
            t.wait_ge(sA1, A10 + 1)
            for h in range(16):
                d = h // 2
                ins = MM(t, 
                    Z2Q[R(h):R(h) + 64, d * 64:d * 64 + 64],
                    k.ones_row[0:1, 0:64],
                    b2r[0:1, h * 80:h * 80 + 64],
                    start=(h in (0, 1)), stop=False)
                for c in range(2):
                    ins = MM(t, 
                        Z2Q[R(h):R(h) + 64, d * 64:d * 64 + 64],
                        x2f[:, pad(c, h):pad(c, h) + 64],
                        w2a[:, (2 * h + c) * 80:(2 * h + c) * 80 + 64],
                        start=False, stop=(h in (14, 15) and c == 1))
            k.inc(ins, sT4, 1)
            t.wait_ge(sV2, V20 + 1)
            for h in range(16):
                ins = MM(t, 
                    GFM[0:64, h * 16:h * 16 + 16],
                    gz2q[R(h):R(h) + 16, (h // 2) * 64:(h // 2) * 64 + 64],
                    k.ident[R(h):R(h) + 16, R(h):R(h) + 16],
                    start=(h == 0), stop=(h == 15))
            k.inc(ins, sT5, 1)
            t.wait_ge(sV3, V30 + 1)
            for h in range(16):
                for c in range(2):
                    ins = MM(t, 
                        psb[:, pad(c, h) + 32:pad(c, h) + 48],
                        w2t[0:64, h * 256 + c * 128:h * 256 + (c + 1) * 128],
                        gfm[0:64, h * 16:h * 16 + 16],
                        start=False, stop=True)
            k.inc(ins, sT6, 1)
            t.wait_ge(sV4, V40 + 1)
            s11 = [psa[:, 256:384], psa[:, 512:640],
                   psa[:, 1024:1152], psa[:, 1536:1664]]
            for i in range(16):
                if i >= 4:
                    t.wait_ge(sV6, V60 + i - 3)
                ins = MM(t, 
                    s11[i % 4],
                    gz1f[:, i * 128:(i + 1) * 128],
                    k.ident[:, :], start=True, stop=True)
                k.inc(ins, sT7, 1)
            t.wait_ge(sV5, V50 + 1)
            t.wait_ge(sV6, V60 + 16)
            for h in range(16):
                d = h // 2
                for c in range(2):
                    ins = MM(t, 
                        psb[:, pad(c, h) + 16:pad(c, h) + 32],
                        gz1tmq[R(h):R(h) + 16,
                               (c * 8 + d) * 128:(c * 8 + d) * 128 + 128],
                        a1cT[R(h):R(h) + 16, d * 16:d * 16 + 16],
                        start=False, stop=(h in (7, 15)))
            k.inc(ins, sT8, 1)
            t.wait_ge(sV11, V110 + 1)
            for h in range(16):
                d = h // 2
                if h >= 2:
                    t.wait_ge(sV12, V120 + h - 1)
                w1slot = (psa[0:65, 1536:1792] if h % 2 == 0
                          else psa[0:65, 0:256])
                for c in range(2):
                    ins = MM(t, 
                        w1slot[0:65, c * 128:c * 128 + 128],
                        lx1b[R(h):R(h) + 16, d * 65:d * 65 + 65],
                        gz1tmq[R(h):R(h) + 16,
                               (c * 8 + d) * 128:(c * 8 + d) * 128 + 128],
                        start=(c == 0), stop=(c == 1))
                k.inc(ins, sT6, 1)
            t.wait_ge(sA2, A20 + 1)
            t.wait_ge(sV7, V70 + 1)
            for h in range(16):
                d = h // 2
                ins = MM(t, 
                    Z2B[R(h):R(h) + 64, d * 128:d * 128 + 80],
                    k.ones_row[0:1, 0:64],
                    b2r[0:1, h * 80:h * 80 + 80],
                    start=(h in (0, 1, 8, 9)), stop=False)
                for c in range(2):
                    ins = MM(t, 
                        Z2B[R(h):R(h) + 64, d * 128:d * 128 + 80],
                        x2b[:, pad(c, h):pad(c, h) + 64],
                        w2a[:, (2 * h + c) * 80:(2 * h + c) * 80 + 80],
                        start=False, stop=False)
            k.inc(ins, sT1, 1)
            t.wait_ge(sV8, V80 + 1)
            for h in range(16):
                d = h // 2
                ins = MM(t, 
                    psa[R(h):R(h) + 16, 512 + d * 16:512 + d * 16 + 16],
                    a2cq[R(h):R(h) + 16, d * 16:d * 16 + 16],
                    k.ident[R(h):R(h) + 16, R(h):R(h) + 16],
                    start=(h in (0, 1)), stop=(h in (14, 15)))
            k.inc(ins, sT2, 1)
            t.wait_ge(sV9, V90 + 1)
            for h in range(16):
                d = h // 2
                ins = MM(t, 
                    Z2B[R(h):R(h) + 16, d * 128:d * 128 + 64],
                    a2cT[R(h):R(h) + 16, d * 16:d * 16 + 16],
                    gz2q[R(h):R(h) + 16, d * 64:d * 64 + 64],
                    start=False, stop=(h in (6, 7, 14, 15)))
            k.inc(ins, sT3, 1)
            t.wait_ge(sV10, V100 + 1)
            for d in range(8):
                ins = MM(t, 
                    psa[0:64, d * 128:d * 128 + 128],
                    lnzq[:, d * 64:d * 64 + 64],
                    k.ident[:, :], start=(d == 0 or d == 4),
                    stop=(d == 3 or d == 7))
            k.inc(ins, sT4, 1)
            t.wait_ge(sV12, V120 + 16)
            t.wait_ge(sV16, V160 + 1)
            s19 = [psa[:, 0:128], psa[:, 512:640],
                   psa[:, 1024:1152], psa[:, 1536:1664]]
            for i in range(16):
                if i >= 4:
                    t.wait_ge(sV13, V130 + i - 3)
                ins = MM(t, 
                    s19[i % 4],
                    x2f[:, i * 128:(i + 1) * 128],
                    k.ident[:, :], start=True, stop=True)
                k.inc(ins, sT7, 1)
            t.wait_ge(sV13, V130 + 17)
            for h in range(16):
                d = h // 2
                if h >= 2:
                    t.wait_ge(sV14, V140 + h - 1)
                w2slot = psa[:, 1024:1152] if h % 2 == 0 else psa[:, 512:640]
                dbslot = (psa[0:1, 1152:1216] if h % 2 == 0
                          else psa[0:1, 640:704])
                ins = MM(t, 
                    w2slot[:, 0:64],
                    x2tmq[R(h):R(h) + 16, d * 260:d * 260 + 128],
                    gz2q[R(h):R(h) + 16, d * 64:d * 64 + 64],
                    start=True, stop=False)
                ins = MM(t, 
                    dbslot[0:1, 0:64],
                    x2tmq[R(h):R(h) + 16, d * 260 + 256:d * 260 + 257],
                    gz2q[R(h):R(h) + 16, d * 64:d * 64 + 64],
                    start=False, stop=False)
                ins = MM(t, 
                    w2slot[:, 64:128],
                    x2tmq[R(h):R(h) + 16, d * 260 + 128:d * 260 + 256],
                    gz2q[R(h):R(h) + 16, d * 64:d * 64 + 64],
                    start=False, stop=True)
                k.inc(ins, sT8, 1)
            t.wait_ge(sV12, V120 + 16)
            for h in range(16):
                t.wait_ge(sV14, V140 + h + 1)
                if h >= 2:
                    t.wait_ge(sV15, V150 + h - 1)
                base = 1536 if h % 2 == 0 else 0
                for c in range(2):
                    ins = MM(t, 
                        psa[0:64, base + c * 128:base + c * 128 + 128],
                        w2a[:, (2 * h + c) * 80:(2 * h + c) * 80 + 64],
                        k.ident[:, :], start=(c == 0), stop=(c == 1))
                k.inc(ins, sT1, 1)

        @block.scalar
        def _(s):
            z1view = psb[:, :].rearrange("p (b z) -> p b z", z=64)
            x2fv = x2f.rearrange("p (b z) -> p b z", z=64)
            x2bv = x2b.rearrange("p (b z) -> p b z", z=64)
            s.wait_ge(sT2, T20 + 1)
            s.activation(out=x2fv[:, :, 0:16], in_=z1view[:, :, 0:16],
                         func=AF.Gelu_apprx_tanh)
            s.drain()
            k.inc(s.nop(), sA1, 1)
            s.wait_ge(sV1, V10 + 1)
            s.activation(out=gsc3.rearrange("p (b z) -> p b z", z=64)[:, :, 0:16],
                         in_=gsc2.rearrange("p (b z) -> p b z", z=64)[:, :, 0:16],
                         func=AF.Tanh)
            s.drain()
            k.inc(s.nop(), sA3, 1)
            s.wait_ge(sV1, V10 + 2)
            s.activation(out=colp[:, 16:24], in_=colp[:, 0:8], func=AF.Sqrt,
                         bias=k.cpool[:, 1:2])
            s.drain()
            k.inc(s.nop(), sA3, 1)
            s.wait_ge(sT8, T80 + 1)
            s.activation(out=x2bv[:, :, 0:16], in_=z1view[:, :, 16:32],
                         func=AF.Gelu_apprx_tanh)
            s.drain()
            k.inc(s.nop(), sA2, 1)
            s.wait_ge(sV1, V10 + 3)
            s.activation(out=colp[:, 24:32], in_=colp[:, 8:16], func=AF.Sqrt,
                         bias=k.cpool[:, 1:2])
            s.drain()
            k.inc(s.nop(), sA3, 1)

        @block.vector
        def _(v):
            zv = psb[:, :].rearrange("p (b z) -> p b z", z=64)[:, :, 0:16]
            g1v = gsc1.rearrange("p (b z) -> p b z", z=64)[:, :, 0:16]
            g2v = gsc2.rearrange("p (b z) -> p b z", z=64)[:, :, 0:16]
            g3v = gsc3.rearrange("p (b z) -> p b z", z=64)[:, :, 0:16]
            gbv = gbuf.rearrange("p (b z) -> p b z", z=64)[:, :, 0:16]
            v.wait_ge(sT3, T30 + 1)
            v.tensor_copy(out=lrq[:, :], in_=LRC)
            v.drain()
            v.tensor_copy(out=etamq[0:16, :], in_=ETM[0:16, :])
            v.tensor_copy(out=etamq[64:80, :], in_=ETM[64:80, :])
            v.drain()
            v.tensor_scalar(out=lastq4[:, :], in0=lrq[:, :], scalar1=t15c,
                            scalar2=None, op0=ALU.mult, op1=ALU.bypass)
            v.drain()
            A13 = a1cT.rearrange("p (d z) -> p d z", z=16)
            AT13 = AT1.rearrange("p (d z) -> p d z", z=16)
            v.tensor_scalar(out=A13, in0=AT13, scalar1=1.0, scalar2=None,
                            op0=ALU.add, op1=ALU.bypass)
            v.drain()
            for d in range(8):
                v.tensor_tensor(out=a1cT[:, d * 16:d * 16 + 16],
                                in0=a1cT[:, d * 16:d * 16 + 16],
                                in1=trilT[:, :], op=ALU.mult)
            v.drain()
            for d in range(8):
                v.tensor_scalar(out=a1cT[:, d * 16:d * 16 + 16],
                                in0=a1cT[:, d * 16:d * 16 + 16],
                                scalar1=lrq[:, d:d + 1],
                                scalar2=None, op0=ALU.mult, op1=ALU.bypass)
            v.drain()
            for d in range(8):
                v.tensor_tensor(out=a1cT[:, d * 16:d * 16 + 16],
                                in0=a1cT[:, d * 16:d * 16 + 16],
                                in1=negtr[:, :], op=ALU.mult)
            v.drain()
            k.inc(v.nop(), sV5, 1)
            v.wait_ge(sT5, T50 + 1)
            v.tensor_copy(
                out=x1tmq.rearrange("p (d z) -> p d z", z=65)[:, :, 0:64],
                in_=psa[:, 1536:2048].rearrange("p (d z) -> p d z", z=64))
            v.drain()
            for d in range(8):
                v.tensor_scalar(out=lx1b[:, d * 65:d * 65 + 65],
                                in0=x1tmq[:, d * 65:d * 65 + 65],
                                scalar1=lastq4[:, d:d + 1], scalar2=None,
                                op0=ALU.mult, op1=ALU.bypass)
            v.drain()
            k.inc(v.nop(), sV11, 1)
            v.wait_ge(sT2, T20 + 1)
            v.tensor_copy(out=gbv, in_=zv)
            v.drain()
            v.tensor_tensor(out=g1v, in0=gbv, in1=gbv, op=ALU.mult)
            v.drain()
            v.tensor_scalar(out=g2v, in0=g1v, scalar1=0.0356774081,
                            scalar2=0.79788456, op0=ALU.mult, op1=ALU.add)
            v.drain()
            v.tensor_tensor(out=g2v, in0=g2v, in1=gbv, op=ALU.mult)
            v.drain()
            k.inc(v.nop(), sV1, 1)
            v.wait_ge(sA3, A30 + 1)
            v.tensor_scalar(out=g1v, in0=g1v, scalar1=0.1070322243,
                            scalar2=0.79788456, op0=ALU.mult, op1=ALU.add)
            v.drain()
            v.tensor_tensor(out=g1v, in0=g1v, in1=gbv, op=ALU.mult)
            v.drain()
            v.tensor_tensor(out=g2v, in0=g3v, in1=g3v, op=ALU.mult)
            v.drain()
            v.tensor_scalar(out=g2v, in0=g2v, scalar1=-0.5, scalar2=0.5,
                            op0=ALU.mult, op1=ALU.add)
            v.drain()
            v.tensor_tensor(out=g1v, in0=g1v, in1=g2v, op=ALU.mult)
            v.drain()
            v.tensor_scalar(out=g2v, in0=g3v, scalar1=0.5, scalar2=0.5,
                            op0=ALU.mult, op1=ALU.add)
            v.drain()
            v.tensor_tensor(out=gbv, in0=g1v, in1=g2v, op=ALU.add)
            v.drain()
            v.wait_ge(sT4, T40 + 1)
            v.wait_ge(sT1, T10 + 1)
            Z3 = Z2Q.rearrange("p (d z) -> p d z", z=64)
            C13 = gsc1[:, 0:512].rearrange("p (d z) -> p d z", z=64)
            C23 = gsc2[:, 0:512].rearrange("p (d z) -> p d z", z=64)
            C33 = gsc3[:, 512:1024].rearrange("p (d z) -> p d z", z=64)
            v.tensor_reduce(out=colp[:, 0:8], in_=Z3, axis=AX.X, op=ALU.add)
            v.drain()
            v.tensor_scalar(out=colp[:, 0:8], in0=colp[:, 0:8],
                            scalar1=1.0 / 64, scalar2=None,
                            op0=ALU.mult, op1=ALU.bypass)
            v.drain()
            for d in range(8):
                v.tensor_scalar(out=gsc1[:, d * 64:d * 64 + 64],
                                in0=Z2Q[:, d * 64:d * 64 + 64],
                                scalar1=colp[:, d:d + 1],
                                scalar2=None, op0=ALU.subtract, op1=ALU.bypass)
            v.drain()
            v.tensor_tensor(out=C23, in0=C13, in1=C13, op=ALU.mult)
            v.drain()
            v.tensor_reduce(out=colp[:, 0:8], in_=C23, axis=AX.X, op=ALU.add)
            v.drain()
            v.tensor_scalar(out=colp[:, 0:8], in0=colp[:, 0:8],
                            scalar1=1.0 / 64, scalar2=None,
                            op0=ALU.mult, op1=ALU.bypass)
            v.drain()
            k.inc(v.nop(), sV1, 1)
            v.wait_ge(sA3, A30 + 2)
            v.reciprocal(out=colp[:, 16:24], in_=colp[:, 16:24])
            v.drain()
            for d in range(8):
                v.tensor_scalar(out=gsc1[:, d * 64:d * 64 + 64],
                                in0=gsc1[:, d * 64:d * 64 + 64],
                                scalar1=colp[:, 16 + d:17 + d], scalar2=None,
                                op0=ALU.mult, op1=ALU.bypass)
            v.drain()
            LW3 = lnwq.rearrange("p (d z) -> p d z", z=64)
            LB3 = lnbq.rearrange("p (d z) -> p d z", z=64)
            XV3 = XVDQ.rearrange("p (d z) -> p d z", z=64)
            v.tensor_tensor(out=C23, in0=C13, in1=LW3, op=ALU.mult)
            v.drain()
            v.tensor_tensor(out=C23, in0=C23, in1=LB3, op=ALU.add)
            v.drain()
            v.tensor_tensor(out=C23, in0=C23, in1=XV3, op=ALU.subtract)
            v.drain()
            v.tensor_tensor(out=C23, in0=C23, in1=LW3, op=ALU.mult)
            v.drain()
            v.tensor_reduce(out=colp[:, 0:8], in_=C23, axis=AX.X, op=ALU.add)
            v.drain()
            v.tensor_tensor(out=C33, in0=C23, in1=C13, op=ALU.mult)
            v.drain()
            v.tensor_reduce(out=colp[:, 8:16], in_=C33, axis=AX.X, op=ALU.add)
            v.drain()
            for d in range(8):
                v.tensor_scalar(out=gsc1[:, d * 64:d * 64 + 64],
                                in0=gsc1[:, d * 64:d * 64 + 64],
                                scalar1=colp[:, 8 + d:9 + d],
                                scalar2=1.0 / 64, op0=ALU.mult, op1=ALU.mult)
            v.drain()
            v.tensor_scalar(out=colp[:, 0:8], in0=colp[:, 0:8],
                            scalar1=1.0 / 64, scalar2=None,
                            op0=ALU.mult, op1=ALU.bypass)
            v.drain()
            for d in range(8):
                v.tensor_scalar(out=gsc2[:, d * 64:d * 64 + 64],
                                in0=gsc2[:, d * 64:d * 64 + 64],
                                scalar1=colp[:, d:d + 1],
                                scalar2=None, op0=ALU.subtract, op1=ALU.bypass)
            v.drain()
            v.tensor_tensor(out=C23, in0=C23, in1=C13, op=ALU.subtract)
            v.drain()
            for d in range(8):
                v.tensor_scalar(out=gz2q[:, d * 64:d * 64 + 64],
                                in0=gsc2[:, d * 64:d * 64 + 64],
                                scalar1=colp[:, 16 + d:17 + d], scalar2=None,
                                op0=ALU.mult, op1=ALU.bypass)
            v.drain()
            k.inc(v.nop(), sV2, 1)
            v.wait_ge(sT5, T50 + 2)
            v.tensor_copy(out=gfm[0:64, :], in_=GFM)
            v.drain()
            k.inc(v.nop(), sV3, 1)
            v.wait_ge(sT6, T60 + 1)
            v.tensor_tensor(
                out=gz1f.rearrange("p (b z) -> p b z", z=64)[:, :, 0:16],
                in0=psb[:, :].rearrange("p (b z) -> p b z", z=64)[:, :, 32:48],
                in1=gbv, op=ALU.mult)
            v.drain()
            k.inc(v.nop(), sV4, 1)
            s11 = [psa[:, 256:384], psa[:, 512:640],
                   psa[:, 1024:1152], psa[:, 1536:1664]]
            for i in range(16):
                v.wait_ge(sT7, T70 + i + 1)
                v.tensor_copy(out=gz1tmq[:, i * 128:(i + 1) * 128],
                              in_=s11[i % 4])
                v.drain()
                k.inc(v.nop(), sV6, 1)
            v.wait_ge(sA1, A10 + 1)
            for c in range(2):
                v.tensor_copy(
                    out=w2a.rearrange("p (d q c2 z) -> p d q c2 z",
                                      d=8, q=2, c2=2)[:, :, :, c, 64:80],
                    in_=x2f[:, c * 1024:(c + 1) * 1024]
                        .rearrange("p (d q z) -> p d q z", q=2, z=64)
                        [:, :, :, 0:16])
                v.drain()
            k.inc(v.nop(), sV7, 1)
            for h in range(16):
                v.wait_ge(sT6, T60 + 1 + h + 1)
                w1slot = (psa[0:65, 1536:1792] if h % 2 == 0
                          else psa[0:65, 0:256])
                for c in range(2):
                    v.tensor_tensor(
                        out=w1a[0:65, h * 256 + c * 128:h * 256 + (c + 1) * 128],
                        in0=w1a[0:65, h * 256 + c * 128:h * 256 + (c + 1) * 128],
                        in1=w1slot[0:65, c * 128:c * 128 + 128],
                        op=ALU.subtract)
                v.drain()
                k.inc(v.nop(), sV12, 1)
            v.wait_ge(sT1, T10 + 2)
            A23 = a2cq.rearrange("p (d z) -> p d z", z=16)
            ZA3 = Z2B.rearrange("p (d z) -> p d z", z=128)[:, :, 64:80]
            ET3 = etamq.rearrange("p (d z) -> p d z", z=16)
            v.tensor_scalar(out=A23, in0=ZA3, scalar1=1.0, scalar2=None,
                            op0=ALU.add, op1=ALU.bypass)
            v.drain()
            for d in range(8):
                v.tensor_tensor(out=a2cq[:, d * 16:d * 16 + 16],
                                in0=a2cq[:, d * 16:d * 16 + 16],
                                in1=trilQ[:, :], op=ALU.mult)
            v.drain()
            v.tensor_tensor(out=A23, in0=A23, in1=ET3, op=ALU.mult)
            v.drain()
            v.tensor_scalar(out=A23, in0=A23, scalar1=ntqc,
                            scalar2=None, op0=ALU.mult, op1=ALU.bypass)
            v.drain()
            k.inc(v.nop(), sV8, 1)
            v.wait_ge(sT2, T20 + 2)
            v.tensor_copy(out=a2cT[0:16, :], in_=psa[0:16, 512:640])
            v.tensor_copy(out=a2cT[64:80, :], in_=psa[64:80, 512:640])
            v.drain()
            k.inc(v.nop(), sV9, 1)
            v.wait_ge(sT3, T30 + 2)
            ZB3 = Z2B.rearrange("p (d z) -> p d z", z=128)[:, :, 0:64]
            D13 = gsc1[:, 0:512].rearrange("p (d z) -> p d z", z=64)
            D23 = gsc2[:, 0:512].rearrange("p (d z) -> p d z", z=64)
            v.tensor_reduce(out=colp[:, 8:16], in_=ZB3, axis=AX.X, op=ALU.add)
            v.drain()
            v.tensor_scalar(out=colp[:, 8:16], in0=colp[:, 8:16],
                            scalar1=1.0 / 64, scalar2=None,
                            op0=ALU.mult, op1=ALU.bypass)
            v.drain()
            for d in range(8):
                v.tensor_scalar(out=gsc1[:, d * 64:d * 64 + 64],
                                in0=Z2B[:, d * 128:d * 128 + 64],
                                scalar1=colp[:, 8 + d:9 + d],
                                scalar2=None, op0=ALU.subtract, op1=ALU.bypass)
            v.drain()
            v.tensor_tensor(out=D23, in0=D13, in1=D13, op=ALU.mult)
            v.drain()
            v.tensor_reduce(out=colp[:, 8:16], in_=D23, axis=AX.X, op=ALU.add)
            v.drain()
            v.tensor_scalar(out=colp[:, 8:16], in0=colp[:, 8:16],
                            scalar1=1.0 / 64, scalar2=None,
                            op0=ALU.mult, op1=ALU.bypass)
            v.drain()
            k.inc(v.nop(), sV1, 1)
            v.wait_ge(sA3, A30 + 3)
            v.reciprocal(out=colp[:, 24:32], in_=colp[:, 24:32])
            v.drain()
            for d in range(8):
                v.tensor_scalar(out=gsc1[:, d * 64:d * 64 + 64],
                                in0=gsc1[:, d * 64:d * 64 + 64],
                                scalar1=colp[:, 24 + d:25 + d], scalar2=None,
                                op0=ALU.mult, op1=ALU.bypass)
            v.drain()
            LWj = lnwq.rearrange("p (d z) -> p d z", z=64)
            LBj = lnbq.rearrange("p (d z) -> p d z", z=64)
            LZj = lnzq.rearrange("p (d z) -> p d z", z=64)
            v.tensor_tensor(out=D13, in0=D13, in1=LWj, op=ALU.mult)
            v.drain()
            v.tensor_tensor(out=LZj, in0=D13, in1=LBj, op=ALU.add)
            v.drain()
            k.inc(v.nop(), sV10, 1)
            v.wait_ge(sT4, T40 + 2)
            for d in range(8):
                v.tensor_tensor(
                    out=xqws[0:64, d * 256:(d + 1) * 256]
                        .rearrange("p (j z) -> p j z", z=128)
                        [:, :, r * 16:r * 16 + 16],
                    in0=psa[0:64, d * 128:(d + 1) * 128]
                        .rearrange("p (j z) -> p j z", z=64)[:, :, 0:16],
                    in1=xkq[0:64, d * 512:(d + 1) * 512]
                        .rearrange("p (j z) -> p j z", z=256)
                        [:, :, r * 32 + 16:r * 32 + 32],
                    op=ALU.add)
                v.drain()
            k.inc(v.nop(), sV16, 1)
            s19 = [psa[:, 0:128], psa[:, 512:640],
                   psa[:, 1024:1152], psa[:, 1536:1664]]
            for i in range(16):
                v.wait_ge(sT7, T70 + 16 + i + 1)
                c, d = i // 8, i % 8
                v.tensor_copy(out=x2tmq[:, d * 260 + c * 128:
                                        d * 260 + (c + 1) * 128],
                              in_=s19[i % 4])
                v.drain()
                k.inc(v.nop(), sV13, 1)
            v.tensor_copy(
                out=x2tmq.rearrange("p (d z) -> p d z", z=260)[:, :, 256:257],
                in_=lastq4[:, :].rearrange("p (d z) -> p d z", z=1))
            v.drain()
            for d in range(8):
                v.tensor_scalar(out=x2tmq[:, d * 260:d * 260 + 256],
                                in0=x2tmq[:, d * 260:d * 260 + 256],
                                scalar1=lastq4[:, d:d + 1], scalar2=None,
                                op0=ALU.mult, op1=ALU.bypass)
                v.drain()
            k.inc(v.nop(), sV13, 1)
            for h in range(16):
                v.wait_ge(sT8, T80 + 1 + h + 1)
                w2slot = psa[:, 1024:1152] if h % 2 == 0 else psa[:, 512:640]
                dbslot = (psa[0:1, 1152:1216] if h % 2 == 0
                          else psa[0:1, 640:704])
                for c in range(2):
                    sl = slice((2 * h + c) * 80, (2 * h + c) * 80 + 64)
                    v.tensor_tensor(out=w2a[:, sl], in0=w2a[:, sl],
                                    in1=w2slot[:, c * 64:c * 64 + 64],
                                    op=ALU.subtract)
                v.tensor_tensor(out=b2r[0:1, h * 80:h * 80 + 64],
                                in0=b2r[0:1, h * 80:h * 80 + 64],
                                in1=dbslot[0:1, 0:64], op=ALU.subtract)
                v.drain()
                k.inc(v.nop(), sV14, 1)
            for h in range(16):
                v.wait_ge(sT1, T10 + 2 + h + 1)
                base = 1536 if h % 2 == 0 else 0
                for c in range(2):
                    v.tensor_copy(
                        out=w2t[0:64, h * 256 + c * 128:h * 256 + (c + 1) * 128],
                        in_=psa[0:64, base + c * 128:base + c * 128 + 128])
                v.drain()
                k.inc(v.nop(), sV15, 1)


def build_l0_scan(k, dram, n_segs=16):
    nc = k.nc
    with contextlib.ExitStack() as es:
        def sb(nm, shape, dt=F32):
            return es.enter_context(nc.sbuf_tensor(k.name(nm), shape, dt)).ap()
        psa = es.enter_context(nc.psum_tensor(k.name("s_psA"), [128, 2048], F32)).ap()
        psb = es.enter_context(nc.psum_tensor(k.name("s_psB"), [128, 2048], F32)).ap()
        xkq = sb("s_xkq", [65, 4160])
        xvds = sb("s_xvd", [64, 2112])
        lrsg = sb("s_lrs", [1, 2112])
        xqws = sb("s_xqw", [64, 2048])
        w1a = sb("s_w1a", [65, 4096])
        w2a = sb("s_w2a", [128, 2560])
        w2t = sb("s_w2t", [64, 4096])
        b2r = sb("s_b2r", [1, 1280])
        x2f = sb("s_x2f", [128, 2048])
        x2b = sb("s_x2b", [128, 2048])
        gbuf = sb("s_gbf", [128, 2048])
        gz1f = sb("s_gz1", [128, 2048])
        gsc1 = sb("s_gs1", [128, 2048])
        gsc2 = sb("s_gs2", [128, 2048])
        gsc3 = sb("s_gs3", [128, 2048])
        gz2q = sb("s_gz2", [128, 512])
        gz1tmq = sb("s_g1t", [128, 2048])
        x2tmq = sb("s_x2t", [128, 2080])
        x1tmq = sb("s_x1t", [128, 520])
        lx1b = sb("s_lx1", [128, 520])
        gfm = sb("s_gfm", [64, 256])
        etamq = sb("s_etm", [128, 128])
        a1cT = sb("s_a1c", [128, 128])
        a2cq = sb("s_a2q", [128, 128])
        a2cT = sb("s_a2c", [128, 128])
        lnzq = sb("s_lnz", [128, 512])
        colp = sb("s_col", [128, 32])
        lrq = sb("s_lrq", [128, 8])
        lastq4 = sb("s_lq4", [128, 8])
        trilT = sb("s_trT", [128, 16])
        trilQ = sb("s_trQ", [128, 16])
        lnwq = sb("s_lnw", [128, 512])
        lnbq = sb("s_lnb", [128, 512])
        negtr = sb("s_ngr", [128, 16])
        ntqcb = sb("s_ntq", [128, 1])
        t15cb = sb("s_t15", [128, 1])
        trow = sb("s_trw", [1, 128])

        ldi = k.dsem()
        ev0s = k.sem()
        mmi = k.sem()
        ev0v = k.v(ev0s)
        mmi0 = k.v(mmi)
        with nc.Block() as block:
            @block.sync
            def _(s):
                k.inc(s.dma_start(out=w1a[0:65, :], in_=dram["l0a_w1a"][:, :]), ldi, 16)
                k.inc(s.dma_start(out=w2a[:, :], in_=dram["l0a_w2a"][:, :]), ldi, 16)
                k.inc(s.dma_start(out=w2t[0:64, :], in_=dram["l0a_w2t"][:, :]), ldi, 16)
                k.inc(s.dma_start(out=b2r[0:1, :], in_=dram["l0a_b2r"][:, :]), ldi, 16)
                k.inc(s.dma_start(out=trilT[:, :], in_=dram["l0a_trilT"][:, :]), ldi, 16)
                k.inc(s.dma_start(out=trilQ[:, :], in_=dram["l0a_trilQ"][:, :]), ldi, 16)
                k.inc(s.dma_start(out=lnwq[:, :], in_=dram["l0a_lnwq"][:, :]), ldi, 16)
                k.inc(s.dma_start(out=lnbq[:, :], in_=dram["l0a_lnbq"][:, :]), ldi, 16)
                k.inc(s.dma_start(out=trow[0:1, 0:16], in_=dram["l0a_tidxb"][:, :]),
                      ldi, 16)
                k.inc(s.dma_start(out=trow[0:1, 16:32], in_=dram["l0a_ltidx"][:, :]),
                      ldi, 16)
                s.wait_ge(ldi, k.v(ldi))

            @block.gpsimd
            def _(g):
                for buf in (x2f, x2b, gbuf, gz1f, gsc1, gsc2, gsc3, gz2q,
                            gz1tmq, x2tmq, x1tmq, gfm, etamq, a1cT, a2cq,
                            a2cT, lnzq, colp, lrq, lastq4, negtr, ntqcb,
                            t15cb, xqws, lrsg, xvds):
                    g.memset(buf[:, :], 0.0)
                g.memset(xkq[:, :], 0.0)
                g.memset(trow[0:1, 48:128], 0.0)
                g.drain()
                g.memset(xkq[64:65, :], 1.0)
                g.memset(x1tmq[:, :].rearrange("p (u z) -> p u z", z=65)
                         [:, :, 64:65], 1.0)
                g.drain()

        with nc.Block() as block:
            @block.vector
            def _(v):
                v.tensor_tensor(out=trow[0:1, 0:16], in0=trow[0:1, 0:16],
                                in1=trow[0:1, 16:32], op=ALU.add)
                v.drain()
                v.tensor_scalar(out=trow[0:1, 0:16], in0=trow[0:1, 0:16],
                                scalar1=0.0, scalar2=None,
                                op0=ALU.max, op1=ALU.bypass)
                v.drain()
                v.tensor_scalar(out=trow[0:1, 16:32], in0=trow[0:1, 0:16],
                                scalar1=1.0 / 64, scalar2=None,
                                op0=ALU.mult, op1=ALU.bypass)
                v.drain()
                v.tensor_scalar(out=trow[0:1, 32:48], in0=trow[0:1, 16:32],
                                scalar1=-1.0, scalar2=None,
                                op0=ALU.mult, op1=ALU.bypass)
                v.drain()
                k.inc(v.nop(), ev0s, 1)

            @block.tensor
            def _(t):
                t.wait_ge(ev0s, ev0v + 1)
                k.inc(MM(t, psa[:, 0:16], k.ones_row[0:1, :],
                               trow[0:1, 32:48], start=True, stop=False), mmi, 1)
                for j in range(2):
                    k.inc(MM(t, psa[64 * j:64 * j + 64, 16:17],
                                   trow[0:1, 32:96], k.cpool[0:1, 3:4],
                                   start=False, stop=False), mmi, 1)
                k.inc(MM(t, psa[:, 17:18], k.ones_row[0:1, :],
                               trow[0:1, 31:32], start=False, stop=True), mmi, 1)

            @block.vector
            def _(v2):
                v2.wait_ge(mmi, mmi0 + 4)
                v2.tensor_copy(out=negtr[:, :], in_=psa[:, 0:16])
                v2.drain()
                v2.tensor_copy(out=ntqcb[:, :], in_=psa[:, 16:17])
                v2.drain()
                v2.tensor_copy(out=t15cb[:, :], in_=psa[:, 17:18])
                v2.drain()

        bufs = (xkq, xvds, lrsg, xqws, w1a, w2a, w2t, b2r, x2f, x2b, gbuf,
                gz1f, gsc1, gsc2, gsc3, gz2q, gz1tmq, x2tmq, x1tmq, gfm,
                etamq, a1cT, a2cq, a2cT, lnzq, colp, lrq, lastq4, trilT,
                trilQ, lnwq, lnbq, negtr, ntqcb, t15cb, lx1b)

        for seg in range(n_segs):
            lds = k.dsem()
            with nc.Block() as block:
                @block.sync
                def _(s):
                    for h in range(16):
                        sl = slice(seg * 128, (seg + 1) * 128)
                        k.inc(s.dma_start(
                            out=xkq[0:64, h * 256:(h + 1) * 256]
                                .rearrange("p (r z) -> p r z", z=32)[:, :, 0:16],
                            in_=dram["l0a_xk"][h // 2, (h % 2) * 64:(h % 2) * 64 + 64, sl]
                                .rearrange("p (r t) -> p r t", t=16)), lds, 16)
                        k.inc(s.dma_start(
                            out=xkq[0:64, h * 256:(h + 1) * 256]
                                .rearrange("p (r z) -> p r z", z=32)[:, :, 16:32],
                            in_=dram["l0a_xq"][h // 2, (h % 2) * 64:(h % 2) * 64 + 64, sl]
                                .rearrange("p (r t) -> p r t", t=16)), lds, 16)
                        k.inc(s.dma_start(
                            out=xvds[0:64, h * 128:(h + 1) * 128],
                            in_=dram["l0a_xvd"][h // 2, (h % 2) * 64:(h % 2) * 64 + 64, sl]),
                            lds, 16)
                    k.inc(s.dma_start(
                        out=lrsg[0:1, 0:2048].rearrange("p (h z) -> p h z", z=128),
                        in_=dram["l0a_lr"][:, seg * 128:(seg + 1) * 128]
                            .rearrange("(o h) t -> o h t", o=1)), lds, 16)
                    s.wait_ge(lds, k.v(lds))

            for r in range(8):
                n = seg * 8 + r
                _l0_chunk(k, dram, None, n, r, psa, psb, bufs)

            sts = k.gsem()
            with nc.Block() as block:
                @block.gpsimd
                def _(g):
                    for h in range(16):
                        sl = slice(seg * 128, (seg + 1) * 128)
                        k.inc(g.dma_start(
                            out=dram["l0a_xqw"][h // 2, (h % 2) * 64:(h % 2) * 64 + 64, sl],
                            in_=xqws[0:64, h * 128:(h + 1) * 128]), sts, 16)
                    g.wait_ge(sts, k.v(sts))


def build_l0_post(k, dram, x_in):
    nc = k.nc
    xt_out = []
    with contextlib.ExitStack() as es:
        def sb(nm, shape, dt=F32):
            return es.enter_context(nc.sbuf_tensor(k.name(nm), shape, dt)).ap()
        qt = [sb(f"q_qw{i}", [128, 2048]) for i in range(8)]
        psa = es.enter_context(nc.psum_tensor(k.name("q_psA"), [128, 2048], F32)).ap()
        psb = es.enter_context(nc.psum_tensor(k.name("q_psB"), [128, 2048], F32)).ap()

        ldq = k.dsem()
        with nc.Block() as block:
            @block.sync
            def _(s):
                for i in range(8):
                    k.inc(s.dma_start(out=qt[i][:, :], in_=dram["l0a_xqw"][i, :, :]),
                          ldq, 16)
                s.wait_ge(ldq, k.v(ldq))

        with contextlib.ExitStack() as es2:
            rowA = es2.enter_context(nc.sbuf_tensor(k.name("q_rA"), [1, 2048], F32)).ap()
            rowB = es2.enter_context(nc.sbuf_tensor(k.name("q_rB"), [1, 2048], F32)).ap()
            rowC = es2.enter_context(nc.sbuf_tensor(k.name("q_rC"), [1, 2048], F32)).ap()
            sq = es2.enter_context(nc.sbuf_tensor(k.name("q_sq"), [128, 2048], F32)).ap()
            pwc = es2.enter_context(nc.sbuf_tensor(k.name("q_pw"), [128, 8], F32)).ap()
            pbc = es2.enter_context(nc.sbuf_tensor(k.name("q_pb"), [128, 8], F32)).ap()
            ldc = k.dsem()
            with nc.Block() as block:
                @block.sync
                def _(s):
                    k.inc(s.dma_start(out=pwc[:, :], in_=dram["l0a_postw"][:, :]), ldc, 16)
                    k.inc(s.dma_start(out=pbc[:, :], in_=dram["l0a_postb"][:, :]), ldc, 16)
                    s.wait_ge(ldc, k.v(ldc))
            k.ln_rows(qt, rowA, rowB, rowC, [sq], psa[0:1, :],
                      eps_ap=k.cpool[:, 1:2])
            mmb = k.sem()
            mmb0 = k.v(mmb)
            with nc.Block() as block:
                @block.tensor
                def _(t):
                    k.bcast_row(t, rowA, psa, 2048)
                    k.inc(k.bcast_row(t, rowB, psb, 2048), mmb, 1)

                @block.vector
                def _(v):
                    v.wait_ge(mmb, mmb0 + 1)
                    for i in range(8):
                        v.tensor_tensor(out=qt[i][:, :], in0=qt[i][:, :],
                                        in1=psa[:, :], op=ALU.subtract)
                        v.drain()
                        v.tensor_tensor(out=qt[i][:, :], in0=qt[i][:, :],
                                        in1=psb[:, :], op=ALU.mult)
                        v.drain()
                        v.tensor_scalar(out=qt[i][:, :], in0=qt[i][:, :],
                                        scalar1=pwc[:, i:i + 1],
                                        scalar2=pbc[:, i:i + 1],
                                        op0=ALU.mult, op1=ALU.add)
                        v.drain()

        with contextlib.ExitStack() as es3:
            xnh = es3.enter_context(nc.sbuf_tensor(k.name("q_xnh"), [128, 16384], BF16)).ap()
            xnl = es3.enter_context(nc.sbuf_tensor(k.name("q_xnl"), [128, 16384], BF16)).ap()
            wblk = es3.enter_context(nc.sbuf_tensor(k.name("q_wb"), [128, 2048], F32)).ap()
            wbh = es3.enter_context(nc.sbuf_tensor(k.name("q_wh"), [128, 2048], BF16)).ap()
            wbl = es3.enter_context(nc.sbuf_tensor(k.name("q_wl"), [128, 2048], BF16)).ap()
            gob = es3.enter_context(nc.sbuf_tensor(k.name("q_go"), [128, 4096], F32)).ap()
            ldn = k.dsem()
            with nc.Block() as block:
                @block.sync
                def _(s):
                    for i in range(8):
                        sl = slice(i * 2048, (i + 1) * 2048)
                        k.inc(s.dma_start(out=xnh[:, sl],
                                          in_=dram["l0a_xnh"][i, :, :]), ldn, 16)
                        k.inc(s.dma_start(out=xnl[:, sl],
                                          in_=dram["l0a_xnl"][i, :, :]), ldn, 16)
                    s.wait_ge(ldn, k.v(ldn))

            ldw = [k.dsem(), k.dsem()]
            spw = k.sem()
            mmo = k.sem()
            gel = k.sem()
            evo = k.sem()
            ldw0 = [k.v(ldw[0]), k.v(ldw[1])]
            spw0, mmo0, evo0 = k.v(spw), k.v(mmo), k.v(evo)
            gel0 = k.v(gel)
            with nc.Block() as block:
                @block.sync
                def _(s):
                    for wc in range(8):
                        if wc >= 2:
                            s.wait_ge(spw, spw0 + wc - 1)
                        k.inc(s.dma_start(
                            out=wblk[:, (wc % 2) * 1024:(wc % 2) * 1024 + 1024]
                                .rearrange("p (c m) -> p c m", c=8),
                            in_=dram["l0a_wgT"][:, wc * 128:(wc + 1) * 128]
                                .rearrange("(c p) m -> p c m", p=128)),
                            ldw[wc % 2], 16)

                @block.tensor
                def _(t):
                    for wc in range(8):
                        ps = psa if wc % 2 == 0 else psb
                        t.wait_ge(spw, spw0 + wc + 1)
                        if wc >= 2:
                            t.wait_ge(evo, evo0 + wc - 1)
                        ins = None
                        for c in range(8):
                            lh = wbh[:, (wc % 2) * 1024 + c * 128:
                                     (wc % 2) * 1024 + (c + 1) * 128]
                            ll = wbl[:, (wc % 2) * 1024 + c * 128:
                                     (wc % 2) * 1024 + (c + 1) * 128]
                            for q in range(NT):
                                pss = ps[:, q * 512:(q + 1) * 512]
                                xh = xnh[:, c * 2048 + q * 512:c * 2048 + (q + 1) * 512]
                                xl = xnl[:, c * 2048 + q * 512:c * 2048 + (q + 1) * 512]
                                ins = t.matmul(pss, lh, xh, start=(c == 0), stop=False)
                                ins = t.matmul(pss, lh, xl, start=False, stop=False)
                                ins = t.matmul(pss, ll, xh, start=False, stop=(c == 7))
                        k.inc(ins, mmo, 1)

                @block.scalar
                def _(s):
                    for wc in range(8):
                        ps = psa if wc % 2 == 0 else psb
                        gsl = slice((wc % 2) * 2048, (wc % 2) * 2048 + 2048)
                        s.wait_ge(mmo, mmo0 + wc + 1)
                        if wc >= 2:
                            s.wait_ge(evo, evo0 + wc - 1)
                        s.activation(out=gob[:, gsl], in_=ps[:, :],
                                     func=AF.Gelu_apprx_tanh)
                        s.drain()
                        k.inc(s.nop(), gel, 1)

                @block.vector
                def _(v):
                    def gate_mult(wc):
                        gsl = slice((wc % 2) * 2048, (wc % 2) * 2048 + 2048)
                        v.wait_ge(gel, gel0 + wc + 1)
                        v.tensor_tensor(out=qt[wc][:, :], in0=qt[wc][:, :],
                                        in1=gob[:, gsl], op=ALU.mult)
                        v.drain()
                        k.inc(v.nop(), evo, 1)
                    for wc in range(8):
                        wsl = slice((wc % 2) * 1024, (wc % 2) * 1024 + 1024)
                        v.wait_ge(ldw[wc % 2], ldw0[wc % 2] + 16 * (wc // 2 + 1))
                        v.tensor_copy(out=wbh[:, wsl], in_=wblk[:, wsl])
                        v.drain()
                        v.tensor_tensor(out=wbl[:, wsl], in0=wblk[:, wsl],
                                        in1=wbh[:, wsl], op=ALU.subtract)
                        v.drain()
                        k.inc(v.nop(), spw, 1)
                        if wc >= 1:
                            gate_mult(wc - 1)
                    gate_mult(7)

        xt = [sb(f"q_xt{i}", [128, 2048]) for i in range(8)]
        with contextlib.ExitStack() as es4:
            wof = es4.enter_context(nc.sbuf_tensor(k.name("q_wo"), [128, 2048], F32)).ap()
            ldx = k.dsem()
            ldo = [k.dsem(), k.dsem()]
            mmw = k.sem()
            evw = k.sem()
            ldx0, mmw0, evw0 = k.v(ldx), k.v(mmw), k.v(evw)
            ldo0 = [k.v(ldo[0]), k.v(ldo[1])]
            with nc.Block() as block:
                @block.sync
                def _(s):
                    for dt in range(8):
                        k.inc(s.dma_start(out=xt[dt][:, :], in_=x_in[dt, :, :]),
                              ldx, 16)
                        if dt >= 2:
                            s.wait_ge(mmw, mmw0 + dt - 1)
                        k.inc(s.dma_start(
                            out=wof[:, (dt % 2) * 1024:(dt % 2) * 1024 + 1024]
                                .rearrange("p (c m) -> p c m", c=8),
                            in_=dram["l0a_woT"][:, dt * 128:(dt + 1) * 128]
                                .rearrange("(c p) m -> p c m", p=128)),
                            ldo[dt % 2], 16)

                @block.tensor
                def _(t):
                    for dt in range(8):
                        ps = psa if dt % 2 == 0 else psb
                        t.wait_ge(ldo[dt % 2], ldo0[dt % 2] + 16 * (dt // 2 + 1))
                        if dt >= 2:
                            t.wait_ge(evw, evw0 + dt - 1)
                        ins = None
                        for c in range(8):
                            lw = wof[:, (dt % 2) * 1024 + c * 128:
                                     (dt % 2) * 1024 + (c + 1) * 128]
                            for q in range(NT):
                                ins = t.matmul(ps[:, q * 512:(q + 1) * 512], lw,
                                               qt[c][:, q * 512:(q + 1) * 512],
                                               start=(c == 0), stop=(c == 7))
                        k.inc(ins, mmw, 1)

                @block.vector
                def _(v):
                    for dt in range(8):
                        ps = psa if dt % 2 == 0 else psb
                        v.wait_ge(mmw, mmw0 + dt + 1)
                        v.wait_ge(ldx, ldx0 + 16 * (dt + 1))
                        v.tensor_tensor(out=xt[dt][:, :], in0=xt[dt][:, :],
                                        in1=ps[:, :], op=ALU.add)
                        v.drain()
                        k.inc(v.nop(), evw, 1)
        xt_out = xt
        return es.pop_all(), xt_out, psa, psb


def build_test_l0_attn(n_segs=16, dbg_xqw=False):
    nc = bass.Bass("TRN2", target_bir_lowering=False)
    x_in = nc.dram_tensor("x_in", [8, 128, 2048], F32, kind="ExternalInput").ap()
    x_out = nc.dram_tensor("x_out", [8, 128, 2048], F32, kind="ExternalOutput").ap()
    dram = l0_attn_dram(nc)
    if dbg_xqw:
        dram["l0a_xqw"] = nc.dram_tensor("xqw_out", [8, 128, 2048], F32,
                                         kind="ExternalOutput").ap()
    k = K(nc)
    k.init_common()
    with contextlib.ExitStack() as s1:
        xt = [s1.enter_context(nc.sbuf_tensor(k.name("xt"), [128, 2048], F32)).ap()
              for _ in range(8)]
        k.load_x_fm(x_in, xt)
        build_l0_ln_tohbm(k, xt, dram)
    with contextlib.ExitStack() as s2:
        xnh = s2.enter_context(nc.sbuf_tensor("xnh0", [128, 16384], BF16)).ap()
        xnl = s2.enter_context(nc.sbuf_tensor("xnl0", [128, 16384], BF16)).ap()
        load_xn0(k, dram, xnh, xnl)
        build_l0_proj(k, dram, xnh, xnl)
    build_l0_scan(k, dram, n_segs=n_segs)
    if n_segs == 16:
        hold, xt, _psa, _psb = build_l0_post(k, dram, x_in)
        k.store_x_fm(x_out, xt)
        hold.close()
    mybir.codegen_inst_isa_subclasses(nc)
    return nc


def load_xn0(k, dram, xnh, xnl):
    nc = k.nc
    ldn = k.dsem()
    with nc.Block() as block:
        @block.sync
        def _(s):
            for i in range(8):
                sl = slice(i * 2048, (i + 1) * 2048)
                k.inc(s.dma_start(out=xnh[:, sl], in_=dram["l0a_xnh"][i, :, :]),
                      ldn, 16)
                k.inc(s.dma_start(out=xnl[:, sl], in_=dram["l0a_xnl"][i, :, :]),
                      ldn, 16)
            s.wait_ge(ldn, k.v(ldn))



def _build_launch_A():
    nc = bass.Bass("TRN2", target_bir_lowering=False)
    x_in = nc.dram_tensor("x_in", [8, 128, 2048], F32, kind="ExternalInput").ap()
    x_out = nc.dram_tensor("x_out", [8, 128, 2048], F32, kind="ExternalOutput").ap()
    lm_out = nc.dram_tensor("lmax1", [1, 1], F32, kind="ExternalOutput").ap()
    dram = l0_attn_dram(nc)
    k = K(nc)
    k.init_common()
    with contextlib.ExitStack() as s1:
        xt = [s1.enter_context(nc.sbuf_tensor(k.name("xt"), [128, 2048], F32)).ap()
              for _ in range(8)]
        k.load_x_fm(x_in, xt)
        build_l0_ln_tohbm(k, xt, dram)
    with contextlib.ExitStack() as s2:
        xnh = s2.enter_context(nc.sbuf_tensor("xnh0", [128, 16384], BF16)).ap()
        xnl = s2.enter_context(nc.sbuf_tensor("xnl0", [128, 16384], BF16)).ap()
        load_xn0(k, dram, xnh, xnl)
        build_l0_proj(k, dram, xnh, xnl)
    build_l0_scan(k, dram, n_segs=16)
    hold, xt, psa, psb = build_l0_post(k, dram, x_in)
    k.store_x_fm(x_out, xt)
    with contextlib.ExitStack() as s3:
        def sb(nm, shape, dt=F32):
            return s3.enter_context(nc.sbuf_tensor(k.name(nm), shape, dt)).ap()
        rows4 = [sb(f"m_r{i}", [1, 2048]) for i in range(4)]
        sq = sb("m_sq", [128, 2048])
        col = sb("m_c1", [128, 1])
        col2 = sb("m_c2", [128, 1])
        pam = sb("m_pam", [128, 16])
        lms = sb("m_lm", [1, 1])
        emit_ln_absmax(k, xt, lms, rows4, psb[:, 0:1024], psa[0:1, 0:2048],
                       col, col2, pam, [sq])
        stm = k.gsem()
        with nc.Block() as block:
            @block.gpsimd
            def _(g):
                k.inc(g.dma_start(out=lm_out[0:1, :], in_=lms[0:1, :]), stm, 16)
                g.wait_ge(stm, k.v(stm))
    hold.close()
    mybir.codegen_inst_isa_subclasses(nc)
    return nc


def _build_launch_B():
    nc = bass.Bass("TRN2", target_bir_lowering=False)
    x_in = nc.dram_tensor("x_in", [8, 128, 2048], F32, kind="ExternalInput").ap()
    dram = l0_ffn_dram(nc, "max2")
    k = K(nc)
    k.init_common()
    xt = [nc.alloc_sbuf_tensor(k.name("xt"), [128, 2048], F32).ap()
          for _ in range(8)]
    k.load_x_fm(x_in, xt)
    build_l0_ffn(k, xt, dram, upto="max2")
    mybir.codegen_inst_isa_subclasses(nc)
    return nc


def _build_launch_C():
    nc = bass.Bass("TRN2", target_bir_lowering=False)
    x_in = nc.dram_tensor("x_in", [8, 128, 2048], F32, kind="ExternalInput").ap()
    x_out = nc.dram_tensor("x_out", [8, 128, 2048], F32, kind="ExternalOutput").ap()
    dram = {}
    dram.update(l0_ffn_dram(nc, "full"))
    dram.update(l1_attn_dram(nc))
    dram.update(l1_ffn_dram(nc))
    k = K(nc)
    k.init_common()
    with contextlib.ExitStack() as s1:
        xt = [s1.enter_context(nc.sbuf_tensor(k.name("xt"), [128, 2048], F32)).ap()
              for _ in range(8)]
        k.load_x_fm(x_in, xt)
        build_l0_ffn(k, xt, dram, upto="full")
        build_l1_ln_tohbm(k, xt, dram)
        spill_xt(k, xt, dram["l1a_xspill"])
    with contextlib.ExitStack() as s2:
        xnh = s2.enter_context(nc.sbuf_tensor("xnh_all", [128, 16384], BF16)).ap()
        xnl = s2.enter_context(nc.sbuf_tensor("xnl_all", [128, 16384], BF16)).ap()
        load_xn(k, dram, xnh, xnl)
        build_l1_lact(k, dram, xnh, xnl)
        build_l1_attn_pairs(k, dram, xnh, xnl)
    with contextlib.ExitStack() as s3:
        xt = [s3.enter_context(nc.sbuf_tensor(k.name("xt"), [128, 2048], F32)).ap()
              for _ in range(8)]
        build_l1_attn_outproj(k, xt, dram)
        build_l1_ffn(k, xt, dram)
        k.store_x_fm(x_out, xt)
    mybir.codegen_inst_isa_subclasses(nc)
    return nc


_NC_CACHE = {}


def kernel(**inputs):
    from concourse.bass_utils import run_bass_kernel_spmd
    z = inputs
    x0 = np.asarray(z["L0_attn_hidden_states"])
    w_l0a = l0_attn_inputs(
        z["L0_attn_Wq"], z["L0_attn_Wk"], z["L0_attn_Wv"], z["L0_attn_Wo"],
        z["L0_attn_Wg"], z["L0_attn_lr_w"], z["L0_attn_lr_b"],
        z["L0_attn_learnable_token_idx"], z["L0_attn_norm_w"],
        z["L0_attn_norm_b"], z["L0_attn_W1"], z["L0_attn_b1"],
        z["L0_attn_W2"], z["L0_attn_b2"], z["L0_attn_post_w"],
        z["L0_attn_post_b"])
    w_l0f = l0_ffn_inputs(z["L0_ffn_W_in"], z["L0_ffn_b_in"],
                          z["L0_ffn_W_out"], z["L0_ffn_b_out"])
    w_l1a = l1_attn_inputs(z["L1_attn_ln_w"], z["L1_attn_ln_b"],
                           z["L1_attn_in_proj_w"], z["L1_attn_in_proj_b"],
                           z["L1_attn_out_proj_w"], z["L1_attn_out_proj_b"],
                           z["L1_attn_w0"], z["L1_attn_w1"], z["L1_attn_w2"])
    w_l1f = l1_ffn_inputs(z["L1_ffn_w1"], z["L1_ffn_conv_w"],
                          z["L1_ffn_conv_b"], z["L1_ffn_w2"])

    if "A" not in _NC_CACHE:
        _NC_CACHE["A"] = _build_launch_A()
        _NC_CACHE["B"] = _build_launch_B()
        _NC_CACHE["C"] = _build_launch_C()
    ncA, ncB, ncC = _NC_CACHE["A"], _NC_CACHE["B"], _NC_CACHE["C"]

    cores = list(range(8))
    maps = []
    for b in range(8):
        m = {"x_in": x_fm_layout(x0[b])}
        m.update(w_l0a)
        maps.append(m)
    resA = run_bass_kernel_spmd(ncA, maps, core_ids=cores)
    x1sh = [np.asarray(resA.results[b]["x_out"]) for b in range(8)]
    gm1 = np.concatenate([np.asarray(resA.results[b]["lmax1"]).reshape(1, 1)
                          for b in range(8)], axis=1)

    maps = []
    for b in range(8):
        m = {"x_in": x1sh[b], "l0f_gmax1": gm1}
        m.update({kk: vv for kk, vv in w_l0f.items() if kk in
                  ("l0f_win", "l0f_binc")})
        maps.append(m)
    resB = run_bass_kernel_spmd(ncB, maps, core_ids=cores)
    gm2 = np.concatenate([np.asarray(resB.results[b]["l0f_lmax"])[0:1, 1:2]
                          for b in range(8)], axis=1)

    maps = []
    for b in range(8):
        m = {"x_in": x1sh[b], "l0f_gmax1": gm1, "l0f_gmax2": gm2}
        m.update(w_l0f)
        m.update(w_l1a)
        m.update(w_l1f)
        maps.append(m)
    resC = run_bass_kernel_spmd(ncC, maps, core_ids=cores)
    out = np.stack([x_fm_unlayout(np.asarray(resC.results[b]["x_out"]))
                    for b in range(8)]).astype(np.float32)
    return out
